# version 05
# baseline (speedup 1.0000x reference)
import numpy as np
import concourse.bacc as bacc
import concourse.bass as bass
import concourse.mybir as mybir
from concourse.bass_core_rust import IndirectOffsetOnAxis
from contextlib import ExitStack

F32 = mybir.dt.float32
F16 = mybir.dt.float16
U32 = mybir.dt.uint32
I16 = mybir.dt.int16
AF = mybir.ActivationFunctionType
AL = mybir.AluOpType

DM = 1024
DI = 2048
NS = 16
NB = DI // 128
NI = NS * NB


def ts(i, n):
    return slice(i * n, (i + 1) * n)


class Ctr:
    def __init__(self, nc, ctx, name):
        self.s = ctx.enter_context(nc.semaphore(name))
        self.n = 0

    def inc(self, instr, amt=1):
        self.n += amt
        instr.then_inc(self.s, amt)
        return instr


class MK:
    def __init__(self, cfg):
        self.cfg = cfg
        self.L = cfg["L"]
        self.HALO = cfg["HALO"]
        self.T = cfg["T"]
        self.NCH = self.L // self.T
        assert self.L % self.T == 0
        self.nc = bacc.Bacc("TRN2", target_bir_lowering=False)
        self.ctx = ExitStack()

    def sbuf(self, ctx, n_, s_, d_=F32):
        return ctx.enter_context(self.nc.sbuf_tensor(n_, s_, d_))

    def C(self, name):
        return Ctr(self.nc, self.ctx, name)

    def declare_io(self):
        nc, L, HALO = self.nc, self.L, self.HALO
        self.FFN = self.cfg.get("FFN", False)
        self.DFF = self.cfg.get("DFF", 4096)
        self.TOK = 8 * L
        self.K = self.TOK // 16
        self.L2 = self.TOK // 16
        self.RS = min(512, self.K)
        self.NKB = self.K // self.RS
        dt = nc.dram_tensor
        self.xs = dt("xs", [HALO + L, DM], F32, kind="ExternalInput")
        self.wu_in = dt("wu_in", [DI, DM], F32, kind="ExternalInput")
        self.wz_in = dt("wz_in", [DI, DM], F32, kind="ExternalInput")
        self.convw_in = dt("convw_in", [DI, 4], F32, kind="ExternalInput")
        self.xpw_in = dt("xpw_in", [17, DI], F32, kind="ExternalInput")
        self.dtw_in = dt("dtw_in", [DI, 1], F32, kind="ExternalInput")
        self.dtb_in = dt("dtb_in", [1, DI], F32, kind="ExternalInput")
        self.ap_in = dt("ap_in", [DI, NS], F32, kind="ExternalInput")
        self.dp_in = dt("dp_in", [1, DI], F32, kind="ExternalInput")
        self.ow_in = dt("ow_in", [DM, DI], F32, kind="ExternalInput")
        self.ob_in = dt("ob_in", [1, DM], F32, kind="ExternalInput")
        self.rw_in = dt("rw_in", [DM, 8], F32, kind="ExternalInput")
        self.rb_in = dt("rb_in", [1, 8], F32, kind="ExternalInput")

        if self.FFN:
            DFF, TOK, K, L2 = self.DFF, self.TOK, self.K, self.L2
            self.scrow_in = dt("scrow_in", [1, 80], U32, kind="ExternalInput")
            self.w1_in = dt("w1_in", [DM, DFF], F32, kind="ExternalInput")
            self.b1_in = dt("b1_in", [1, DFF], F32, kind="ExternalInput")
            self.w2_in = dt("w2_in", [DFF, DM], F32, kind="ExternalInput")
            self.b2_in = dt("b2_in", [1, DM], F32, kind="ExternalInput")
            self.out_f = dt("out", [L, DM], F32, kind="ExternalOutput")
            self.x1_out = dt("x1_dram", [L, DM], F32)
            self.scT_dram = dt("scT_dram", [8, L], F32)
            self.ag_x1 = dt("ag_x1", [TOK, DM], F32)
            self.ag_scT = dt("ag_scT", [64, L], F32)
            self.w116 = dt("w116", [128, 2, DM // 128, DFF], F16)
            self.w216 = dt("w216", [128, 2, DFF // 128, DM], F16)
            self.idxp = dt("idxp", [1, K + L2], I16)
            self.idx_list = dt("idx_list", [1, K], I16)
            self.glo_dram = dt("glo_dram", [1, K + L2], I16)
            self.ghi_dram = dt("ghi_dram", [1, K + L2], I16)
            self.ag_idx = dt("ag_idx", [8, K], I16)
            self.contrib = dt("contrib", [K, DM], F32)
            self.th_dram = dt("th_dram", [128, DM // 128, K], F16)
            self.tl_dram = dt("tl_dram", [128, DM // 128, K], F16)
            self.ag_contrib = dt("ag_contrib", [8 * K, DM], F32)
            self.yfull = dt("yfull", [TOK, DM], F32)
        else:
            self.x1_out = dt("x1", [L, DM], F32, kind="ExternalOutput")
            self.sc_out = dt("sc", [L, 8], F32, kind="ExternalOutput")

        self.wu16 = dt("wu16", [128, 2, 8, DI], F16)
        self.wz16 = dt("wz16", [128, 2, 8, DI], F16)
        self.ow16 = dt("ow16", [128, 2, 16, DM], F16)
        self.ub_dram = dt("ub_dram", [128, NB, L], F32)
        self.y_dram = dt("y_dram", [128, NB, L], F32)
        self.y3_dram = dt("y3_dram", [128, 2, NB, L], F16)

    def alloc_common(self):
        cx = self.ctx
        sb = self.sbuf
        self.xp_sb = sb(cx, "xp_sb", [128, 16, 17])
        self.convw_sb = sb(cx, "convw_sb", [128, NB, 4])
        self.dtw_sb = sb(cx, "dtw_sb", [128, NB])
        self.dtb_sb = sb(cx, "dtb_sb", [128, NB])
        self.dpar_sb = sb(cx, "dpar_sb", [128, NB])
        self.a_sb = sb(cx, "a_sb", [128, NB * NS])
        self.ob_sb = sb(cx, "ob_sb", [128, 8])
        self.rw_sb = sb(cx, "rw_sb", [128, 8, 8])
        self.rb_sb = sb(cx, "rb_sb", [128, 8])
        self.ones_sb = sb(cx, "ones_sb", [1, 512])
        self.ident = sb(cx, "ident", [128, 128])
        if self.FFN:
            self.scrow_sb = sb(cx, "scrow_sb", [1, 80], U32)
            self.b1_sb = sb(cx, "b1_sb", [128, self.DFF // 128])
            self.b2_sb = sb(cx, "b2_sb", [128, DM // 128])

        psu = lambda n_, s_: cx.enter_context(self.nc.psum_tensor(n_, s_, F32))
        self.p_a = psu("p_a", [128, 512])
        self.p_b = psu("p_b", [128, 512])
        self.p_c = psu("p_c", [128, 512])
        self.p_d = psu("p_d", [128, 512])
        self.p_e = psu("p_e", [128, 512])
        self.p_f = psu("p_f", [128, 512])
        self.p_g = psu("p_g", [128, 512])
        self.p_h = psu("p_h", [128, 512])

        for nm in ["c_ld", "c_k", "c_w16", "c_xin", "c_tp", "c_xch", "c_xcl",
                   "c_mm", "c_ev", "c_hc", "c_cv", "c_sil", "c_xd", "c_bev",
                   "c_dt", "c_dl", "c_brm", "c_bre", "c_sa", "c_su", "c_sh",
                   "c_sp", "c_ya", "c_spl", "c_wld", "c_zev", "c_gt1", "c_gt2",
                   "c_y3c", "c_y3l", "c_y3s", "c_oev", "c_otp", "c_ocp",
                   "c_ln1", "c_ln2", "c_ln3", "c_rt1", "c_rt2", "c_rt3",
                   "c_sfx", "c_out", "c_wlo0", "c_wlo1", "c_xin0", "c_xin1", "c_bv0", "c_bv1", "c_y3s0", "c_y3s1", "c_gp", "c_sc", "c_dv", "c_xr", "c_lgr",
                   "c_sct", "c_ag", "c_bis", "c_cmp", "c_mrg", "c_gth", "c_h1m",
                   "c_h1r", "c_h1c", "c_eo", "c_ct", "c_sca", "c_pul", "c_w1s0",
                   "c_w1s1", "c_w2s0", "c_w2s1", "c_fts", "c_fev", "c_gr", "c_gat", "c_sdm", "c_sd0", "c_sd1"]:
            setattr(self, nm, self.C(nm))

    def emit_init(self):
        s, nc = self, self.nc
        IB = self.cfg.get("IB", 99)
        ictx = ExitStack()
        wstg = self.sbuf(ictx, "wstg", [128, 2048])
        vstg = self.sbuf(ictx, "vstg", [17, 2048])
        fsh = self.sbuf(ictx, "fsh", [128, 512], F16)
        fsl = self.sbuf(ictx, "fsl", [128, 512], F16)

        if (blk := self.blk) or True:
            @blk.sync
            def _(sync):
                s.c_ld.inc(sync.dma_start(out=s.convw_sb[:, :, :], in_=s.convw_in.ap().rearrange("(b p) t -> p b t", p=128)), 16)
                s.c_ld.inc(sync.dma_start(out=s.a_sb.ap().rearrange("p (b n) -> p b n", n=NS), in_=s.ap_in.ap().rearrange("(b p) n -> p b n", p=128)), 16)
                s.c_ld.inc(sync.dma_start(out=s.rw_sb[:, :, :], in_=s.rw_in.ap().rearrange("(k p) e -> p k e", p=128)), 16)
                s.c_ld.inc(sync.dma_start(out=s.rb_sb[:, :], in_=s.rb_in.ap().broadcast_to([128, 8])), 16)
                s.c_ld.inc(sync.dma_start(out=vstg[0:17, :], in_=s.xpw_in.ap()), 16)
                if s.FFN:
                    s.c_ld.inc(sync.dma_start(out=s.scrow_sb[:, :], in_=s.scrow_in.ap()), 16)

            if IB < 2:
                return
            @blk.vector
            def _(v):
                v.wait_ge(s.c_ld.s, s.c_ld.n)
                s.c_k.inc(v.memset(s.ones_sb[:, :], 1.0))
                s.c_k.inc(v.memset(s.ident[:, :], 1.0))

            if IB < 3:
                return
            @blk.scalar
            def _(sc):
                from concourse.hw_specs import get_activation_tables
                tabs = list(get_activation_tables(nc.m.arch))
                set_id = tabs.index("natural_log_exp_and_others")
                sc.add_instruction(mybir.InstLoadActFuncSet(
                    name=nc.get_next_instruction_name(), act_func_set_id=set_id,
                    ins=[], outs=[]))
                sc.wait_ge(s.c_ld.s, s.c_ld.n)
                s.c_k.inc(sc.activation(s.a_sb[:, :], s.a_sb[:, :], AF.Exp))

            if IB < 4:
                return
            @blk.gpsimd
            def _(g):
                g.wait_ge(s.c_k.s, s.c_k.n)
                s.c_k.inc(g.affine_select(s.ident[:, :], s.ident[:, :], [[1, 128]],
                                          AL.is_equal, 0.0, base=0, channel_multiplier=-1))

            if IB < 5:
                return
            @blk.vector
            def _(v):
                v.wait_ge(s.c_k.s, s.c_k.n)
                s.c_k.inc(v.tensor_scalar_mul(s.a_sb[:, :], s.a_sb[:, :], -1.0))

        tp0x, ev0x = s.c_tp.n, s.c_ev.n
        if IB < 6:
            ictx.close()
            return
        if (blk := self.blk) or True:
            @blk.tensor
            def _(t):
                t.wait_ge(s.c_k.s, s.c_k.n)
                for r in range(4):
                    pb = s.p_c if r % 2 == 0 else s.p_g
                    if r >= 2:
                        t.wait_ge(s.c_ev.s, ev0x + 4 * (r - 1))
                    for q in range(4):
                        a = r * 4 + q
                        s.c_tp.inc(t.transpose(pb[:, q * 128: q * 128 + 17],
                                               vstg[0:17, ts(a, 128)], s.ident[0:17, 0:17]))

            @blk.scalar
            def _(sc):
                for r in range(4):
                    pb = s.p_c if r % 2 == 0 else s.p_g
                    sc.wait_ge(s.c_tp.s, tp0x + 4 * (r + 1))
                    for q in range(4):
                        a = r * 4 + q
                        s.c_ev.inc(sc.activation(s.xp_sb[:, a, :], pb[:, q * 128: q * 128 + 17], AF.Copy))

        vecjobs = [(s.dtw_in.ap().rearrange("(a p) o -> a (p o)", p=128), 16, s.dtw_sb),
                   (s.dtb_in.ap().rearrange("o (a p) -> (o a) p", p=128), 16, s.dtb_sb),
                   (s.dp_in.ap().rearrange("o (a p) -> (o a) p", p=128), 16, s.dpar_sb),
                   (s.ob_in.ap().rearrange("o (a p) -> (o a) p", p=128), 8, s.ob_sb)]
        ofs = [0, 128, 256, 384]
        if IB < 7:
            ictx.close()
            return
        if (blk := self.blk) or True:
            @blk.sync
            def _(sync):
                sync.wait_ge(s.c_tp.s, s.c_tp.n)
                for i, (src, rows, dst) in enumerate(vecjobs):
                    s.c_ld.inc(sync.dma_start(out=vstg[0:rows, ofs[i]:ofs[i] + 128], in_=src), 16)

            @blk.tensor
            def _(t):
                t.wait_ge(s.c_ld.s, s.c_ld.n)
                t.wait_ge(s.c_ev.s, s.c_ev.n)
                for i, (src, rows, dst) in enumerate(vecjobs):
                    s.c_tp.inc(t.transpose(s.p_c[:, ofs[i]:ofs[i] + rows], vstg[0:rows, ofs[i]:ofs[i] + 128], s.ident[0:rows, 0:rows]))

            @blk.scalar
            def _(sc):
                sc.wait_ge(s.c_tp.s, s.c_tp.n)
                for i, (src, rows, dst) in enumerate(vecjobs):
                    s.c_ev.inc(sc.activation(dst.ap()[:, 0:rows], s.p_c[:, ofs[i]:ofs[i] + rows], AF.Copy))

        jobs = [(s.wu_in, s.wu16, 8, 16), (s.wz_in, s.wz16, 8, 16), (s.ow_in, s.ow16, 16, 8)]
        self.w16_mark = {}
        if IB < 8:
            jobs = []
        for (src, dst, nk, natile) in jobs:
            width = 128 * nk
            for a in range(natile):
                if (blk := self.blk) or True:
                    @blk.sync
                    def _(sync, src=src, a=a, width=width):
                        sync.wait_ge(s.c_tp.s, s.c_tp.n)
                        s.c_ld.inc(sync.dma_start(out=wstg[:, 0:width], in_=src.ap()[ts(a, 128), 0:width]), 16)

                    tp0w, xch0w, xcl0w, w160 = s.c_tp.n, s.c_xch.n, s.c_xcl.n, s.c_w16.n

                    @blk.tensor
                    def _(t, nk=nk, tp0w=tp0w, w160=w160):
                        t.wait_ge(s.c_ld.s, s.c_ld.n)
                        for kg in range(nk // 4):
                            if kg >= 1:
                                t.wait_ge(s.c_w16.s, w160 + 128 * kg)
                            for q in range(4):
                                k = kg * 4 + q
                                s.c_tp.inc(t.transpose(s.p_f[:, ts(q, 128)], wstg[:, ts(k, 128)], s.ident[:, :]))

                    @blk.scalar
                    def _(sc, nk=nk, tp0w=tp0w, w160=w160):
                        for kg in range(nk // 4):
                            sc.wait_ge(s.c_tp.s, tp0w + 4 * (kg + 1))
                            sc.wait_ge(s.c_w16.s, w160 + 128 * kg)
                            s.c_xch.inc(sc.activation(fsh[:, :], s.p_f[:, :], AF.Copy))

                    @blk.vector
                    def _(v, nk=nk, xch0w=xch0w, w160=w160):
                        for kg in range(nk // 4):
                            v.wait_ge(s.c_xch.s, xch0w + kg + 1)
                            v.wait_ge(s.c_w16.s, w160 + 128 * kg)
                            s.c_xcl.inc(v.tensor_tensor(fsl[:, :], s.p_f[:, :], fsh[:, :], AL.subtract))

                    @blk.sync
                    def _(sync, dst=dst, a=a, nk=nk, xcl0w=xcl0w):
                        for kg in range(nk // 4):
                            sync.wait_ge(s.c_xcl.s, xcl0w + kg + 1)
                            for q in range(4):
                                k = kg * 4 + q
                                s.c_w16.inc(sync.dma_start(out=dst[:, 0, k, ts(a, 128)], in_=fsh[:, ts(q, 128)]), 16)
                                s.c_w16.inc(sync.dma_start(out=dst[:, 1, k, ts(a, 128)], in_=fsl[:, ts(q, 128)]), 16)
            self.w16_mark[dst.name] = s.c_w16.n
        ictx.close()

    def open_phase(self, name):
        self._blk_cm = self.nc.Block(name)
        self.blk = self._blk_cm.__enter__()

    def close_phase(self):
        self._blk_cm.__exit__(None, None, None)
        self.blk = None

    def barrier_all(self, ctrs):
        s, nc = self, self.nc
        blk = self.blk
        for engname in ["tensor", "scalar", "vector", "gpsimd", "sync"]:
            def mk_(eng):
                for c in ctrs:
                    eng.wait_ge(c.s, c.n)
            blk.__getattribute__(engname)(mk_)

    def emit_xchunk(self, off, Tc):
        s, nc = self, self.nc
        nsub = Tc // 128
        tp0, xch0, xcl0 = s.c_tp.n, s.c_xch.n, s.c_xcl.n
        xin_vals = []
        if (blk := self.blk) or True:
            @blk.sync
            def _(sync):
                for j in range(nsub):
                    cx = s.c_xin0 if j % 2 == 0 else s.c_xin1
                    if j >= 2:
                        sync.wait_ge(s.c_tp.s, tp0 + 8 * (j - 1))
                    else:
                        sync.wait_ge(s.c_tp.s, tp0)
                    cx.inc(sync.dma_start(out=s.xnat[:, j % 2, :],
                                          in_=s.xs[off + j * 128: off + (j + 1) * 128, :]), 16)
                    xin_vals.append((cx, cx.n))

            @blk.tensor
            def _(t):
                t.wait_ge(s.c_k.s, s.c_k.n)
                for j in range(nsub):
                    t.wait_ge(xin_vals[j][0].s, xin_vals[j][1])
                    for half in range(2):
                        u = j * 2 + half
                        pb = s.p_c if u % 2 == 0 else s.p_g
                        t.wait_ge(s.c_xcl.s, xcl0 + u - 2 + 1 if u >= 2 else xcl0)
                        for q in range(4):
                            mc = half * 4 + q
                            s.c_tp.inc(t.transpose(pb[:, ts(q, 128)], s.xnat[:, j % 2, ts(mc, 128)], s.ident[:, :]))

            @blk.scalar
            def _(sc):
                for u in range(nsub * 2):
                    j, half = u // 2, u % 2
                    pb = s.p_c if u % 2 == 0 else s.p_g
                    sc.wait_ge(s.c_tp.s, tp0 + 4 * (u + 1))
                    s.c_xch.inc(sc.activation(s.xh[:, half * 4:half * 4 + 4, ts(j, 128)], pb[:, :], AF.Copy))

            @blk.vector
            def _(v):
                for u in range(nsub * 2):
                    j, half = u // 2, u % 2
                    pb = s.p_c if u % 2 == 0 else s.p_g
                    v.wait_ge(s.c_xch.s, xch0 + u + 1)
                    s.c_xcl.inc(v.tensor_tensor(s.xl[:, half * 4:half * 4 + 4, ts(j, 128)],
                                                pb[:, :], s.xh[:, half * 4:half * 4 + 4, ts(j, 128)],
                                                AL.subtract))

    def emit_proj(self, wdram, Tc, evac, evctr=None):
        s, nc = self, self.nc
        evctr = evctr if evctr is not None else s.c_ev
        mm0, ev0 = s.c_mm.n, evctr.n
        xcl_all, wld_all = s.c_xcl.n, s.c_wld.n
        wlo_vals = []
        if (blk := self.blk) or True:
            @blk.sync
            def _(sync):
                for ct in range(NB):
                    cw = s.c_wlo0 if ct % 2 == 0 else s.c_wlo1
                    if ct >= 2:
                        sync.wait_ge(s.c_mm.s, mm0 + 24 * (ct - 1))
                    cw.inc(sync.dma_start(out=s.wlo[:, ct % 2, :, :],
                                          in_=wdram[:, 1, :, ts(ct, 128)]), 16)
                    wlo_vals.append((cw, cw.n))

            @blk.tensor
            def _(t):
                t.wait_ge(s.c_wld.s, wld_all)
                t.wait_ge(s.c_xcl.s, xcl_all)
                for ct in range(NB):
                    pb = s.p_a if ct % 2 == 0 else s.p_b
                    if ct >= 2:
                        t.wait_ge(evctr.s, ev0 + (ct - 2) + 1)
                    t.wait_ge(wlo_vals[ct][0].s, wlo_vals[ct][1])
                    for k in range(8):
                        wh = s.wbig[:, k, ts(ct, 128)]
                        wl = s.wlo[:, ct % 2, k, :]
                        s.c_mm.inc(t.matmul(pb[:, 0:Tc], wh, s.xh[:, k, 0:Tc], start=(k == 0), stop=False))
                        s.c_mm.inc(t.matmul(pb[:, 0:Tc], wh, s.xl[:, k, 0:Tc], start=False, stop=False))
                        s.c_mm.inc(t.matmul(pb[:, 0:Tc], wl, s.xh[:, k, 0:Tc], start=False, stop=(k == 7)))

            @blk.scalar
            def _(sc):
                for ct in range(NB):
                    pb = s.p_a if ct % 2 == 0 else s.p_b
                    sc.wait_ge(s.c_mm.s, mm0 + 24 * (ct + 1))
                    evac(sc, ct, pb)

    def emit_m1(self):
        s, nc = self, self.nc
        HALO, T = self.HALO, self.T

        m12 = ExitStack()
        self.whi = self.sbuf(m12, "whi", [128, 2, 8, 128], F16)
        self.wlo = self.sbuf(m12, "wlo", [128, 2, 8, 128], F16)
        self.xh = self.sbuf(m12, "xh", [128, 8, 512], F16)
        self.xl = self.sbuf(m12, "xl", [128, 8, 512], F16)
        self.xnat = self.sbuf(m12, "xnat", [128, 2, DM])
        self.ubar = self.sbuf(m12, "ubar", [128, NB, 512])
        self.yac = self.sbuf(m12, "yac", [128, NB, 512])
        self.ucac = self.sbuf(m12, "ucac", [128, 2, 512])
        self.brow = self.sbuf(m12, "brow", [17, 512])
        self.bvec = self.sbuf(m12, "bvec", [1, 2, 512])
        self.m12 = m12

        m1 = ExitStack()
        self.delta = self.sbuf(m1, "delta", [128, NB, 512])
        self.utmp = self.sbuf(m1, "utmp", [128, 2, 4, 516])
        self.brep = self.sbuf(m1, "brep", [128, 2, 512])
        self.sa = self.sbuf(m1, "sa", [128, 2, 512])
        self.su = self.sbuf(m1, "su", [128, 4, 512])
        self.sh = self.sbuf(m1, "sh", [128, 2, 512])
        self.sgt = self.sbuf(m1, "sgt", [128, 2, 512])
        self.state = self.sbuf(m1, "state", [128, NB * NS])
        self.uhalo = self.sbuf(m1, "uhalo", [128, 2, NB, 4, 3])
        self.ch_id = 0

        self.barrier_all([s.c_w16, s.c_k, s.c_ev, s.c_ld, s.c_w1s0,
                          s.c_w1s1, s.c_w2s0, s.c_w2s1, s.c_xch, s.c_xcl])
        if (blk := self.blk) or True:
            @blk.vector
            def _(v):
                v.wait_ge(s.c_k.s, s.c_k.n)
                s.c_k.inc(v.memset(s.state[:, :], 0.0))
                s.c_k.inc(v.memset(s.uhalo[:, :, :, :], 0.0))

        chunks = ([(0, HALO, True)] if HALO else []) + \
                 [(HALO + i * T, T, False) for i in range(self.NCH)]
        for ci, (off, Tc, halo) in enumerate(chunks):
            self.emit_xchunk(off, Tc)
            self.emit_m1_chunk(off, Tc, halo)
            self.ch_id += 1

        m1.close()

    def emit_m1_inproj_conv(self, Tc):
        s, nc = self, self.nc
        cv0, hc0, ev0 = s.c_cv.n, s.c_hc.n, s.c_ev.n
        sil0, zev_sil0 = s.c_sil.n, s.c_zev.n
        xd_prev, spl_prev = s.c_xd.n, s.c_spl.n
        mm0 = s.c_mm.n
        xcl_all, wld_all = s.c_xcl.n, s.c_wld.n
        wlo_vals = []
        blk = self.blk
        par = s.ch_id % 2

        @blk.sync
        def _(sync):
            for ct in range(NB):
                cw = s.c_wlo0 if ct % 2 == 0 else s.c_wlo1
                if ct >= 2:
                    sync.wait_ge(s.c_mm.s, mm0 + 24 * (ct - 1))
                cw.inc(sync.dma_start(out=s.whi[:, ct % 2, :, :],
                                      in_=s.wu16[:, 0, :, ts(ct, 128)]), 16)
                cw.inc(sync.dma_start(out=s.wlo[:, ct % 2, :, :],
                                      in_=s.wu16[:, 1, :, ts(ct, 128)]), 16)
                wlo_vals.append((cw, cw.n))

        @blk.tensor
        def _(t):
            t.wait_ge(s.c_xcl.s, xcl_all)
            for ct in range(NB):
                pb = s.p_a if ct % 2 == 0 else s.p_b
                if ct >= 2:
                    t.wait_ge(s.c_ev.s, ev0 + 4 * (ct - 1))
                t.wait_ge(wlo_vals[ct][0].s, wlo_vals[ct][1])
                for k in range(8):
                    wh = s.whi[:, ct % 2, k, :]
                    wl = s.wlo[:, ct % 2, k, :]
                    s.c_mm.inc(t.matmul(pb[:, 0:Tc], wh, s.xh[:, k, 0:Tc], start=(k == 0), stop=False))
                    s.c_mm.inc(t.matmul(pb[:, 0:Tc], wh, s.xl[:, k, 0:Tc], start=False, stop=False))
                    s.c_mm.inc(t.matmul(pb[:, 0:Tc], wl, s.xh[:, k, 0:Tc], start=False, stop=(k == 7)))

        @blk.scalar
        def _(sc):
            sc.wait_ge(s.c_xd.s, xd_prev)

            def sigma_chain(j):
                if j >= 2:
                    sc.wait_ge(s.c_sil.s, sil0 + (j - 2) + 1)
                s.c_sc.inc(sc.activation(s.sgt[:, j % 2, 0:Tc], s.ucac[:, j % 2, 0:Tc], AF.Exp, scale=-1.0))
                sc.wait_ge(s.c_sc.s, s.c_sc.n)
                s.c_sc.inc(sc.activation(s.sgt[:, j % 2, 0:Tc], s.sgt[:, j % 2, 0:Tc], AF.Ln, bias=1.0))
                sc.wait_ge(s.c_sc.s, s.c_sc.n)
                s.c_zev.inc(sc.activation(s.sgt[:, j % 2, 0:Tc], s.sgt[:, j % 2, 0:Tc], AF.Exp, scale=-1.0))

            for ct in range(NB):
                pb = s.p_a if ct % 2 == 0 else s.p_b
                sc.wait_ge(s.c_mm.s, mm0 + 24 * (ct + 1))
                if ct >= 2:
                    sc.wait_ge(s.c_cv.s, cv0 + (ct - 2) + 1)
                for k in range(4):
                    s.c_ev.inc(sc.activation(s.utmp[:, ct % 2, k, 3:3 + Tc], pb[:, 0:Tc],
                                             AF.Identity, scale=s.convw_sb[:, ct, k:k + 1]))
                if ct >= 2:
                    sigma_chain(ct - 2)
            for j in (NB - 2, NB - 1):
                sc.wait_ge(s.c_cv.s, cv0 + j + 1)
                sigma_chain(j)

        @blk.vector
        def _(v):
            v.wait_ge(s.c_spl.s, spl_prev)

            def sigma_mult(j):
                v.wait_ge(s.c_zev.s, zev_sil0 + j + 1)
                s.c_sil.inc(v.tensor_tensor(s.ubar[:, j, 0:Tc], s.ucac[:, j % 2, 0:Tc],
                                            s.sgt[:, j % 2, 0:Tc], AL.mult))

            for ct in range(NB):
                v.wait_ge(s.c_ev.s, ev0 + 4 * (ct + 1))
                s.c_hc.inc(v.tensor_copy(s.utmp[:, ct % 2, :, 0:3], s.uhalo[:, par, ct, :, :]))
                s.c_hc.inc(v.tensor_copy(s.uhalo[:, 1 - par, ct, :, :], s.utmp[:, ct % 2, :, Tc:Tc + 3]))
                if ct >= 2:
                    sigma_mult(ct - 2)
            for j in (NB - 2, NB - 1):
                sigma_mult(j)

        @blk.gpsimd
        def _(g):
            for cp in range(NB // 2):
                c0, c1 = 2 * cp, 2 * cp + 1
                g.wait_ge(s.c_hc.s, hc0 + 2 * c1 + 2)
                if c0 >= 2:
                    g.wait_ge(s.c_sil.s, sil0 + (c1 - 2) + 1)
                gplast = {}
                for k in range(1, 4):
                    for ct in (c0, c1):
                        if k > 1:
                            g.wait_ge(s.c_gp.s, gplast[ct])
                        if k == 1:
                            ii = g.tensor_tensor(s.ucac[:, ct % 2, 0:Tc],
                                                 s.utmp[:, ct % 2, 0, 0:Tc],
                                                 s.utmp[:, ct % 2, 1, 1:1 + Tc], AL.add)
                        else:
                            ii = g.tensor_tensor(s.ucac[:, ct % 2, 0:Tc],
                                                 s.ucac[:, ct % 2, 0:Tc],
                                                 s.utmp[:, ct % 2, k, k:k + Tc], AL.add)
                        if k == 3:
                            s.c_cv.inc(ii)
                        else:
                            s.c_gp.inc(ii)
                            gplast[ct] = s.c_gp.n

    def emit_m1_chunk(self, off, Tc, halo):
        s, nc = self, self.nc
        dl_prev, ya_prev, spl_prev = s.c_dl.n, s.c_ya.n, s.c_spl.n
        self.emit_m1_inproj_conv(Tc)
        sil_all = s.c_sil.n
        xd0, bev0, dt0 = s.c_xd.n, s.c_bev.n, s.c_dt.n
        if (blk := self.blk) or True:
            @blk.tensor
            def _(t):
                t.wait_ge(s.c_sil.s, sil_all)
                t.wait_ge(s.c_dl.s, dl_prev)
                for k in range(NB):
                    s.c_xd.inc(t.matmul(s.p_d[0:17, 0:Tc], s.xp_sb[:, k, :], s.ubar[:, k, 0:Tc],
                                        start=(k == 0), stop=(k == NB - 1)))

            @blk.scalar
            def _(sc):
                sc.wait_ge(s.c_xd.s, xd0 + NB)
                s.c_bev.inc(sc.activation(s.brow[:, 0:Tc], s.p_d[0:17, 0:Tc], AF.Copy))



            @blk.tensor
            def _(t):
                t.wait_ge(s.c_bev.s, bev0 + 1)
                s.c_dt.inc(t.matmul(s.p_d[:, 0:Tc], s.ones_sb[0:1, 0:128], s.brow[0:1, 0:Tc],
                                    start=True, stop=True))

            @blk.scalar
            def _(sc):
                sc.wait_ge(s.c_dt.s, dt0 + 1)
                sc.wait_ge(s.c_ya.s, ya_prev)
                for b in range(NB):
                    s.c_sc.inc(sc.activation(s.delta[:, b, 0:Tc], s.p_d[:, 0:Tc], AF.Exp,
                                             bias=s.dtb_sb[:, b:b + 1], scale=s.dtw_sb[:, b:b + 1]))
                    sc.wait_ge(s.c_sc.s, s.c_sc.n)
                    s.c_dl.inc(sc.activation(s.delta[:, b, 0:Tc], s.delta[:, b, 0:Tc], AF.Ln,
                                             bias=1.0))

        dl_all = s.c_dl.n
        brm0, bre0 = s.c_brm.n, s.c_bre.n
        sa0, su0, sh0, sp0, ya0 = s.c_sa.n, s.c_su.n, s.c_sh.n, s.c_sp.n, s.c_ya.n
        bev_all = s.c_bev.n
        bv_vals = []
        if (blk := self.blk) or True:
            @blk.sync
            def _(sync):
                sync.wait_ge(s.c_bev.s, bev_all)
                for n in range(NS):
                    cb = s.c_bv0 if n % 2 == 0 else s.c_bv1
                    if n >= 2:
                        sync.wait_ge(s.c_brm.s, brm0 + (n - 1))
                    cb.inc(sync.dma_start(out=s.bvec[0:1, n % 2, 0:Tc],
                                          in_=s.brow[1 + n:2 + n, 0:Tc]), 16)
                    bv_vals.append((cb, cb.n))

            @blk.tensor
            def _(t):
                for n in range(NS):
                    t.wait_ge(bv_vals[n][0].s, bv_vals[n][1])
                    if n >= 2:
                        t.wait_ge(s.c_su.s, su0 + (n - 1) * NB)
                        if halo:
                            t.wait_ge(s.c_sh.s, sh0 + (n - 1) * NB)
                        else:
                            t.wait_ge(s.c_sp.s, sp0 + (n - 1) * NB)
                    pb = s.p_e if n % 2 == 0 else s.p_f
                    s.c_brm.inc(t.matmul(pb[:, 0:Tc], s.ones_sb[0:1, 0:128], s.bvec[0:1, n % 2, 0:Tc],
                                         start=True, stop=True))

            @blk.scalar
            def _(sc):
                sc.wait_ge(s.c_dl.s, dl_all)
                for n in range(NS):
                    pb = s.p_e if n % 2 == 0 else s.p_f
                    sc.wait_ge(s.c_brm.s, brm0 + n + 1)
                    s.c_bre.inc(sc.activation(s.brep[:, n % 2, 0:Tc], pb[:, 0:Tc], AF.Copy))
                    for b in range(NB):
                        i = n * NB + b
                        col = b * NS + n
                        if i >= 2:
                            sc.wait_ge(s.c_sh.s, sh0 + (i - 2) + 1)
                        s.c_sa.inc(sc.activation(s.sa[:, i % 2, 0:Tc], s.delta[:, b, 0:Tc], AF.Exp,
                                                 scale=s.a_sb[:, col:col + 1]))

            @blk.gpsimd
            def _(g):
                g.wait_ge(s.c_dl.s, dl_all)
                for i in range(NI):
                    n, b = i // NB, i % NB
                    g.wait_ge(s.c_bre.s, bre0 + n + 1)
                    if i >= 2:
                        g.wait_ge(s.c_sh.s, sh0 + (i - 2) + 1)
                    s.c_su.inc(g.tensor_tensor(s.su[:, i % 2, 0:Tc], s.delta[:, b, 0:Tc],
                                               s.brep[:, n % 2, 0:Tc], AL.mult))
                    if (not halo) and i >= 1:
                        ip = i - 1
                        np_, bp = ip // NB, ip % NB
                        if np_ > 0:
                            g.wait_ge(s.c_sp.s, sp0 + ip + 1)
                            if ip >= NB:
                                g.wait_ge(s.c_ya.s, ya0 + (ip - NB) + 1)
                            s.c_ya.inc(g.tensor_tensor(s.yac[:, bp, 0:Tc], s.yac[:, bp, 0:Tc],
                                                       s.su[:, 2 + ip % 2, 0:Tc], AL.add))
                        else:
                            s.c_ya.inc(g.engine_nop())
                if not halo:
                    ip = NI - 1
                    bp = ip % NB
                    g.wait_ge(s.c_sp.s, sp0 + ip + 1)
                    g.wait_ge(s.c_ya.s, ya0 + (ip - NB) + 1)
                    s.c_ya.inc(g.tensor_tensor(s.yac[:, bp, 0:Tc], s.yac[:, bp, 0:Tc],
                                               s.su[:, 2 + ip % 2, 0:Tc], AL.add))

            @blk.vector
            def _(v):
                if not halo:
                    v.wait_ge(s.c_spl.s, spl_prev)
                for i in range(NI):
                    n, b = i // NB, i % NB
                    col = b * NS + n
                    v.wait_ge(s.c_sa.s, sa0 + i + 1)
                    v.wait_ge(s.c_su.s, su0 + i + 1)
                    s.c_dv.inc(v.tensor_tensor_scan(s.sh[:, i % 2, 0:Tc], s.sa[:, i % 2, 0:Tc],
                                                    s.su[:, i % 2, 0:Tc], s.state[:, col:col + 1],
                                                    AL.mult, AL.add))
                    v.wait_ge(s.c_dv.s, s.c_dv.n)
                    s.c_sh.inc(v.tensor_copy(s.state[:, col:col + 1], s.sh[:, i % 2, Tc - 1:Tc]))
                    if not halo:
                        if n == 0:
                            s.c_sp.inc(v.tensor_tensor(s.yac[:, b, 0:Tc], s.sh[:, i % 2, 0:Tc],
                                                       s.brep[:, n % 2, 0:Tc], AL.mult))
                        else:
                            if i >= 2:
                                v.wait_ge(s.c_ya.s, ya0 + (i - 2) + 1)
                            s.c_sp.inc(v.tensor_tensor(s.su[:, 2 + i % 2, 0:Tc], s.sh[:, i % 2, 0:Tc],
                                                       s.brep[:, n % 2, 0:Tc], AL.mult))

        if not halo:
            real_off = off - self.HALO
            ya_all, sil_all2 = s.c_ya.n, s.c_sil.n
            if (blk := self.blk) or True:
                @blk.sync
                def _(sync):
                    sync.wait_ge(s.c_ya.s, ya_all)
                    sync.wait_ge(s.c_sil.s, sil_all2)
                    s.c_spl.inc(sync.dma_start(out=s.ub_dram[:, :, real_off:real_off + Tc],
                                               in_=s.ubar[:, :, 0:Tc]), 16)
                    s.c_spl.inc(sync.dma_start(out=s.y_dram[:, :, real_off:real_off + Tc],
                                               in_=s.yac[:, :, 0:Tc]), 16)

    def emit_m2(self):
        s, nc = self, self.nc
        T = self.T
        m2 = ExitStack()
        self.wbig = self.sbuf(m2, "wbig", [128, 8, DI], F16)
        self.y3h = self.sbuf(m2, "y3h", [128, 2, 512], F16)
        self.y3l = self.sbuf(m2, "y3l", [128, 2, 512], F16)

        self.barrier_all([s.c_spl, s.c_sh, s.c_ya, s.c_out])
        if (blk := self.blk) or True:
            @blk.sync
            def _(sync):
                s.c_wld.inc(sync.dma_start(out=s.wbig.ap(), in_=s.wz16.ap()[:, 0, :, :]), 16)

        for ci in range(self.NCH):
            off = self.HALO + ci * T
            self.emit_xchunk(off, T)
            self.emit_m2_chunk(ci * T, T)
        m2.close()
        self.m12.close()

    def emit_m2_chunk(self, real_off, Tc):
        s, nc = self, self.nc
        zev0, g10, g20, y3c0, y3l0 = s.c_zev.n, s.c_gt1.n, s.c_gt2.n, s.c_y3c.n, s.c_y3l.n
        zgt0 = s.c_rt2.n
        y3s_even0, y3s_odd0 = s.c_y3s0.n, s.c_y3s1.n
        xin0 = s.c_xin.n
        if (blk := self.blk) or True:
            @blk.sync
            def _(sync):
                sync.wait_ge(s.c_gt2.s, g20)
                s.c_xin.inc(sync.dma_start(out=s.ubar.ap(), in_=s.ub_dram[:, :, real_off:real_off + Tc]), 16)
                s.c_xin.inc(sync.dma_start(out=s.yac.ap(), in_=s.y_dram[:, :, real_off:real_off + Tc]), 16)

        sgt2 = s.y3h
        zsg0 = s.c_zev.n

        def evac_z(sc, ct, pb):
            if ct >= 2:
                sc.wait_ge(s.c_gt2.s, g20 + (ct - 2) + 1)
            s.c_sc.inc(sc.activation(s.ucac[:, ct % 2, 0:Tc], pb[:, 0:Tc], AF.Exp, scale=-1.0))
            sc.wait_ge(s.c_sc.s, s.c_sc.n)
            s.c_sc.inc(sc.activation(s.ucac[:, ct % 2, 0:Tc], s.ucac[:, ct % 2, 0:Tc], AF.Ln, bias=1.0))
            sc.wait_ge(s.c_sc.s, s.c_sc.n)
            s.c_zev.inc(sc.activation(s.ucac[:, ct % 2, 0:Tc], s.ucac[:, ct % 2, 0:Tc], AF.Exp, scale=-1.0))

        self.emit_proj(s.wz16, Tc, evac_z, evctr=s.c_rt2)

        if (blk := self.blk) or True:
            @blk.vector
            def _(v):
                v.wait_ge(s.c_xin.s, xin0 + 32)
                for ct in range(NB):
                    s.c_gt1.inc(v.scalar_tensor_tensor(s.yac[:, ct, 0:Tc], s.ubar[:, ct, 0:Tc],
                                                       s.dpar_sb[:, ct:ct + 1], s.yac[:, ct, 0:Tc],
                                                       AL.mult, AL.add))

            @blk.vector
            def _(v):
                for ct in range(NB):
                    pb = s.p_a if ct % 2 == 0 else s.p_b
                    v.wait_ge(s.c_zev.s, zev0 + ct + 1)
                    s.c_rt2.inc(v.tensor_tensor(s.ucac[:, ct % 2, 0:Tc], pb[:, 0:Tc],
                                                s.ucac[:, ct % 2, 0:Tc], AL.mult))

            @blk.gpsimd
            def _(g):
                for ct in range(NB):
                    g.wait_ge(s.c_rt2.s, zgt0 + ct + 1)
                    g.wait_ge(s.c_gt1.s, g10 + ct + 1)
                    s.c_gt2.inc(g.tensor_tensor(s.yac[:, ct, 0:Tc], s.yac[:, ct, 0:Tc],
                                                s.ucac[:, ct % 2, 0:Tc], AL.mult))

            @blk.vector
            def _(v):
                for ct in range(NB):
                    v.wait_ge(s.c_gt2.s, g20 + ct + 1)
                    cy = s.c_y3s0 if ct % 2 == 0 else s.c_y3s1
                    base = y3s_even0 if ct % 2 == 0 else y3s_odd0
                    v.wait_ge(cy.s, base + 32 * (ct // 2))
                    s.c_y3c.inc(v.tensor_copy(s.y3h[:, ct % 2, 0:Tc], s.yac[:, ct, 0:Tc]))

            @blk.gpsimd
            def _(g):
                for ct in range(NB):
                    g.wait_ge(s.c_y3c.s, y3c0 + ct + 1)
                    s.c_y3l.inc(g.tensor_tensor(s.y3l[:, ct % 2, 0:Tc], s.yac[:, ct, 0:Tc],
                                                s.y3h[:, ct % 2, 0:Tc], AL.subtract))

            @blk.sync
            def _(sync):
                for ct in range(NB):
                    cy = s.c_y3s0 if ct % 2 == 0 else s.c_y3s1
                    sync.wait_ge(s.c_y3l.s, y3l0 + ct + 1)
                    cy.inc(sync.dma_start(out=s.y3_dram[:, 0, ct, real_off:real_off + Tc],
                                          in_=s.y3h[:, ct % 2, 0:Tc]), 16)
                    cy.inc(sync.dma_start(out=s.y3_dram[:, 1, ct, real_off:real_off + Tc],
                                          in_=s.y3l[:, ct % 2, 0:Tc]), 16)

    def emit_m3(self):
        s, nc = self, self.nc
        T = self.T
        m3 = ExitStack()
        self.wbig2 = self.sbuf(m3, "wbig2", [128, 2, 16, DM], F16)
        self.y3hc = self.sbuf(m3, "y3hc", [128, NB, 512], F16)
        self.y3lc = self.sbuf(m3, "y3lc", [128, NB, 512], F16)
        self.om = self.sbuf(m3, "om", [128, 8, 512])
        self.x1t = self.sbuf(m3, "x1t", [128, 4, DM])
        self.x1T = self.sbuf(m3, "x1T", [128, 8, 512], F32)
        self.xr = self.sbuf(m3, "xr", [128, 4, DM])
        self.lnq = self.sbuf(m3, "lnq", [128, DM])
        self.lnc = self.sbuf(m3, "lnc", [128, 4, 8])
        self.lg8 = self.sbuf(m3, "lg8", [8, 512])
        self.lgt = self.sbuf(m3, "lgt", [128, 4, 8])
        self.sfx = self.sbuf(m3, "sfx", [128, 4, 8])

        self.barrier_all([s.c_y3s0, s.c_y3s1, s.c_gt2, s.c_zev])
        if (blk := self.blk) or True:
            @blk.sync
            def _(sync):
                s.c_wld.inc(sync.dma_start(out=s.wbig2.ap(), in_=s.ow16.ap()), 16)

        for ci in range(self.NCH):
            self.emit_m3_chunk(ci * T, T)
        m3.close()

    def emit_m3_chunk(self, real_off, Tc):
        s, nc = self, self.nc
        nsub = Tc // 128
        xin0 = s.c_xin.n
        oev0, otp0, ocp0 = s.c_oev.n, s.c_otp.n, s.c_ocp.n
        mm0 = s.c_mm.n
        ln10, ln20 = s.c_ln1.n, s.c_ln2.n
        rt10, rt20, rt30 = s.c_rt1.n, s.c_rt2.n, s.c_rt3.n
        sfx0, out_prev = s.c_sfx.n, s.c_out.n
        lgr_prev, sct_prev = s.c_lgr.n, s.c_sct.n

        if (blk := self.blk) or True:
            @blk.sync
            def _(sync):
                sync.wait_ge(s.c_oev.s, oev0)
                sync.wait_ge(s.c_out.s, out_prev)
                s.c_xin.inc(sync.dma_start(out=s.y3hc[:, :, 0:Tc], in_=s.y3_dram[:, 0, :, real_off:real_off + Tc]), 16)
                s.c_xin.inc(sync.dma_start(out=s.y3lc[:, :, 0:Tc], in_=s.y3_dram[:, 1, :, real_off:real_off + Tc]), 16)
                sync.wait_ge(s.c_ocp.s, ocp0)
                for j in range(nsub):
                    s.c_xr.inc(sync.dma_start(out=s.xr[:, j, :],
                                              in_=s.xs[s.HALO + real_off + j * 128: s.HALO + real_off + (j + 1) * 128, :]), 16)

            @blk.tensor
            def _(t):
                t.wait_ge(s.c_wld.s, s.c_wld.n)
                t.wait_ge(s.c_xin.s, xin0 + 32)
                for m in range(8):
                    pb = s.p_a if m % 2 == 0 else s.p_b
                    if m >= 2:
                        t.wait_ge(s.c_oev.s, oev0 + (m - 2) + 1)
                    for k in range(16):
                        wh = s.wbig2[:, 0, k, ts(m, 128)]
                        wl = s.wbig2[:, 1, k, ts(m, 128)]
                        s.c_mm.inc(t.matmul(pb[:, 0:Tc], wh, s.y3hc[:, k, 0:Tc], start=(k == 0), stop=False))
                        s.c_mm.inc(t.matmul(pb[:, 0:Tc], wh, s.y3lc[:, k, 0:Tc], start=False, stop=False))
                        s.c_mm.inc(t.matmul(pb[:, 0:Tc], wl, s.y3hc[:, k, 0:Tc], start=False, stop=(k == 15)))

            @blk.scalar
            def _(sc):
                for m in range(8):
                    pb = s.p_a if m % 2 == 0 else s.p_b
                    sc.wait_ge(s.c_mm.s, mm0 + 48 * (m + 1))
                    s.c_oev.inc(sc.activation(s.om[:, m, 0:Tc], pb[:, 0:Tc], AF.Identity,
                                              bias=s.ob_sb[:, m:m + 1]))

            @blk.tensor
            def _(t):
                for j in range(nsub):
                    for mg in range(2):
                        u = j * 2 + mg
                        pb = s.p_e if u % 2 == 0 else s.p_f
                        t.wait_ge(s.c_oev.s, oev0 + mg * 4 + 4)
                        t.wait_ge(s.c_ocp.s, ocp0 + u - 1 + 1 if u >= 1 else ocp0)
                        for q in range(4):
                            m = mg * 4 + q
                            s.c_otp.inc(t.transpose(pb[:, ts(q, 128)], s.om[:, m, ts(j, 128)], s.ident[:, :]))

            @blk.vector
            def _(v):
                v.wait_ge(s.c_out.s, out_prev)
                v.wait_ge(s.c_xr.s, s.c_xr.n)
                for j in range(nsub):
                    for mg in range(2):
                        u = j * 2 + mg
                        pb = s.p_e if u % 2 == 0 else s.p_f
                        v.wait_ge(s.c_otp.s, otp0 + 4 * (u + 1))
                        s.c_ocp.inc(v.tensor_tensor(s.x1t[:, j, ts(mg, 512)], pb[:, :],
                                                    s.xr[:, j, ts(mg, 512)], AL.add))

            @blk.vector
            def _(v):
                for j in range(nsub):
                    v.wait_ge(s.c_ocp.s, ocp0 + 2 * (j + 1))
                    s.c_dv.inc(v.tensor_tensor(s.lnq[:, :], s.x1t[:, j, :], s.x1t[:, j, :], AL.mult))
                    v.wait_ge(s.c_dv.s, s.c_dv.n)
                    s.c_dv.inc(v.reduce_sum(s.lnc[:, j, 1:2], s.lnq[:, :], axis=mybir.AxisListType.X))
                    s.c_dv.inc(v.reduce_sum(s.lnc[:, j, 0:1], s.x1t[:, j, :], axis=mybir.AxisListType.X))
                    v.wait_ge(s.c_dv.s, s.c_dv.n)
                    s.c_dv.inc(v.tensor_scalar_mul(s.lnc[:, j, 2:3], s.lnc[:, j, 0:1], 1.0 / DM))
                    s.c_dv.inc(v.tensor_scalar_mul(s.lnc[:, j, 3:4], s.lnc[:, j, 1:2], 1.0 / DM))
                    v.wait_ge(s.c_dv.s, s.c_dv.n)
                    s.c_dv.inc(v.tensor_tensor(s.lnc[:, j, 4:5], s.lnc[:, j, 2:3], s.lnc[:, j, 2:3], AL.mult))
                    v.wait_ge(s.c_dv.s, s.c_dv.n)
                    s.c_dv.inc(v.tensor_tensor(s.lnc[:, j, 4:5], s.lnc[:, j, 3:4], s.lnc[:, j, 4:5], AL.subtract))
                    v.wait_ge(s.c_dv.s, s.c_dv.n)
                    s.c_ln1.inc(v.tensor_scalar_add(s.lnc[:, j, 4:5], s.lnc[:, j, 4:5], 1e-5))

            @blk.scalar
            def _(sc):
                for j in range(nsub):
                    sc.wait_ge(s.c_ln1.s, ln10 + j + 1)
                    s.c_sc.inc(sc.activation(s.lnc[:, j, 5:6], s.lnc[:, j, 4:5], AF.Ln))
                    sc.wait_ge(s.c_sc.s, s.c_sc.n)
                    s.c_ln2.inc(sc.activation(s.lnc[:, j, 6:7], s.lnc[:, j, 5:6], AF.Exp,
                                              scale=-0.5))

            @blk.vector
            def _(v):
                for j in range(nsub):
                    v.wait_ge(s.c_ln2.s, ln20 + j + 1)
                    s.c_dv.inc(v.tensor_tensor(s.lnc[:, j, 7:8], s.lnc[:, j, 2:3], s.lnc[:, j, 6:7], AL.mult))
                    v.wait_ge(s.c_dv.s, s.c_dv.n)
                    s.c_ln3.inc(v.tensor_scalar_mul(s.lnc[:, j, 7:8], s.lnc[:, j, 7:8], -1.0))

            @blk.scalar
            def _(sc):
                for j in range(nsub):
                    sc.wait_ge(s.c_ln3.s, s.c_ln3.n - nsub + j + 1)
                    s.c_sfx.inc(sc.activation(s.x1t[:, j, :], s.x1t[:, j, :], AF.Identity,
                                              scale=s.lnc[:, j, 6:7], bias=s.lnc[:, j, 7:8]))

            @blk.tensor
            def _(t):
                for j in range(nsub):
                    for mg in range(2):
                        u = j * 2 + mg
                        pb = s.p_e if u % 2 == 0 else s.p_f
                        t.wait_ge(s.c_sfx.s, sfx0 + j + 1)
                        if u < 2:
                            t.wait_ge(s.c_ocp.s, ocp0 + 2 * nsub - 1 + u)
                        t.wait_ge(s.c_rt1.s, rt10 + u - 1 + 1 if u >= 1 else rt10)
                        for q in range(4):
                            mc = mg * 4 + q
                            s.c_otp.inc(t.transpose(pb[:, ts(q, 128)], s.x1t[:, j, ts(mc, 128)], s.ident[:, :]))

            @blk.vector
            def _(v):
                base = s.c_otp.n - 2 * nsub * 4
                for j in range(nsub):
                    for mg in range(2):
                        u = j * 2 + mg
                        pb = s.p_e if u % 2 == 0 else s.p_f
                        v.wait_ge(s.c_otp.s, base + 4 * (u + 1))
                        s.c_rt1.inc(v.tensor_copy(s.x1T[:, ts(mg, 4).start:ts(mg, 4).stop, ts(j, 128)]
                                                  if False else s.x1T[:, mg * 4:mg * 4 + 4, ts(j, 128)], pb[:, :]))

            @blk.tensor
            def _(t):
                t.wait_ge(s.c_rt1.s, rt10 + 2 * nsub)
                t.wait_ge(s.c_bev.s, s.c_bev.n)
                for k in range(8):
                    s.c_rt2.inc(t.matmul(s.p_d[0:8, 0:Tc], s.rw_sb[:, k, :], s.x1T[:, k, 0:Tc],
                                         start=(k == 0), stop=(k == 7)))

            @blk.scalar
            def _(sc):
                sc.wait_ge(s.c_rt2.s, rt20 + 8)
                if s.FFN:
                    sc.wait_ge(s.c_out.s, out_prev)
                s.c_rt3.inc(sc.activation(s.lg8[:, 0:Tc], s.p_d[0:8, 0:Tc], AF.Copy))

            @blk.tensor
            def _(t):
                t.wait_ge(s.c_rt3.s, rt30 + 1)
                t.wait_ge(s.c_lgr.s, lgr_prev)
                t.wait_ge(s.c_sct.s, sct_prev)
                for j in range(nsub):
                    s.c_otp.inc(t.transpose(s.p_h[:, ts(j, 8)], s.lg8[0:8, ts(j, 128)], s.ident[0:8, 0:8]))

            @blk.vector
            def _(v):
                base = s.c_otp.n - nsub
                v.wait_ge(s.c_otp.s, base + nsub)
                for j in range(nsub):
                    s.c_lgr.inc(v.tensor_tensor(s.lgt[:, j, :], s.p_h[:, ts(j, 8)], s.rb_sb[:, :], AL.add))
                    v.wait_ge(s.c_lgr.s, s.c_lgr.n)
                    s.c_dv.inc(v.reduce_max(s.lnc[:, j, 0:1], s.lgt[:, j, :], axis=mybir.AxisListType.X))
                    v.wait_ge(s.c_dv.s, s.c_dv.n)
                    s.c_ln1.inc(v.tensor_scalar_mul(s.lnc[:, j, 1:2], s.lnc[:, j, 0:1], -1.0))

            @blk.scalar
            def _(sc):
                base = s.c_ln1.n - nsub
                for j in range(nsub):
                    sc.wait_ge(s.c_ln1.s, base + j + 1)
                    s.c_ln2.inc(sc.activation(s.sfx[:, j, :], s.lgt[:, j, :], AF.Exp,
                                              bias=s.lnc[:, j, 1:2]))

            @blk.vector
            def _(v):
                base = s.c_ln2.n - nsub
                for j in range(nsub):
                    v.wait_ge(s.c_ln2.s, base + j + 1)
                    s.c_dv.inc(v.reduce_sum(s.lnc[:, j, 2:3], s.sfx[:, j, :], axis=mybir.AxisListType.X))
                    v.wait_ge(s.c_dv.s, s.c_dv.n)
                    s.c_dv.inc(v.reciprocal(s.lnc[:, j, 3:4], s.lnc[:, j, 2:3]))
                    v.wait_ge(s.c_dv.s, s.c_dv.n)
                    s.c_rt1.inc(v.tensor_scalar(s.sfx[:, j, :], s.sfx[:, j, :],
                                                s.lnc[:, j, 3:4], None, AL.mult))

            if s.FFN:
                sct_p0 = s.c_otp.n

                @blk.tensor
                def _(t):
                    base = s.c_rt1.n - nsub
                    t.wait_ge(s.c_lgr.s, s.c_lgr.n)
                    for j in range(nsub):
                        t.wait_ge(s.c_rt1.s, base + j + 1)
                        s.c_otp.inc(t.transpose(s.p_h[0:8, ts(j, 128)], s.sfx[:, j, :],
                                                s.ident[:, :]))

                @blk.scalar
                def _(sc):
                    sc.wait_ge(s.c_otp.s, sct_p0 + nsub)
                    s.c_sct.inc(sc.activation(s.lg8[:, 0:nsub * 128], s.p_h[0:8, 0:nsub * 128],
                                              AF.Copy))

                @blk.sync
                def _(sync):
                    base = s.c_rt1.n - nsub
                    sync.wait_ge(s.c_sct.s, s.c_sct.n)
                    s.c_out.inc(sync.dma_start(out=s.scT_dram[:, real_off:real_off + Tc],
                                               in_=s.lg8[:, 0:Tc]), 16)
                    for j in range(nsub):
                        sync.wait_ge(s.c_rt1.s, base + j + 1)
                        s.c_out.inc(sync.dma_start(out=s.x1_out[real_off + j * 128: real_off + (j + 1) * 128, :],
                                                   in_=s.x1t[:, j, :]), 16)
            else:
                @blk.sync
                def _(sync):
                    base = s.c_rt1.n - nsub
                    for j in range(nsub):
                        sync.wait_ge(s.c_rt1.s, base + j + 1)
                        s.c_out.inc(sync.dma_start(out=s.sc_out[real_off + j * 128: real_off + (j + 1) * 128, :],
                                                   in_=s.sfx[:, j, :]), 16)
                        s.c_out.inc(sync.dma_start(out=s.x1_out[real_off + j * 128: real_off + (j + 1) * 128, :],
                                                   in_=s.x1t[:, j, :]), 16)

    def emit_ffn_split(self):
        s, nc = self, self.nc
        DFF = self.DFF
        fctx = ExitStack()
        wst = self.sbuf(fctx, "wst", [128, 2, max(DFF, DM)])
        fh = self.sbuf(fctx, "fh", [128, 2, 512], F16)
        fl = self.sbuf(fctx, "fl", [128, 2, 512], F16)
        ld0, hh0, ll0, st0 = s.c_ld.n, s.c_xch.n, s.c_xcl.n, s.c_w16.n
        jobs = [(s.w1_in, s.w116, DFF, DM // 128), (s.w2_in, s.w216, DM, DFF // 128)]
        pieces = []
        for (src, dst, W, natile) in jobs:
            for a in range(natile):
                pieces.append((src, dst, W, a))
        qof = [0]
        for (_, _, W, _) in pieces:
            qof.append(qof[-1] + W // 512)

        wvals = []
        svals = []
        if (blk := self.blk) or True:
            @blk.sync
            def _(sync):
                qs = 0
                for u in range(len(pieces) + 1):
                    if u < len(pieces):
                        src, dst, W, a = pieces[u]
                        cw = s.c_w1s0 if u % 2 == 0 else s.c_w1s1
                        if u >= 2:
                            sync.wait_ge(s.c_xcl.s, ll0 + qof[u - 1])
                        cw.inc(sync.dma_start(out=wst[:, u % 2, 0:W],
                                              in_=src.ap()[ts(a, 128), :]), 16)
                        wvals.append((cw, cw.n))
                    if u >= 1:
                        psrc, pdst, pW, pa = pieces[u - 1]
                        for q in range(pW // 512):
                            cst = s.c_w2s0 if qs % 2 == 0 else s.c_w2s1
                            sync.wait_ge(s.c_xcl.s, ll0 + qs + 1)
                            cst.inc(sync.dma_start(out=pdst[:, 0, pa, ts(q, 512)],
                                                   in_=fh[:, qs % 2, :]), 16)
                            cst.inc(sync.dma_start(out=pdst[:, 1, pa, ts(q, 512)],
                                                   in_=fl[:, qs % 2, :]), 16)
                            svals.append((cst, cst.n))
                            qs += 1

            @blk.scalar
            def _(sc):
                qq = 0
                for u, (src, dst, W, a) in enumerate(pieces):
                    sc.wait_ge(wvals[u][0].s, wvals[u][1])
                    for q in range(W // 512):
                        if qq >= 2:
                            sc.wait_ge(svals[qq - 2][0].s, svals[qq - 2][1])
                        s.c_xch.inc(sc.activation(fh[:, qq % 2, :],
                                                  wst[:, u % 2, ts(q, 512)], AF.Copy))
                        qq += 1

            @blk.vector
            def _(v):
                qq = 0
                for u, (src, dst, W, a) in enumerate(pieces):
                    for q in range(W // 512):
                        v.wait_ge(s.c_xch.s, hh0 + qq + 1)
                        if qq >= 2:
                            v.wait_ge(svals[qq - 2][0].s, svals[qq - 2][1])
                        s.c_xcl.inc(v.tensor_tensor(fl[:, qq % 2, :],
                                                    wst[:, u % 2, ts(q, 512)],
                                                    fh[:, qq % 2, :], AL.subtract))
                        qq += 1

        fctx.close()

    def emit_ffn_bvec(self):
        s, nc = self, self.nc
        DFF = self.DFF
        fctx = ExitStack()
        brow2 = self.sbuf(fctx, "brow2b", [1, max(DFF, DM)])
        ld0, tp0, ev0 = s.c_ld.n, s.c_tp.n, s.c_ev.n
        if (blk := self.blk) or True:
            @blk.sync
            def _(sync):
                s.c_ld.inc(sync.dma_start(out=brow2[0:1, 0:DFF], in_=s.b1_in.ap()), 16)

            @blk.tensor
            def _(t):
                t.wait_ge(s.c_ld.s, s.c_ld.n)
                t.wait_ge(s.c_k.s, s.c_k.n)
                for i in range(DFF // 128):
                    s.c_tp.inc(t.transpose(s.p_a[:, i:i + 1], brow2[0:1, ts(i, 128)],
                                           s.ident[0:1, 0:1]))

            @blk.scalar
            def _(sc):
                sc.wait_ge(s.c_tp.s, s.c_tp.n)
                s.c_ev.inc(sc.activation(s.b1_sb[:, :], s.p_a[:, 0:DFF // 128], AF.Copy))

            @blk.sync
            def _(sync):
                sync.wait_ge(s.c_ev.s, s.c_ev.n)
                s.c_ld.inc(sync.dma_start(out=brow2[0:1, 0:DM], in_=s.b2_in.ap()), 16)

            @blk.tensor
            def _(t):
                t.wait_ge(s.c_ld.s, s.c_ld.n)
                t.wait_ge(s.c_ev.s, s.c_ev.n)
                for i in range(DM // 128):
                    s.c_tp.inc(t.transpose(s.p_a[:, i:i + 1], brow2[0:1, ts(i, 128)],
                                           s.ident[0:1, 0:1]))

            @blk.scalar
            def _(sc):
                sc.wait_ge(s.c_tp.s, s.c_tp.n)
                s.c_ev.inc(sc.activation(s.b2_sb[:, :], s.p_a[:, 0:DM // 128], AF.Copy))
        fctx.close()

    def emit_f1_ag(self):
        s = self
        if (blk := self.blk) or True:
            @blk.gpsimd
            def _(g):
                g.wait_ge(s.c_out.s, s.c_out.n)
                s.c_ag.inc(g.collective_compute(
                    "AllGather", AL.bypass, replica_groups=[list(range(8))],
                    ins=[s.x1_out.ap().opt()], outs=[s.ag_x1.ap().opt()]))
                s.c_ag.inc(g.collective_compute(
                    "AllGather", AL.bypass, replica_groups=[list(range(8))],
                    ins=[s.scT_dram.ap().opt()], outs=[s.ag_scT.ap().opt()]))

    def emit_f2_routing(self):
        s, nc = self, self.nc
        L, TOK, K = self.L, self.TOK, self.K
        L4 = TOK // 32
        fctx = ExitStack()
        s16 = self.sbuf(fctx, "s16", [32, L4])
        gtm = self.sbuf(fctx, "gtm", [32, L4])
        eqm = self.sbuf(fctx, "eqm", [32, L4])
        cum = self.sbuf(fctx, "cum", [32, L4])
        sel = self.sbuf(fctx, "sel", [32, L4])
        one6 = self.sbuf(fctx, "one6", [32, L4])
        o66 = self.sbuf(fctx, "o66", [32, 32])
        sc9 = self.sbuf(fctx, "sc9", [32, 16])
        rowb = self.sbuf(fctx, "rowb", [1, 128])
        offs_u = self.sbuf(fctx, "offs_u", [1, 32], U32)
        iot = self.sbuf(fctx, "iot", [32, L4], I16)
        sl16 = self.sbuf(fctx, "sl16", [32, L4], I16)
        dfv = self.sbuf(fctx, "dfv", [32, L4])
        glo = self.sbuf(fctx, "glo", [32, L4], I16)
        ghi = self.sbuf(fctx, "ghi", [32, L4], I16)
        icomp = self.sbuf(fctx, "icomp", [32, L4], I16)
        gcl = self.sbuf(fctx, "gcl", [32, L4], I16)
        gch = self.sbuf(fctx, "gch", [32, L4], I16)

        LO, HI, MID, CNT, CNTT, CND, CNDL, T0, T1, NEQ, EQC, CGT = range(12)
        bis0, gr0, cmp0, mg0 = s.c_bis.n, s.c_gr.n, s.c_cmp.n, s.c_mrg.n
        NIT = 42

        if (blk := self.blk) or True:
            @blk.gpsimd
            def _(g):
                g.wait_ge(s.c_ag.s, s.c_ag.n)
                for r in range(32):
                    rr = g.value_load(s.scrow_sb[0:1, r:r + 1], min_val=0, max_val=63)
                    s.c_sdm.inc(g.dma_start(
                        out=s16[r:r + 1, :],
                        in_=s.ag_scT.ap()[bass.ds(rr, 1), ts(r % 4, L4)]), 16)

            @blk.vector
            def _(v):
                def dv(ii):
                    s.c_dv.inc(ii)
                    v.wait_ge(s.c_dv.s, s.c_dv.n)

                v.wait_ge(s.c_sdm.s, s.c_sdm.n)
                dv(v.memset(one6[:, :], 1.0))
                dv(v.memset(o66[:, :], 1.0))
                dv(v.memset(sc9[:, LO:LO + 1], 0.0))
                dv(v.memset(sc9[:, HI:HI + 1], 1.0))
                for it in range(NIT):
                    dv(v.tensor_tensor(sc9[:, MID:MID + 1], sc9[:, LO:LO + 1],
                                       sc9[:, HI:HI + 1], AL.add))
                    dv(v.tensor_scalar_mul(sc9[:, MID:MID + 1], sc9[:, MID:MID + 1], 0.5))
                    s.c_bis.inc(v.tensor_scalar(gtm[:, :], s16[:, :], sc9[:, MID:MID + 1],
                                                0.0, AL.is_ge, AL.add,
                                                accum_out=sc9[:, CNT:CNT + 1]))
                    v.wait_ge(s.c_cmp.s, cmp0 + it + 1)
                    dv(v.tensor_scalar(sc9[:, CND:CND + 1], sc9[:, CNTT:CNTT + 1],
                                       float(K), None, AL.is_ge))
                    dv(v.tensor_scalar(sc9[:, CNDL:CNDL + 1], sc9[:, CNTT:CNTT + 1],
                                       float(K), None, AL.is_lt))
                    dv(v.tensor_tensor(sc9[:, T0:T0 + 1], sc9[:, CND:CND + 1],
                                       sc9[:, MID:MID + 1], AL.mult))
                    dv(v.tensor_tensor(sc9[:, T1:T1 + 1], sc9[:, CNDL:CNDL + 1],
                                       sc9[:, LO:LO + 1], AL.mult))
                    dv(v.tensor_tensor(sc9[:, LO:LO + 1], sc9[:, T0:T0 + 1],
                                       sc9[:, T1:T1 + 1], AL.add))
                    dv(v.tensor_tensor(sc9[:, T0:T0 + 1], sc9[:, CNDL:CNDL + 1],
                                       sc9[:, MID:MID + 1], AL.mult))
                    dv(v.tensor_tensor(sc9[:, T1:T1 + 1], sc9[:, CND:CND + 1],
                                       sc9[:, HI:HI + 1], AL.mult))
                    dv(v.tensor_tensor(sc9[:, HI:HI + 1], sc9[:, T0:T0 + 1],
                                       sc9[:, T1:T1 + 1], AL.add))
                s.c_bis.inc(v.tensor_scalar(gtm[:, :], s16[:, :], sc9[:, HI:HI + 1],
                                            0.0, AL.is_ge, AL.add,
                                            accum_out=sc9[:, CGT:CGT + 1]))
                s.c_bis.inc(v.tensor_scalar(eqm[:, :], s16[:, :], sc9[:, LO:LO + 1],
                                            0.0, AL.is_equal, AL.add,
                                            accum_out=sc9[:, EQC:EQC + 1]))

                v.wait_ge(s.c_cmp.s, cmp0 + NIT + 2)
                dv(v.tensor_tensor_scan(rowb[0:1, 32:64], s.ones_sb[0:1, 0:32],
                                        rowb[0:1, 0:32], 0.0, AL.mult, AL.add))
                dv(v.tensor_tensor(rowb[0:1, 32:64], rowb[0:1, 32:64],
                                   rowb[0:1, 0:32], AL.subtract))
                s.c_bis.inc(v.engine_nop())

                v.wait_ge(s.c_cmp.s, cmp0 + NIT + 3)
                dv(v.tensor_scalar_mul(sc9[:, NEQ:NEQ + 1], sc9[:, CNTT:CNTT + 1], -1.0))
                dv(v.tensor_scalar_add(sc9[:, NEQ:NEQ + 1], sc9[:, NEQ:NEQ + 1], float(K)))
                dv(v.tensor_tensor_scan(cum[:, :], one6[:, :], eqm[:, :], 0.0,
                                        AL.mult, AL.add))
                dv(v.scalar_tensor_tensor(dfv[:, :], cum[:, :], sc9[:, T0:T0 + 1],
                                          eqm[:, :], AL.add, AL.subtract))
                dv(v.tensor_scalar(dfv[:, :], dfv[:, :], sc9[:, NEQ:NEQ + 1],
                                   None, AL.is_lt))
                dv(v.tensor_tensor(sel[:, :], dfv[:, :], eqm[:, :], AL.mult))
                dv(v.tensor_tensor(sel[:, :], sel[:, :], gtm[:, :], AL.add))
                dv(v.tensor_tensor_scan(cum[:, :], one6[:, :], sel[:, :], 0.0,
                                        AL.mult, AL.add))
                dv(v.tensor_tensor(dfv[:, :], sel[:, :], cum[:, :], AL.mult))
                dv(v.tensor_scalar_add(dfv[:, :], dfv[:, :], -1.0))
                dv(v.tensor_copy(sl16[:, :], dfv[:, :]))
                dv(v.tensor_copy(glo[:, :], s16[:, :].bitcast(I16).rearrange(
                    "p (j two) -> p two j", two=2)[:, 0, :]))
                dv(v.tensor_copy(ghi[:, :], s16[:, :].bitcast(I16).rearrange(
                    "p (j two) -> p two j", two=2)[:, 1, :]))
                dv(v.tensor_copy(sc9[:, T1:T1 + 1], cum[:, L4 - 1:L4]))
                s.c_bis.inc(v.engine_nop())

                v.wait_ge(s.c_cmp.s, cmp0 + NIT + 4)
                dv(v.tensor_tensor_scan(rowb[0:1, 96:128], s.ones_sb[0:1, 0:32],
                                        rowb[0:1, 64:96], 0.0, AL.mult, AL.add))
                dv(v.tensor_tensor(rowb[0:1, 96:128], rowb[0:1, 96:128],
                                   rowb[0:1, 64:96], AL.subtract))
                dv(v.tensor_copy(offs_u[0:1, 0:32], rowb[0:1, 96:128]))
                s.c_bis.inc(v.engine_nop())

            @blk.tensor
            def _(t):
                t.wait_ge(s.c_k.s, s.c_k.n)
                for it in range(NIT + 1):
                    t.wait_ge(s.c_bis.s, bis0 + it + 1)
                    if it >= 1:
                        t.wait_ge(s.c_cmp.s, cmp0 + it)
                    s.c_gr.inc(t.matmul(s.p_a[0:32, 0:1], o66[:, 0:32],
                                        sc9[:, CNT:CNT + 1] if it < NIT
                                        else sc9[:, CGT:CGT + 1],
                                        start=True, stop=True))
                t.wait_ge(s.c_bis.s, bis0 + NIT + 2)
                t.wait_ge(s.c_cmp.s, cmp0 + NIT + 1)
                s.c_gr.inc(t.transpose(s.p_a[0:1, 2:34], sc9[:, EQC:EQC + 1],
                                       s.ident[0:32, 0:32]))
                t.wait_ge(s.c_bis.s, bis0 + NIT + 3)
                t.wait_ge(s.c_cmp.s, cmp0 + NIT + 2)
                s.c_gr.inc(t.transpose(s.p_a[0:32, 40:41], rowb[0:1, 32:64],
                                       s.ident[0:1, 0:1]))
                t.wait_ge(s.c_bis.s, bis0 + NIT + 4)
                t.wait_ge(s.c_cmp.s, cmp0 + NIT + 3)
                s.c_gr.inc(t.transpose(s.p_a[0:1, 48:80], sc9[:, T1:T1 + 1],
                                       s.ident[0:32, 0:32]))

            @blk.scalar
            def _(sc):
                for it in range(NIT + 1):
                    sc.wait_ge(s.c_gr.s, gr0 + it + 1)
                    s.c_cmp.inc(sc.activation(sc9[:, CNTT:CNTT + 1], s.p_a[0:32, 0:1],
                                              AF.Copy))
                sc.wait_ge(s.c_gr.s, gr0 + NIT + 2)
                s.c_cmp.inc(sc.activation(rowb[0:1, 0:32], s.p_a[0:1, 2:34], AF.Copy))
                sc.wait_ge(s.c_gr.s, gr0 + NIT + 3)
                s.c_cmp.inc(sc.activation(sc9[:, T0:T0 + 1], s.p_a[0:32, 40:41],
                                          AF.Copy))
                sc.wait_ge(s.c_gr.s, gr0 + NIT + 4)
                s.c_cmp.inc(sc.activation(rowb[0:1, 64:96], s.p_a[0:1, 48:80], AF.Copy))

            @blk.gpsimd
            def _(g):
                g.wait_ge(s.c_bis.s, bis0 + NIT + 5)
                s.c_gp.inc(g.iota(iot[:, :], [[1, L4]], base=0, channel_multiplier=L4,
                                  allow_small_or_imprecise_dtypes=True))
                g.wait_ge(s.c_gp.s, s.c_gp.n)
                s.c_gp.inc(g.local_scatter(icomp[:, :], iot[:, :], sl16[:, :],
                                           channels=32, num_elems=L4, num_idxs=L4))
                g.wait_ge(s.c_gp.s, s.c_gp.n)
                s.c_gp.inc(g.local_scatter(gcl[:, :], glo[:, :], sl16[:, :],
                                           channels=32, num_elems=L4, num_idxs=L4))
                g.wait_ge(s.c_gp.s, s.c_gp.n)
                s.c_gp.inc(g.local_scatter(gch[:, :], ghi[:, :], sl16[:, :],
                                           channels=32, num_elems=L4, num_idxs=L4))
                g.wait_ge(s.c_gp.s, s.c_gp.n)
                for r in range(32):
                    off = g.value_load(offs_u[0:1, r:r + 1], min_val=0, max_val=K)
                    s.c_mrg.inc(g.dma_start(out=s.idxp[0:1, bass.ds(off, L4)],
                                            in_=icomp[r:r + 1, :]), 16)
                    g.wait_ge(s.c_mrg.s, s.c_mrg.n)
                    off = g.value_load(offs_u[0:1, r:r + 1], min_val=0, max_val=K)
                    s.c_mrg.inc(g.dma_start(out=s.glo_dram[0:1, bass.ds(off, L4)],
                                            in_=gcl[r:r + 1, :]), 16)
                    g.wait_ge(s.c_mrg.s, s.c_mrg.n)
                    off = g.value_load(offs_u[0:1, r:r + 1], min_val=0, max_val=K)
                    s.c_mrg.inc(g.dma_start(out=s.ghi_dram[0:1, bass.ds(off, L4)],
                                            in_=gch[r:r + 1, :]), 16)
                    g.wait_ge(s.c_mrg.s, s.c_mrg.n)
                s.c_mrg.inc(g.dma_start(out=s.idx_list[0:1, :],
                                        in_=s.idxp[0:1, 0:K]), 16)
                g.wait_ge(s.c_mrg.s, s.c_mrg.n)
                s.c_ag.inc(g.collective_compute(
                    "AllGather", AL.bypass, replica_groups=[list(range(8))],
                    ins=[s.idx_list.ap().opt()], outs=[s.ag_idx.ap().opt()]))
        fctx.close()

    def emit_f3_expert(self):
        s, nc = self, self.nc
        K, DFF = self.K, self.DFF
        K16, KBK = K // 16, K // 128
        NFT, NHT = DFF // 128, DM // 128
        NKB, RS = self.NKB, self.RS

        fA = ExitStack()
        th = self.sbuf(fA, "th", [128, NHT, K], F16)
        tl = self.sbuf(fA, "tl", [128, NHT, K], F16)
        fB = ExitStack()
        toks = self.sbuf(fB, "toks", [128, KBK, DM])
        gidx = self.sbuf(fB, "gidx", [128, KBK], I16)
        gidx32 = self.sbuf(fB, "gidx32", [128, KBK], U32)

        F3P = self.cfg.get("F3P", 9)
        gth0, tp0, xch0, xcl0 = s.c_gth.n, s.c_tp.n, s.c_xch.n, s.c_xcl.n
        gat0 = s.c_gat.n
        if (blk := self.blk) or True:
            @blk.sync
            def _(sync):
                sync.wait_ge(s.c_ag.s, s.c_ag.n)
                s.c_gth.inc(sync.dma_start(
                    out=gidx[:, :],
                    in_=s.idx_list.ap().rearrange("o (kt p) -> (o p) kt", p=128),
                    allow_slow_non_contiguous=True), 16)

            @blk.vector
            def _(v):
                v.wait_ge(s.c_gth.s, gth0 + 16)
                s.c_gt1.inc(v.tensor_copy(gidx32[:, :], gidx[:, :]))

            if F3P < 1:
                fB.close()
                fA.close()
                return

            @blk.gpsimd
            def _(g):
                g.wait_ge(s.c_gt1.s, s.c_gt1.n)
                for kt in range(KBK):
                    s.c_gat.inc(g.indirect_dma_start(
                        out=toks[:, kt, :], out_offset=None,
                        in_=s.ag_x1[:, :],
                        in_offset=IndirectOffsetOnAxis(gidx32[:, kt:kt + 1], 0)), 16)

            if F3P < 2:
                fB.close()
                fA.close()
                return

            @blk.tensor
            def _(t):
                t.wait_ge(s.c_gat.s, s.c_gat.n)
                t.wait_ge(s.c_k.s, s.c_k.n)
                u = 0
                for kt in range(KBK):
                    for hg in range(2):
                        pb = s.p_g if u % 2 == 0 else s.p_h
                        if u >= 2:
                            t.wait_ge(s.c_xcl.s, xcl0 + (u - 2) + 1)
                        for q in range(4):
                            ht = hg * 4 + q
                            s.c_tp.inc(t.transpose(pb[:, ts(q, 128)],
                                                   toks[:, kt, ts(ht, 128)],
                                                   s.ident[:, :]))
                        u += 1

            @blk.scalar
            def _(sc):
                for u in range(KBK * 2):
                    kt, hg = u // 2, u % 2
                    pb = s.p_g if u % 2 == 0 else s.p_h
                    sc.wait_ge(s.c_tp.s, tp0 + 4 * (u + 1))
                    s.c_xch.inc(sc.activation(
                        th[:, hg * 4:hg * 4 + 4, kt * 128:(kt + 1) * 128],
                        pb[:, :], AF.Copy))

            @blk.vector
            def _(v):
                for u in range(KBK * 2):
                    kt, hg = u // 2, u % 2
                    pb = s.p_g if u % 2 == 0 else s.p_h
                    v.wait_ge(s.c_xch.s, xch0 + u + 1)
                    s.c_xcl.inc(v.tensor_tensor(
                        tl[:, hg * 4:hg * 4 + 4, kt * 128:(kt + 1) * 128],
                        pb[:, :], th[:, hg * 4:hg * 4 + 4, kt * 128:(kt + 1) * 128],
                        AL.subtract))

            if F3P < 3:
                fB.close()
                fA.close()
                return

            @blk.sync
            def _(sync):
                sync.wait_ge(s.c_xch.s, s.c_xch.n)
                sync.wait_ge(s.c_xcl.s, s.c_xcl.n)
                s.c_wlo1.inc(sync.dma_start(out=s.th_dram.ap(), in_=th.ap()), 16)
                s.c_wlo1.inc(sync.dma_start(out=s.tl_dram.ap(), in_=tl.ap()), 16)
        fB.close()
        fA.close()
        self.spill_mark = s.c_wlo1.n
        if F3P < 4:
            return

        fC = ExitStack()
        thk = self.sbuf(fC, "thk", [128, 2, NHT, RS], F16)
        tlk = self.sbuf(fC, "tlk", [128, 2, NHT, RS], F16)
        h1h = self.sbuf(fC, "h1h", [128, NFT, 512], F16)
        h1l = self.sbuf(fC, "h1l", [128, NFT, 512], F16)
        w1s = self.sbuf(fC, "w1s", [128, 2, NHT, 256], F16)
        w2s = self.sbuf(fC, "w2s", [128, 2, 2, 512], F16)
        h1f = self.sbuf(fC, "h1f", [128, 2, 512])
        eos = self.sbuf(fC, "eos", [128, 8, 512])
        self.zer512 = self.sbuf(fC, "zer512", [128, 512])
        grep = self.sbuf(fC, "grep", [128, NKB, 512])
        gsb = self.sbuf(fC, "gsb", [1, K])
        gsl = self.sbuf(fC, "gsl", [1, K], I16)
        gsh = self.sbuf(fC, "gsh", [1, K], I16)
        ctT = self.sbuf(fC, "ctT", [128, max(1, self.RS // 128), DM])

        gr0, fev0 = s.c_gr.n, s.c_fev.n
        if (blk := self.blk) or True:
            @blk.sync
            def _(sync):
                sync.wait_ge(s.c_wlo1.s, self.spill_mark)
                s.c_wlo0.inc(sync.dma_start(out=gsl[0:1, :], in_=s.glo_dram.ap()[0:1, 0:K]), 16)
                s.c_wlo0.inc(sync.dma_start(out=gsh[0:1, :], in_=s.ghi_dram.ap()[0:1, 0:K]), 16)

            @blk.vector
            def _(v):
                v.wait_ge(s.c_wlo0.s, s.c_wlo0.n)
                s.c_fts.inc(v.tensor_copy(gsb[0:1, :].bitcast(I16).rearrange(
                    "p (j two) -> p two j", two=2)[:, 0, :], gsl[0:1, :]))
                v.wait_ge(s.c_fts.s, s.c_fts.n)
                s.c_fts.inc(v.tensor_copy(gsb[0:1, :].bitcast(I16).rearrange(
                    "p (j two) -> p two j", two=2)[:, 1, :], gsh[0:1, :]))
                v.wait_ge(s.c_fts.s, s.c_fts.n)
                s.c_fts.inc(v.memset(s.zer512[:, :], 0.0))
        self.fts_mark = s.c_fts.n
        if F3P < 5:
            fC.close()
            return

        if (blk := self.blk) or True:
            @blk.tensor
            def _(t):
                t.wait_ge(s.c_fts.s, s.fts_mark)
                t.wait_ge(s.c_xcl.s, s.c_xcl.n)
                for kb in range(NKB):
                    if kb >= 1:
                        t.wait_ge(s.c_fev.s, fev0 + kb)
                    s.c_gr.inc(t.matmul(s.p_g[:, 0:RS], s.ones_sb[0:1, 0:128],
                                        gsb[0:1, ts(kb, RS)], start=True, stop=True))

            @blk.scalar
            def _(sc):
                for kb in range(NKB):
                    sc.wait_ge(s.c_gr.s, gr0 + kb + 1)
                    s.c_fev.inc(sc.activation(grep[:, kb, 0:RS], s.p_g[:, 0:RS], AF.Copy))

        self.h1m_marks = []
        nkb_run = NKB if F3P >= 6 else 1
        for kb in range(nkb_run):
            self._f3_kb(kb, thk, tlk, h1h, h1l, w1s, w2s, h1f, eos, grep, ctT)
        if F3P < 7:
            fC.close()
            return

        if (blk := self.blk) or True:
            @blk.gpsimd
            def _(g):
                g.wait_ge(s.c_sca.s, s.c_sca.n)
                s.c_ag.inc(g.collective_compute(
                    "AllGather", AL.bypass, replica_groups=[list(range(8))],
                    ins=[s.contrib.ap().opt()], outs=[s.ag_contrib.ap().opt()]))
        fC.close()

    def _f3_kb(self, kb, thk, tlk, h1h, h1l, w1s, w2s, h1f, eos, grep, ctT):
        s, nc = self, self.nc
        K, DFF = self.K, self.DFF
        NFT, NHT = DFF // 128, DM // 128
        RS = self.RS
        w10, w20 = [s.c_w1s0, s.c_w1s1], [s.c_w2s0, s.c_w2s1]
        h1m0, h1r0, h1c0, xcl0 = s.c_h1m.n, s.c_h1r.n, s.c_h1c.n, s.c_xcl.n
        eo0, fev0, ct0 = s.c_eo.n, s.c_fev.n, s.c_ct.n
        blk = self.blk
        w1vals = []
        tkvals = []

        @blk.sync
        def _(sync):
            ctk = s.c_xin0 if kb % 2 == 0 else s.c_xin1
            if kb == 0:
                sync.wait_ge(s.c_wlo1.s, s.spill_mark)
            if kb >= 2:
                sync.wait_ge(s.c_h1m.s, s.h1m_marks[kb - 2])
            ctk.inc(sync.dma_start(out=thk[:, kb % 2, :, :],
                                   in_=s.th_dram.ap()[:, :, ts(kb, RS)]), 16)
            ctk.inc(sync.dma_start(out=tlk[:, kb % 2, :, :],
                                   in_=s.tl_dram.ap()[:, :, ts(kb, RS)]), 16)
            tkvals.append((ctk, ctk.n))
            for ft in range(NFT):
                cw = w10[ft % 2]
                if ft >= 2:
                    sync.wait_ge(s.c_h1m.s, h1m0 + 24 * (ft - 1))
                cw.inc(sync.dma_start(out=w1s[:, :, :, ts(ft % 2, 128)],
                                      in_=s.w116[:, :, :, ts(ft, 128)]), 16)
                w1vals.append((cw, cw.n))

        @blk.tensor
        def _(t):
            t.wait_ge(tkvals[0][0].s, tkvals[0][1])
            for ft in range(NFT):
                pb = s.p_a if ft % 2 == 0 else s.p_b
                t.wait_ge(w1vals[ft][0].s, w1vals[ft][1])
                if ft >= 2:
                    t.wait_ge(s.c_h1r.s, h1r0 + (ft - 2) + 1)
                for ht in range(NHT):
                    wh = w1s[:, 0, ht, ts(ft % 2, 128)]
                    wl = w1s[:, 1, ht, ts(ft % 2, 128)]
                    s.c_h1m.inc(t.matmul(pb[:, 0:RS], wh, thk[:, kb % 2, ht, 0:RS],
                                         start=(ht == 0), stop=False))
                    s.c_h1m.inc(t.matmul(pb[:, 0:RS], wh, tlk[:, kb % 2, ht, 0:RS],
                                         start=False, stop=False))
                    s.c_h1m.inc(t.matmul(pb[:, 0:RS], wl, thk[:, kb % 2, ht, 0:RS],
                                         start=False, stop=(ht == NHT - 1)))
        s.h1m_marks.append(s.c_h1m.n)

        @blk.vector
        def _(v):
            v.wait_ge(s.c_fts.s, s.fts_mark)
            if kb >= 1:
                v.wait_ge(s.c_eo.s, eo0)
            for ft in range(NFT):
                pb = s.p_a if ft % 2 == 0 else s.p_b
                v.wait_ge(s.c_h1m.s, h1m0 + 24 * (ft + 1))
                if ft >= 2:
                    v.wait_ge(s.c_h1c.s, h1c0 + (ft - 2) + 1)
                    v.wait_ge(s.c_xcl.s, xcl0 + (ft - 2) + 1)
                s.c_h1r.inc(v.scalar_tensor_tensor(h1f[:, ft % 2, 0:RS], pb[:, 0:RS],
                                                   s.b1_sb[:, ft:ft + 1],
                                                   s.zer512[:, 0:RS], AL.add, AL.max))
                if ft >= 1:
                    v.wait_ge(s.c_h1c.s, h1c0 + ft)
                    s.c_xcl.inc(v.tensor_tensor(h1l[:, ft - 1, 0:RS],
                                                h1f[:, (ft - 1) % 2, 0:RS],
                                                h1h[:, ft - 1, 0:RS], AL.subtract))
            v.wait_ge(s.c_h1c.s, h1c0 + NFT)
            s.c_xcl.inc(v.tensor_tensor(h1l[:, NFT - 1, 0:RS],
                                        h1f[:, (NFT - 1) % 2, 0:RS],
                                        h1h[:, NFT - 1, 0:RS], AL.subtract))

        @blk.scalar
        def _(sc):
            for ft in range(NFT):
                sc.wait_ge(s.c_h1r.s, h1r0 + ft + 1)
                if kb >= 1:
                    sc.wait_ge(s.c_eo.s, eo0)
                s.c_h1c.inc(sc.activation(h1h[:, ft, 0:RS], h1f[:, ft % 2, 0:RS],
                                          AF.Copy))

        w2vals = []

        @blk.sync
        def _(sync):
            for mg in range(2):
                for ft in range(NFT):
                    u = mg * NFT + ft
                    cw = w20[u % 2]
                    if u >= 2:
                        sync.wait_ge(s.c_eo.s, eo0 + 12 * (u - 1))
                    cw.inc(sync.dma_start(out=w2s[:, :, u % 2, :],
                                          in_=s.w216[:, :, ft, ts(mg, 512)]), 16)
                    w2vals.append((cw, cw.n))

        @blk.tensor
        def _(t):
            banks = [s.p_c, s.p_d, s.p_e, s.p_f]
            for mg in range(2):
                for ft in range(NFT):
                    u = mg * NFT + ft
                    t.wait_ge(w2vals[u][0].s, w2vals[u][1])
                    t.wait_ge(s.c_h1c.s, h1c0 + ft + 1)
                    t.wait_ge(s.c_xcl.s, xcl0 + ft + 1)
                    if ft == 0:
                        t.wait_ge(s.c_fev.s, fev0 + 4 if mg > 0 else fev0)
                    for m in range(4):
                        wh = w2s[:, 0, u % 2, ts(m, 128)]
                        wl = w2s[:, 1, u % 2, ts(m, 128)]
                        s.c_eo.inc(t.matmul(banks[m][:, 0:RS], wh, h1h[:, ft, 0:RS],
                                            start=(ft == 0), stop=False))
                        s.c_eo.inc(t.matmul(banks[m][:, 0:RS], wh, h1l[:, ft, 0:RS],
                                            start=False, stop=False))
                        s.c_eo.inc(t.matmul(banks[m][:, 0:RS], wl, h1h[:, ft, 0:RS],
                                            start=False, stop=(ft == NFT - 1)))

        sct_pre = s.c_sct.n
        sca_pre = s.c_sca.n

        @blk.scalar
        def _(sc):
            banks = [s.p_c, s.p_d, s.p_e, s.p_f]
            for mg in range(2):
                sc.wait_ge(s.c_eo.s, eo0 + 12 * NFT * (mg + 1))
                sc.wait_ge(s.c_ct.s, ct0)
                if kb >= 1:
                    sc.wait_ge(s.c_sct.s, sct_pre)
                for m in range(4):
                    mc = mg * 4 + m
                    s.c_fev.inc(sc.activation(eos[:, mc, 0:RS], banks[m][:, 0:RS],
                                              AF.Identity, bias=s.b2_sb[:, mc:mc + 1]))

        gmul0 = s.c_ct.n

        @blk.vector
        def _(v):
            for mg in range(2):
                for m in range(4):
                    v.wait_ge(s.c_fev.s, fev0 + 4 * mg + m + 1)
                    s.c_ct.inc(v.tensor_tensor(eos[:, mg * 4 + m, 0:RS],
                                               eos[:, mg * 4 + m, 0:RS],
                                               grep[:, kb, 0:RS], AL.mult))

        tpx0 = s.c_tp.n
        ctv1 = s.c_sct.n

        @blk.tensor
        def _(t):
            for mg in range(2):
                for ktl in range(RS // 128):
                    u = mg * (RS // 128) + ktl
                    pb = s.p_g if u % 2 == 0 else s.p_h
                    t.wait_ge(s.c_ct.s, gmul0 + 4 * (mg + 1))
                    if u >= 2:
                        t.wait_ge(s.c_sct.s, ctv1 + (u - 2) + 1)
                    else:
                        t.wait_ge(s.c_sct.s, max(0, ctv1 - (2 - u)))
                    for m in range(4):
                        s.c_tp.inc(t.transpose(pb[:, ts(m, 128)],
                                               eos[:, mg * 4 + m, ts(ktl, 128)],
                                               s.ident[:, :]))

        @blk.scalar
        def _(sc):
            for mg in range(2):
                for ktl in range(RS // 128):
                    u = mg * (RS // 128) + ktl
                    pb = s.p_g if u % 2 == 0 else s.p_h
                    sc.wait_ge(s.c_tp.s, tpx0 + 4 * (u + 1))
                    if kb >= 1:
                        sc.wait_ge(s.c_sca.s, sca_pre)
                    s.c_sct.inc(sc.activation(ctT[:, ktl, ts(mg, 512)],
                                              pb[:, :], AF.Copy))

        @blk.sync
        def _(sync):
            sync.wait_ge(s.c_sct.s, s.c_sct.n)
            s.c_sca.inc(sync.dma_start(
                out=s.contrib.ap()[kb * RS:(kb + 1) * RS, :].rearrange(
                    "(blk p) h -> p blk h", p=128),
                in_=ctT[:, 0:RS // 128, :]), 16)

    def emit_f4_scatter(self):
        s, nc = self, self.nc
        K, TOK, L = self.K, self.TOK, self.L
        KBK = K // 128
        fctx = ExitStack()
        sidx = self.sbuf(fctx, "sidx", [128, 2, KBK], I16)
        sidx32 = self.sbuf(fctx, "sidx32", [128, 2, KBK], U32)
        crows = self.sbuf(fctx, "crows", [128, 2, DM])
        gth0, pul0, gt20, gat0 = s.c_gth.n, s.c_pul.n, s.c_gt2.n, s.c_gat.n
        ldvals = []
        sxvals = []
        KBKt = KBK * 8
        scvals = [(s.c_pul if rg % 2 == 0 else s.c_gat,
                   (pul0 if rg % 2 == 0 else gat0) + 16 * (rg // 2 + 1))
                  for rg in range(KBKt)]
        if (blk := self.blk) or True:
            @blk.sync
            def _(sync):
                sync.wait_ge(s.c_ag.s, s.c_ag.n)
                for e in range(8):
                    cs = s.c_xin0 if e % 2 == 0 else s.c_xin1
                    if e >= 2:
                        sync.wait_ge(s.c_gt2.s, gt20 + (e - 2) + 1)
                    cs.inc(sync.dma_start(
                        out=sidx[:, e % 2, :],
                        in_=s.ag_idx.ap()[e:e + 1, :].rearrange(
                            "o (kt p) -> (o p) kt", p=128),
                        allow_slow_non_contiguous=True), 16)
                    sxvals.append((cs, cs.n))
                    for kt in range(KBK):
                        rg = e * KBK + kt
                        cl = s.c_bv0 if rg % 2 == 0 else s.c_bv1
                        if rg >= 2:
                            sync.wait_ge(scvals[rg - 2][0].s, scvals[rg - 2][1])
                        r0 = e * K + kt * 128
                        cl.inc(sync.dma_start(out=crows[:, rg % 2, :],
                                              in_=s.ag_contrib.ap()[r0:r0 + 128, :]), 16)
                        ldvals.append((cl, cl.n))

            @blk.vector
            def _(v):
                for e in range(8):
                    v.wait_ge(sxvals[e][0].s, sxvals[e][1])
                    if e >= 2:
                        ee = (e - 1) * KBK - 1
                        v.wait_ge(scvals[ee][0].s, scvals[ee][1])
                        if ee >= 1:
                            v.wait_ge(scvals[ee - 1][0].s, scvals[ee - 1][1])
                    s.c_gt2.inc(v.tensor_copy(sidx32[:, e % 2, :], sidx[:, e % 2, :]))

            @blk.gpsimd
            def _(g):
                g.wait_ge(s.c_out.s, s.c_out.n)
                yb = g.value_load(s.scrow_sb[0:1, 32:33], min_val=0, max_val=TOK - L)
                sdm_seed = s.c_sdm.n + 16
                s.c_sdm.inc(g.dma_start(out=s.yfull.ap()[bass.ds(yb, L), :],
                                        in_=s.x1_out.ap()), 16)
                g.wait_ge(s.c_sdm.s, sdm_seed)
                for e in range(8):
                    g.wait_ge(s.c_gt2.s, gt20 + e + 1)
                    if e > 0:
                        rr = e * KBK - 1
                        g.wait_ge(scvals[rr][0].s, scvals[rr][1])
                        if rr >= 1:
                            g.wait_ge(scvals[rr - 1][0].s, scvals[rr - 1][1])
                    for kt in range(KBK):
                        rg = e * KBK + kt
                        cp_ = s.c_pul if rg % 2 == 0 else s.c_gat
                        g.wait_ge(ldvals[rg][0].s, ldvals[rg][1])
                        cp_.inc(g.indirect_dma_start(
                            out=s.yfull[:, :],
                            out_offset=IndirectOffsetOnAxis(
                                sidx32[:, e % 2, kt:kt + 1], 0),
                            in_=crows[:, rg % 2, :], in_offset=None,
                            compute_op=AL.add), 16)
        fctx.close()

    def emit_f5_ln2(self):
        s, nc = self, self.nc
        L = self.L
        fctx = ExitStack()
        ybuf = self.sbuf(fctx, "ybuf", [128, 2, DM])
        lnq2 = self.sbuf(fctx, "lnq2", [128, DM])
        lnc2 = self.sbuf(fctx, "lnc2", [128, 2, 8])
        NT = L // 128
        ln10, ln20, ln30, sfx0, out0 = (s.c_ln1.n, s.c_ln2.n, s.c_ln3.n,
                                        s.c_sfx.n, s.c_out.n)
        ybvals = []
        if (blk := self.blk) or True:
            @blk.gpsimd
            def _(g):
                g.wait_ge(s.c_pul.s, s.c_pul.n)
                g.wait_ge(s.c_gat.s, s.c_gat.n)
                for j in range(NT):
                    cy = s.c_sd0 if j % 2 == 0 else s.c_sd1
                    if j >= 2:
                        g.wait_ge(s.c_out.s, out0 + 16 * (j - 1))
                    yb = g.value_load(s.scrow_sb[0:1, 33 + j:34 + j], min_val=0,
                                      max_val=self.TOK - 128)
                    cy.inc(g.dma_start(out=ybuf[:, j % 2, :],
                                       in_=s.yfull.ap()[bass.ds(yb, 128), :]), 16)
                    ybvals.append((cy, cy.n))

            @blk.vector
            def _(v):
                def dv(ii):
                    s.c_dv.inc(ii)
                    v.wait_ge(s.c_dv.s, s.c_dv.n)

                def stats(j):
                    sl = j % 2
                    v.wait_ge(ybvals[j][0].s, ybvals[j][1])
                    dv(v.tensor_tensor(lnq2[:, :], ybuf[:, sl, :], ybuf[:, sl, :], AL.mult))
                    dv(v.reduce_sum(lnc2[:, sl, 1:2], lnq2[:, :], axis=mybir.AxisListType.X))
                    dv(v.reduce_sum(lnc2[:, sl, 0:1], ybuf[:, sl, :], axis=mybir.AxisListType.X))
                    dv(v.tensor_scalar_mul(lnc2[:, sl, 2:3], lnc2[:, sl, 0:1], 1.0 / DM))
                    dv(v.tensor_scalar_mul(lnc2[:, sl, 3:4], lnc2[:, sl, 1:2], 1.0 / DM))
                    dv(v.tensor_tensor(lnc2[:, sl, 4:5], lnc2[:, sl, 2:3], lnc2[:, sl, 2:3], AL.mult))
                    dv(v.tensor_tensor(lnc2[:, sl, 4:5], lnc2[:, sl, 3:4], lnc2[:, sl, 4:5], AL.subtract))
                    s.c_ln1.inc(v.tensor_scalar_add(lnc2[:, sl, 4:5], lnc2[:, sl, 4:5], 1e-5))

                def rmul(j):
                    sl = j % 2
                    v.wait_ge(s.c_ln2.s, ln20 + j + 1)
                    dv(v.tensor_tensor(lnc2[:, sl, 7:8], lnc2[:, sl, 2:3],
                                       lnc2[:, sl, 6:7], AL.mult))
                    s.c_ln3.inc(v.tensor_scalar_mul(lnc2[:, sl, 7:8], lnc2[:, sl, 7:8], -1.0))

                for j in range(NT):
                    stats(j)
                    if j >= 1:
                        rmul(j - 1)
                rmul(NT - 1)

            @blk.scalar
            def _(sc):
                def lnexp(j):
                    sl = j % 2
                    sc.wait_ge(s.c_ln1.s, ln10 + j + 1)
                    s.c_sc.inc(sc.activation(lnc2[:, sl, 5:6], lnc2[:, sl, 4:5], AF.Ln))
                    sc.wait_ge(s.c_sc.s, s.c_sc.n)
                    s.c_ln2.inc(sc.activation(lnc2[:, sl, 6:7], lnc2[:, sl, 5:6],
                                              AF.Exp, scale=-0.5))

                def apply(j):
                    sl = j % 2
                    sc.wait_ge(s.c_ln3.s, ln30 + j + 1)
                    s.c_sfx.inc(sc.activation(ybuf[:, sl, :], ybuf[:, sl, :],
                                              AF.Identity, scale=lnc2[:, sl, 6:7],
                                              bias=lnc2[:, sl, 7:8]))

                for j in range(NT):
                    lnexp(j)
                    if j >= 1:
                        apply(j - 1)
                apply(NT - 1)

            @blk.sync
            def _(sync):
                for j in range(NT):
                    sync.wait_ge(s.c_sfx.s, sfx0 + j + 1)
                    s.c_out.inc(sync.dma_start(out=s.out_f[ts(j, 128), :],
                                               in_=ybuf[:, j % 2, :]), 16)

            @blk.gpsimd
            def _(g):
                g.wait_ge(s.c_out.s, s.c_out.n)
        fctx.close()

    def build_full(self):
        self.declare_io()
        self.alloc_common()
        self.open_phase("init")
        self.emit_init()
        self.emit_ffn_split()
        self.emit_ffn_bvec()
        self.close_phase()
        self.open_phase("m1")
        self.emit_m1()
        self.close_phase()
        self.open_phase("m2")
        self.emit_m2()
        self.close_phase()
        self.open_phase("m3")
        self.emit_m3()
        self.close_phase()
        fup = self.cfg.get("FUP", 5)
        if fup >= 1:
            self.open_phase("f1")
            self.emit_f1_ag()
            self.close_phase()
        if fup >= 2:
            self.open_phase("f2")
            self.emit_f2_routing()
            self.close_phase()
        if fup >= 3:
            self.open_phase("f3")
            self.emit_f3_expert()
            self.close_phase()
        if fup >= 4:
            self.open_phase("f4")
            self.emit_f4_scatter()
            self.close_phase()
        if fup >= 5:
            self.open_phase("f5")
            self.emit_f5_ln2()
            self.close_phase()
        if fup < 5:
            self.open_phase("fin")
            self.finalize_mamba_only()
            self.close_phase()
        self.ctx.close()
        self.compile()
        return self.nc

    def finalize_mamba_only(self):
        s, nc = self, self.nc
        if (blk := self.blk) or True:
            @blk.gpsimd
            def _(g):
                g.wait_ge(s.c_out.s, s.c_out.n)

    def build_mamba(self):
        ph = self.cfg.get("PH", 3)
        self.declare_io()
        self.alloc_common()
        self.open_phase("init")
        self.emit_init()
        self.close_phase()
        if ph >= 1:
            self.open_phase("m1")
            self.emit_m1()
            self.close_phase()
        if ph >= 2:
            self.open_phase("m2")
            self.emit_m2()
            self.close_phase()
        if ph >= 3:
            self.open_phase("m3")
            self.emit_m3()
            self.close_phase()
        self.open_phase("fin")
        self.finalize_mamba_only()
        self.close_phase()
        self.ctx.close()
        self.compile()
        return self.nc

    def compile(self):
        import bass_rust as _br
        orig = _br.insert_act_table_loads
        _br.insert_act_table_loads = lambda x: None
        try:
            self.nc.compile()
        finally:
            _br.insert_act_table_loads = orig


import numpy as np
from concourse.bass_utils import run_bass_kernel_spmd

_MK_CACHE = {}


def _build():
    key = "full"
    if key not in _MK_CACHE:
        m = MK(dict(L=4096, HALO=128, T=512, FFN=True, DFF=4096))
        nc = m.build_full()
        _MK_CACHE[key] = (m, nc)
    return _MK_CACHE[key]


def kernel(L0_attn_x, L0_attn_in_proj_w, L0_attn_conv_w, L0_attn_x_proj_w,
           L0_attn_dt_w, L0_attn_dt_b, L0_attn_A_param, L0_attn_D_param,
           L0_attn_out_w, L0_attn_out_b, L0_ffn_router_w, L0_ffn_router_b,
           L0_ffn_w1, L0_ffn_b1, L0_ffn_w2, L0_ffn_b2):
    x = np.asarray(L0_attn_x)
    ipw = np.ascontiguousarray(np.asarray(L0_attn_in_proj_w))
    wu, wz = ipw[:DI], ipw[DI:]
    convw = np.ascontiguousarray(np.asarray(L0_attn_conv_w).reshape(DI, 4))
    m, nc = _build()
    Lc, HALO = m.L, m.HALO
    wmap = {
        "wu_in": np.ascontiguousarray(wu),
        "wz_in": np.ascontiguousarray(wz),
        "convw_in": convw,
        "xpw_in": np.ascontiguousarray(np.asarray(L0_attn_x_proj_w)),
        "dtw_in": np.ascontiguousarray(np.asarray(L0_attn_dt_w)),
        "dtb_in": np.ascontiguousarray(np.asarray(L0_attn_dt_b).reshape(1, DI)),
        "ap_in": np.ascontiguousarray(np.asarray(L0_attn_A_param)),
        "dp_in": np.ascontiguousarray(np.asarray(L0_attn_D_param).reshape(1, DI)),
        "ow_in": np.ascontiguousarray(np.asarray(L0_attn_out_w)),
        "ob_in": np.ascontiguousarray(np.asarray(L0_attn_out_b).reshape(1, DM)),
        "rw_in": np.ascontiguousarray(np.asarray(L0_ffn_router_w)),
        "rb_in": np.ascontiguousarray(np.asarray(L0_ffn_router_b).reshape(1, 8)),
    }
    w1 = np.asarray(L0_ffn_w1)
    b1 = np.asarray(L0_ffn_b1)
    w2 = np.asarray(L0_ffn_w2)
    b2 = np.asarray(L0_ffn_b2)
    in_maps = []
    for c in range(8):
        b, q = c // 4, c % 4
        xs = np.zeros((HALO + Lc, DM), np.float32)
        xs[HALO:] = x[b, q * Lc:(q + 1) * Lc]
        if q > 0:
            xs[:HALO] = x[b, q * Lc - HALO:q * Lc]
        scrow = np.zeros((1, 80), np.uint32)
        scrow[0, :32] = np.array([8 * (r // 4) + c for r in range(32)], np.uint32)
        scrow[0, 32] = c * Lc
        for j in range(Lc // 128):
            scrow[0, 33 + j] = c * Lc + j * 128
        in_maps.append({"xs": xs, **wmap, "scrow_in": scrow,
                        "w1_in": np.ascontiguousarray(w1[c]),
                        "b1_in": np.ascontiguousarray(b1[c].reshape(1, m.DFF)),
                        "w2_in": np.ascontiguousarray(w2[c]),
                        "b2_in": np.ascontiguousarray(b2[c].reshape(1, DM))})
    r = run_bass_kernel_spmd(nc, in_maps, core_ids=list(range(8)))
    out = np.empty((2, 16384, DM), np.float32)
    for c in range(8):
        b, q = c // 4, c % 4
        out[b, q * Lc:(q + 1) * Lc] = r.results[c]["out"]
    return out


# version 06 (final)
# speedup vs baseline: 1.0100x; 1.0100x over previous
import numpy as np
import concourse.bacc as bacc
import concourse.bass as bass
import concourse.mybir as mybir
from concourse.bass_core_rust import IndirectOffsetOnAxis
from contextlib import ExitStack

F32 = mybir.dt.float32
F16 = mybir.dt.float16
U32 = mybir.dt.uint32
I16 = mybir.dt.int16
AF = mybir.ActivationFunctionType
AL = mybir.AluOpType

DM = 1024
DI = 2048
NS = 16
NB = DI // 128
NI = NS * NB


def ts(i, n):
    return slice(i * n, (i + 1) * n)


class Ctr:
    def __init__(self, nc, ctx, name):
        self.s = ctx.enter_context(nc.semaphore(name))
        self.n = 0

    def inc(self, instr, amt=1):
        self.n += amt
        instr.then_inc(self.s, amt)
        return instr


class MK:
    def __init__(self, cfg):
        self.cfg = cfg
        self.L = cfg["L"]
        self.HALO = cfg["HALO"]
        self.T = cfg["T"]
        self.NCH = self.L // self.T
        assert self.L % self.T == 0
        self.nc = bacc.Bacc("TRN2", target_bir_lowering=False)
        self.ctx = ExitStack()

    def sbuf(self, ctx, n_, s_, d_=F32):
        return ctx.enter_context(self.nc.sbuf_tensor(n_, s_, d_))

    def C(self, name):
        return Ctr(self.nc, self.ctx, name)

    def declare_io(self):
        nc, L, HALO = self.nc, self.L, self.HALO
        self.FFN = self.cfg.get("FFN", False)
        self.DFF = self.cfg.get("DFF", 4096)
        self.TOK = 8 * L
        self.K = self.TOK // 16
        self.L2 = self.TOK // 16
        self.RS = min(512, self.K)
        self.NKB = self.K // self.RS
        dt = nc.dram_tensor
        self.xs = dt("xs", [HALO + L, DM], F32, kind="ExternalInput")
        self.wu_in = dt("wu_in", [DI, DM], F32, kind="ExternalInput")
        self.wz_in = dt("wz_in", [DI, DM], F32, kind="ExternalInput")
        self.convw_in = dt("convw_in", [DI, 4], F32, kind="ExternalInput")
        self.xpw_in = dt("xpw_in", [17, DI], F32, kind="ExternalInput")
        self.dtw_in = dt("dtw_in", [DI, 1], F32, kind="ExternalInput")
        self.dtb_in = dt("dtb_in", [1, DI], F32, kind="ExternalInput")
        self.ap_in = dt("ap_in", [DI, NS], F32, kind="ExternalInput")
        self.dp_in = dt("dp_in", [1, DI], F32, kind="ExternalInput")
        self.ow_in = dt("ow_in", [DM, DI], F32, kind="ExternalInput")
        self.ob_in = dt("ob_in", [1, DM], F32, kind="ExternalInput")
        self.rw_in = dt("rw_in", [DM, 8], F32, kind="ExternalInput")
        self.rb_in = dt("rb_in", [1, 8], F32, kind="ExternalInput")

        if self.FFN:
            DFF, TOK, K, L2 = self.DFF, self.TOK, self.K, self.L2
            self.scrow_in = dt("scrow_in", [1, 80], U32, kind="ExternalInput")
            self.w1_in = dt("w1_in", [DM, DFF], F32, kind="ExternalInput")
            self.b1_in = dt("b1_in", [1, DFF], F32, kind="ExternalInput")
            self.w2_in = dt("w2_in", [DFF, DM], F32, kind="ExternalInput")
            self.b2_in = dt("b2_in", [1, DM], F32, kind="ExternalInput")
            self.out_f = dt("out", [L, DM], F32, kind="ExternalOutput")
            self.x1_out = dt("x1_dram", [L, DM], F32)
            self.scT_dram = dt("scT_dram", [8, L], F32)
            self.ag_x1 = dt("ag_x1", [TOK, DM], F32)
            self.ag_scT = dt("ag_scT", [64, L], F32)
            self.w116 = dt("w116", [128, 2, DM // 128, DFF], F16)
            self.w216 = dt("w216", [128, 2, DFF // 128, DM], F16)
            self.idxp = dt("idxp", [1, K + L2], I16)
            self.idx_list = dt("idx_list", [1, K], I16)
            self.glo_dram = dt("glo_dram", [1, K + L2], I16)
            self.ghi_dram = dt("ghi_dram", [1, K + L2], I16)
            self.ag_idx = dt("ag_idx", [8, K], I16)
            self.contrib = dt("contrib", [K, DM], F32)
            self.th_dram = dt("th_dram", [128, DM // 128, K], F16)
            self.tl_dram = dt("tl_dram", [128, DM // 128, K], F16)
            self.ag_contrib = dt("ag_contrib", [8 * K, DM], F32)
            self.yfull = dt("yfull", [TOK, DM], F32)
        else:
            self.x1_out = dt("x1", [L, DM], F32, kind="ExternalOutput")
            self.sc_out = dt("sc", [L, 8], F32, kind="ExternalOutput")

        self.wu16 = dt("wu16", [128, 2, 8, DI], F16)
        self.wz16 = dt("wz16", [128, 2, 8, DI], F16)
        self.ow16 = dt("ow16", [128, 2, 16, DM], F16)
        self.ub_dram = dt("ub_dram", [128, NB, L], F32)
        self.y_dram = dt("y_dram", [128, NB, L], F32)
        self.y3_dram = dt("y3_dram", [128, 2, NB, L], F16)

    def alloc_common(self):
        cx = self.ctx
        sb = self.sbuf
        self.xp_sb = sb(cx, "xp_sb", [128, 16, 17])
        self.convw_sb = sb(cx, "convw_sb", [128, NB, 4])
        self.dtw_sb = sb(cx, "dtw_sb", [128, NB])
        self.dtb_sb = sb(cx, "dtb_sb", [128, NB])
        self.dpar_sb = sb(cx, "dpar_sb", [128, NB])
        self.a_sb = sb(cx, "a_sb", [128, NB * NS])
        self.ob_sb = sb(cx, "ob_sb", [128, 8])
        self.rw_sb = sb(cx, "rw_sb", [128, 8, 8])
        self.rb_sb = sb(cx, "rb_sb", [128, 8])
        self.ones_sb = sb(cx, "ones_sb", [1, 512])
        self.ident = sb(cx, "ident", [128, 128])
        if self.FFN:
            self.scrow_sb = sb(cx, "scrow_sb", [1, 80], U32)
            self.b1_sb = sb(cx, "b1_sb", [128, self.DFF // 128])
            self.b2_sb = sb(cx, "b2_sb", [128, DM // 128])

        psu = lambda n_, s_: cx.enter_context(self.nc.psum_tensor(n_, s_, F32))
        self.p_a = psu("p_a", [128, 512])
        self.p_b = psu("p_b", [128, 512])
        self.p_c = psu("p_c", [128, 512])
        self.p_d = psu("p_d", [128, 512])
        self.p_e = psu("p_e", [128, 512])
        self.p_f = psu("p_f", [128, 512])
        self.p_g = psu("p_g", [128, 512])
        self.p_h = psu("p_h", [128, 512])

        for nm in ["c_ld", "c_k", "c_w16", "c_xin", "c_tp", "c_xch", "c_xcl",
                   "c_mm", "c_ev", "c_hc", "c_cv", "c_sil", "c_xd", "c_bev",
                   "c_dt", "c_dl", "c_brm", "c_bre", "c_sa", "c_su", "c_sh",
                   "c_sp", "c_ya", "c_spl", "c_wld", "c_zev", "c_gt1", "c_gt2",
                   "c_y3c", "c_y3l", "c_y3s", "c_oev", "c_otp", "c_ocp",
                   "c_ln1", "c_ln2", "c_ln3", "c_rt1", "c_rt2", "c_rt3",
                   "c_sfx", "c_out", "c_wlo0", "c_wlo1", "c_xin0", "c_xin1", "c_bv0", "c_bv1", "c_y3s0", "c_y3s1", "c_gp", "c_sc", "c_dv", "c_xr", "c_lgr",
                   "c_sct", "c_ag", "c_bis", "c_cmp", "c_mrg", "c_gth", "c_h1m",
                   "c_h1r", "c_h1c", "c_eo", "c_ct", "c_sca", "c_pul", "c_w1s0",
                   "c_w1s1", "c_w2s0", "c_w2s1", "c_fts", "c_fev", "c_gr", "c_gat", "c_sdm", "c_sd0", "c_sd1"]:
            setattr(self, nm, self.C(nm))

    def emit_init(self):
        s, nc = self, self.nc
        IB = self.cfg.get("IB", 99)
        ictx = ExitStack()
        wstg = self.sbuf(ictx, "wstg", [128, 2048])
        vstg = self.sbuf(ictx, "vstg", [17, 2048])
        fsh = self.sbuf(ictx, "fsh", [128, 512], F16)
        fsl = self.sbuf(ictx, "fsl", [128, 512], F16)

        if (blk := self.blk) or True:
            @blk.sync
            def _(sync):
                s.c_ld.inc(sync.dma_start(out=s.convw_sb[:, :, :], in_=s.convw_in.ap().rearrange("(b p) t -> p b t", p=128)), 16)
                s.c_ld.inc(sync.dma_start(out=s.a_sb.ap().rearrange("p (b n) -> p b n", n=NS), in_=s.ap_in.ap().rearrange("(b p) n -> p b n", p=128)), 16)
                s.c_ld.inc(sync.dma_start(out=s.rw_sb[:, :, :], in_=s.rw_in.ap().rearrange("(k p) e -> p k e", p=128)), 16)
                s.c_ld.inc(sync.dma_start(out=s.rb_sb[:, :], in_=s.rb_in.ap().broadcast_to([128, 8])), 16)
                s.c_ld.inc(sync.dma_start(out=vstg[0:17, :], in_=s.xpw_in.ap()), 16)
                if s.FFN:
                    s.c_ld.inc(sync.dma_start(out=s.scrow_sb[:, :], in_=s.scrow_in.ap()), 16)

            if IB < 2:
                return
            @blk.vector
            def _(v):
                v.wait_ge(s.c_ld.s, s.c_ld.n)
                s.c_k.inc(v.memset(s.ones_sb[:, :], 1.0))
                s.c_k.inc(v.memset(s.ident[:, :], 1.0))

            if IB < 3:
                return
            @blk.scalar
            def _(sc):
                from concourse.hw_specs import get_activation_tables
                tabs = list(get_activation_tables(nc.m.arch))
                set_id = tabs.index("natural_log_exp_and_others")
                sc.add_instruction(mybir.InstLoadActFuncSet(
                    name=nc.get_next_instruction_name(), act_func_set_id=set_id,
                    ins=[], outs=[]))
                sc.wait_ge(s.c_ld.s, s.c_ld.n)
                s.c_k.inc(sc.activation(s.a_sb[:, :], s.a_sb[:, :], AF.Exp))

            if IB < 4:
                return
            @blk.gpsimd
            def _(g):
                g.wait_ge(s.c_k.s, s.c_k.n)
                s.c_k.inc(g.affine_select(s.ident[:, :], s.ident[:, :], [[1, 128]],
                                          AL.is_equal, 0.0, base=0, channel_multiplier=-1))

            if IB < 5:
                return
            @blk.vector
            def _(v):
                v.wait_ge(s.c_k.s, s.c_k.n)
                s.c_k.inc(v.tensor_scalar_mul(s.a_sb[:, :], s.a_sb[:, :], -1.0))

        tp0x, ev0x = s.c_tp.n, s.c_ev.n
        if IB < 6:
            ictx.close()
            return
        if (blk := self.blk) or True:
            @blk.tensor
            def _(t):
                t.wait_ge(s.c_k.s, s.c_k.n)
                for r in range(4):
                    pb = s.p_c if r % 2 == 0 else s.p_g
                    if r >= 2:
                        t.wait_ge(s.c_ev.s, ev0x + 4 * (r - 1))
                    for q in range(4):
                        a = r * 4 + q
                        s.c_tp.inc(t.transpose(pb[:, q * 128: q * 128 + 17],
                                               vstg[0:17, ts(a, 128)], s.ident[0:17, 0:17]))

            @blk.scalar
            def _(sc):
                for r in range(4):
                    pb = s.p_c if r % 2 == 0 else s.p_g
                    sc.wait_ge(s.c_tp.s, tp0x + 4 * (r + 1))
                    for q in range(4):
                        a = r * 4 + q
                        s.c_ev.inc(sc.activation(s.xp_sb[:, a, :], pb[:, q * 128: q * 128 + 17], AF.Copy))

        vecjobs = [(s.dtw_in.ap().rearrange("(a p) o -> a (p o)", p=128), 16, s.dtw_sb),
                   (s.dtb_in.ap().rearrange("o (a p) -> (o a) p", p=128), 16, s.dtb_sb),
                   (s.dp_in.ap().rearrange("o (a p) -> (o a) p", p=128), 16, s.dpar_sb),
                   (s.ob_in.ap().rearrange("o (a p) -> (o a) p", p=128), 8, s.ob_sb)]
        ofs = [0, 128, 256, 384]
        if IB < 7:
            ictx.close()
            return
        if (blk := self.blk) or True:
            @blk.sync
            def _(sync):
                sync.wait_ge(s.c_tp.s, s.c_tp.n)
                for i, (src, rows, dst) in enumerate(vecjobs):
                    s.c_ld.inc(sync.dma_start(out=vstg[0:rows, ofs[i]:ofs[i] + 128], in_=src), 16)

            @blk.tensor
            def _(t):
                t.wait_ge(s.c_ld.s, s.c_ld.n)
                t.wait_ge(s.c_ev.s, s.c_ev.n)
                for i, (src, rows, dst) in enumerate(vecjobs):
                    s.c_tp.inc(t.transpose(s.p_c[:, ofs[i]:ofs[i] + rows], vstg[0:rows, ofs[i]:ofs[i] + 128], s.ident[0:rows, 0:rows]))

            @blk.scalar
            def _(sc):
                sc.wait_ge(s.c_tp.s, s.c_tp.n)
                for i, (src, rows, dst) in enumerate(vecjobs):
                    s.c_ev.inc(sc.activation(dst.ap()[:, 0:rows], s.p_c[:, ofs[i]:ofs[i] + rows], AF.Copy))

        jobs = [(s.wu_in, s.wu16, 8, 16), (s.wz_in, s.wz16, 8, 16), (s.ow_in, s.ow16, 16, 8)]
        self.w16_mark = {}
        if IB < 8:
            jobs = []
        for (src, dst, nk, natile) in jobs:
            width = 128 * nk
            for a in range(natile):
                if (blk := self.blk) or True:
                    @blk.sync
                    def _(sync, src=src, a=a, width=width):
                        sync.wait_ge(s.c_tp.s, s.c_tp.n)
                        s.c_ld.inc(sync.dma_start(out=wstg[:, 0:width], in_=src.ap()[ts(a, 128), 0:width]), 16)

                    tp0w, xch0w, xcl0w, w160 = s.c_tp.n, s.c_xch.n, s.c_xcl.n, s.c_w16.n

                    @blk.tensor
                    def _(t, nk=nk, tp0w=tp0w, w160=w160):
                        t.wait_ge(s.c_ld.s, s.c_ld.n)
                        for kg in range(nk // 4):
                            if kg >= 1:
                                t.wait_ge(s.c_w16.s, w160 + 128 * kg)
                            for q in range(4):
                                k = kg * 4 + q
                                s.c_tp.inc(t.transpose(s.p_f[:, ts(q, 128)], wstg[:, ts(k, 128)], s.ident[:, :]))

                    @blk.scalar
                    def _(sc, nk=nk, tp0w=tp0w, w160=w160):
                        for kg in range(nk // 4):
                            sc.wait_ge(s.c_tp.s, tp0w + 4 * (kg + 1))
                            sc.wait_ge(s.c_w16.s, w160 + 128 * kg)
                            s.c_xch.inc(sc.activation(fsh[:, :], s.p_f[:, :], AF.Copy))

                    @blk.vector
                    def _(v, nk=nk, xch0w=xch0w, w160=w160):
                        for kg in range(nk // 4):
                            v.wait_ge(s.c_xch.s, xch0w + kg + 1)
                            v.wait_ge(s.c_w16.s, w160 + 128 * kg)
                            s.c_xcl.inc(v.tensor_tensor(fsl[:, :], s.p_f[:, :], fsh[:, :], AL.subtract))

                    @blk.sync
                    def _(sync, dst=dst, a=a, nk=nk, xcl0w=xcl0w):
                        for kg in range(nk // 4):
                            sync.wait_ge(s.c_xcl.s, xcl0w + kg + 1)
                            for q in range(4):
                                k = kg * 4 + q
                                s.c_w16.inc(sync.dma_start(out=dst[:, 0, k, ts(a, 128)], in_=fsh[:, ts(q, 128)]), 16)
                                s.c_w16.inc(sync.dma_start(out=dst[:, 1, k, ts(a, 128)], in_=fsl[:, ts(q, 128)]), 16)
            self.w16_mark[dst.name] = s.c_w16.n
        ictx.close()

    def open_phase(self, name):
        self._blk_cm = self.nc.Block(name)
        self.blk = self._blk_cm.__enter__()

    def close_phase(self):
        self._blk_cm.__exit__(None, None, None)
        self.blk = None

    def barrier_all(self, ctrs):
        s, nc = self, self.nc
        blk = self.blk
        for engname in ["tensor", "scalar", "vector", "gpsimd", "sync"]:
            def mk_(eng):
                for c in ctrs:
                    eng.wait_ge(c.s, c.n)
            blk.__getattribute__(engname)(mk_)

    def emit_xchunk(self, off, Tc):
        s, nc = self, self.nc
        nsub = Tc // 128
        tp0, xch0, xcl0 = s.c_tp.n, s.c_xch.n, s.c_xcl.n
        xin_vals = []
        if (blk := self.blk) or True:
            @blk.sync
            def _(sync):
                for j in range(nsub):
                    cx = s.c_xin0 if j % 2 == 0 else s.c_xin1
                    if j >= 2:
                        sync.wait_ge(s.c_tp.s, tp0 + 8 * (j - 1))
                    else:
                        sync.wait_ge(s.c_tp.s, tp0)
                    cx.inc(sync.dma_start(out=s.xnat[:, j % 2, :],
                                          in_=s.xs[off + j * 128: off + (j + 1) * 128, :]), 16)
                    xin_vals.append((cx, cx.n))

            @blk.tensor
            def _(t):
                t.wait_ge(s.c_k.s, s.c_k.n)
                for j in range(nsub):
                    t.wait_ge(xin_vals[j][0].s, xin_vals[j][1])
                    for half in range(2):
                        u = j * 2 + half
                        pb = s.p_c if u % 2 == 0 else s.p_g
                        t.wait_ge(s.c_xcl.s, xcl0 + u - 2 + 1 if u >= 2 else xcl0)
                        for q in range(4):
                            mc = half * 4 + q
                            s.c_tp.inc(t.transpose(pb[:, ts(q, 128)], s.xnat[:, j % 2, ts(mc, 128)], s.ident[:, :]))

            @blk.scalar
            def _(sc):
                for u in range(nsub * 2):
                    j, half = u // 2, u % 2
                    pb = s.p_c if u % 2 == 0 else s.p_g
                    sc.wait_ge(s.c_tp.s, tp0 + 4 * (u + 1))
                    s.c_xch.inc(sc.activation(s.xh[:, half * 4:half * 4 + 4, ts(j, 128)], pb[:, :], AF.Copy))

            @blk.vector
            def _(v):
                for u in range(nsub * 2):
                    j, half = u // 2, u % 2
                    pb = s.p_c if u % 2 == 0 else s.p_g
                    v.wait_ge(s.c_xch.s, xch0 + u + 1)
                    s.c_xcl.inc(v.tensor_tensor(s.xl[:, half * 4:half * 4 + 4, ts(j, 128)],
                                                pb[:, :], s.xh[:, half * 4:half * 4 + 4, ts(j, 128)],
                                                AL.subtract))

    def emit_proj(self, wdram, Tc, evac, evctr=None):
        s, nc = self, self.nc
        evctr = evctr if evctr is not None else s.c_ev
        mm0, ev0 = s.c_mm.n, evctr.n
        xcl_all, wld_all = s.c_xcl.n, s.c_wld.n
        wlo_vals = []
        if (blk := self.blk) or True:
            @blk.sync
            def _(sync):
                for ct in range(NB):
                    cw = s.c_wlo0 if ct % 2 == 0 else s.c_wlo1
                    if ct >= 2:
                        sync.wait_ge(s.c_mm.s, mm0 + 24 * (ct - 1))
                    cw.inc(sync.dma_start(out=s.wlo[:, ct % 2, :, :],
                                          in_=wdram[:, 1, :, ts(ct, 128)]), 16)
                    wlo_vals.append((cw, cw.n))

            @blk.tensor
            def _(t):
                t.wait_ge(s.c_wld.s, wld_all)
                t.wait_ge(s.c_xcl.s, xcl_all)
                for ct in range(NB):
                    pb = s.p_a if ct % 2 == 0 else s.p_b
                    if ct >= 2:
                        t.wait_ge(evctr.s, ev0 + (ct - 2) + 1)
                    t.wait_ge(wlo_vals[ct][0].s, wlo_vals[ct][1])
                    for k in range(8):
                        wh = s.wbig[:, k, ts(ct, 128)]
                        wl = s.wlo[:, ct % 2, k, :]
                        s.c_mm.inc(t.matmul(pb[:, 0:Tc], wh, s.xh[:, k, 0:Tc], start=(k == 0), stop=False))
                        s.c_mm.inc(t.matmul(pb[:, 0:Tc], wh, s.xl[:, k, 0:Tc], start=False, stop=False))
                        s.c_mm.inc(t.matmul(pb[:, 0:Tc], wl, s.xh[:, k, 0:Tc], start=False, stop=(k == 7)))

            @blk.scalar
            def _(sc):
                for ct in range(NB):
                    pb = s.p_a if ct % 2 == 0 else s.p_b
                    sc.wait_ge(s.c_mm.s, mm0 + 24 * (ct + 1))
                    evac(sc, ct, pb)

    def emit_m1(self):
        s, nc = self, self.nc
        HALO, T = self.HALO, self.T

        m12 = ExitStack()
        self.whi = self.sbuf(m12, "whi", [128, 2, 8, 128], F16)
        self.wlo = self.sbuf(m12, "wlo", [128, 2, 8, 128], F16)
        self.xh = self.sbuf(m12, "xh", [128, 8, 512], F16)
        self.xl = self.sbuf(m12, "xl", [128, 8, 512], F16)
        self.xnat = self.sbuf(m12, "xnat", [128, 2, DM])
        self.ubar = self.sbuf(m12, "ubar", [128, NB, 512])
        self.yac = self.sbuf(m12, "yac", [128, NB, 512])
        self.ucac = self.sbuf(m12, "ucac", [128, 2, 512])
        self.brow = self.sbuf(m12, "brow", [17, 512])
        self.bvec = self.sbuf(m12, "bvec", [1, 2, 512])
        self.m12 = m12

        m1 = ExitStack()
        self.delta = self.sbuf(m1, "delta", [128, NB, 512])
        self.utmp = self.sbuf(m1, "utmp", [128, 2, 4, 516])
        self.brep = self.sbuf(m1, "brep", [128, 2, 512])
        self.sa = self.sbuf(m1, "sa", [128, 2, 512])
        self.su = self.sbuf(m1, "su", [128, 4, 512])
        self.sh = self.sbuf(m1, "sh", [128, 2, 512])
        self.sgt = self.sbuf(m1, "sgt", [128, 2, 512])
        self.state = self.sbuf(m1, "state", [128, NB * NS])
        self.uhalo = self.sbuf(m1, "uhalo", [128, 2, NB, 4, 3])
        self.ch_id = 0

        self.barrier_all([s.c_w16, s.c_k, s.c_ev, s.c_ld, s.c_w1s0,
                          s.c_w1s1, s.c_w2s0, s.c_w2s1, s.c_xch, s.c_xcl])
        if (blk := self.blk) or True:
            @blk.vector
            def _(v):
                v.wait_ge(s.c_k.s, s.c_k.n)
                s.c_k.inc(v.memset(s.state[:, :], 0.0))
                s.c_k.inc(v.memset(s.uhalo[:, :, :, :], 0.0))

        chunks = ([(0, HALO, True)] if HALO else []) + \
                 [(HALO + i * T, T, False) for i in range(self.NCH)]
        for ci, (off, Tc, halo) in enumerate(chunks):
            self.emit_xchunk(off, Tc)
            self.emit_m1_chunk(off, Tc, halo)
            self.ch_id += 1

        m1.close()

    def emit_m1_inproj_conv(self, Tc):
        s, nc = self, self.nc
        cv0, hc0, ev0 = s.c_cv.n, s.c_hc.n, s.c_ev.n
        sil0, zev_sil0 = s.c_sil.n, s.c_zev.n
        xd_prev, spl_prev = s.c_xd.n, s.c_spl.n
        mm0 = s.c_mm.n
        xcl_all, wld_all = s.c_xcl.n, s.c_wld.n
        wlo_vals = []
        blk = self.blk
        par = s.ch_id % 2

        @blk.sync
        def _(sync):
            for ct in range(NB):
                cw = s.c_wlo0 if ct % 2 == 0 else s.c_wlo1
                if ct >= 2:
                    sync.wait_ge(s.c_mm.s, mm0 + 24 * (ct - 1))
                cw.inc(sync.dma_start(out=s.whi[:, ct % 2, :, :],
                                      in_=s.wu16[:, 0, :, ts(ct, 128)]), 16)
                cw.inc(sync.dma_start(out=s.wlo[:, ct % 2, :, :],
                                      in_=s.wu16[:, 1, :, ts(ct, 128)]), 16)
                wlo_vals.append((cw, cw.n))

        @blk.tensor
        def _(t):
            t.wait_ge(s.c_xcl.s, xcl_all)
            for ct in range(NB):
                pb = s.p_a if ct % 2 == 0 else s.p_b
                if ct >= 2:
                    t.wait_ge(s.c_ev.s, ev0 + 4 * (ct - 1))
                t.wait_ge(wlo_vals[ct][0].s, wlo_vals[ct][1])
                for k in range(8):
                    wh = s.whi[:, ct % 2, k, :]
                    wl = s.wlo[:, ct % 2, k, :]
                    s.c_mm.inc(t.matmul(pb[:, 0:Tc], wh, s.xh[:, k, 0:Tc], start=(k == 0), stop=False))
                    s.c_mm.inc(t.matmul(pb[:, 0:Tc], wh, s.xl[:, k, 0:Tc], start=False, stop=False))
                    s.c_mm.inc(t.matmul(pb[:, 0:Tc], wl, s.xh[:, k, 0:Tc], start=False, stop=(k == 7)))

        @blk.scalar
        def _(sc):
            sc.wait_ge(s.c_xd.s, xd_prev)

            def sigma_chain(j):
                if j >= 2:
                    sc.wait_ge(s.c_sil.s, sil0 + (j - 2) + 1)
                s.c_sc.inc(sc.activation(s.sgt[:, j % 2, 0:Tc], s.ucac[:, j % 2, 0:Tc], AF.Exp, scale=-1.0))
                sc.wait_ge(s.c_sc.s, s.c_sc.n)
                s.c_sc.inc(sc.activation(s.sgt[:, j % 2, 0:Tc], s.sgt[:, j % 2, 0:Tc], AF.Ln, bias=1.0))
                sc.wait_ge(s.c_sc.s, s.c_sc.n)
                s.c_zev.inc(sc.activation(s.sgt[:, j % 2, 0:Tc], s.sgt[:, j % 2, 0:Tc], AF.Exp, scale=-1.0))

            for ct in range(NB):
                pb = s.p_a if ct % 2 == 0 else s.p_b
                sc.wait_ge(s.c_mm.s, mm0 + 24 * (ct + 1))
                if ct >= 2:
                    sc.wait_ge(s.c_cv.s, cv0 + (ct - 2) + 1)
                for k in range(4):
                    s.c_ev.inc(sc.activation(s.utmp[:, ct % 2, k, 3:3 + Tc], pb[:, 0:Tc],
                                             AF.Identity, scale=s.convw_sb[:, ct, k:k + 1]))
                if ct >= 2:
                    sigma_chain(ct - 2)
            for j in (NB - 2, NB - 1):
                sc.wait_ge(s.c_cv.s, cv0 + j + 1)
                sigma_chain(j)

        @blk.vector
        def _(v):
            v.wait_ge(s.c_spl.s, spl_prev)

            def sigma_mult(j):
                v.wait_ge(s.c_zev.s, zev_sil0 + j + 1)
                s.c_sil.inc(v.tensor_tensor(s.ubar[:, j, 0:Tc], s.ucac[:, j % 2, 0:Tc],
                                            s.sgt[:, j % 2, 0:Tc], AL.mult))

            for ct in range(NB):
                v.wait_ge(s.c_ev.s, ev0 + 4 * (ct + 1))
                s.c_hc.inc(v.tensor_copy(s.utmp[:, ct % 2, :, 0:3], s.uhalo[:, par, ct, :, :]))
                s.c_hc.inc(v.tensor_copy(s.uhalo[:, 1 - par, ct, :, :], s.utmp[:, ct % 2, :, Tc:Tc + 3]))
                if ct >= 2:
                    sigma_mult(ct - 2)
            for j in (NB - 2, NB - 1):
                sigma_mult(j)

        @blk.gpsimd
        def _(g):
            for cp in range(NB // 2):
                c0, c1 = 2 * cp, 2 * cp + 1
                g.wait_ge(s.c_hc.s, hc0 + 2 * c1 + 2)
                if c0 >= 2:
                    g.wait_ge(s.c_sil.s, sil0 + (c1 - 2) + 1)
                gplast = {}
                for k in range(1, 4):
                    for ct in (c0, c1):
                        if k > 1:
                            g.wait_ge(s.c_gp.s, gplast[ct])
                        if k == 1:
                            ii = g.tensor_tensor(s.ucac[:, ct % 2, 0:Tc],
                                                 s.utmp[:, ct % 2, 0, 0:Tc],
                                                 s.utmp[:, ct % 2, 1, 1:1 + Tc], AL.add)
                        else:
                            ii = g.tensor_tensor(s.ucac[:, ct % 2, 0:Tc],
                                                 s.ucac[:, ct % 2, 0:Tc],
                                                 s.utmp[:, ct % 2, k, k:k + Tc], AL.add)
                        if k == 3:
                            s.c_cv.inc(ii)
                        else:
                            s.c_gp.inc(ii)
                            gplast[ct] = s.c_gp.n

    def emit_m1_chunk(self, off, Tc, halo):
        s, nc = self, self.nc
        dl_prev, ya_prev, spl_prev = s.c_dl.n, s.c_ya.n, s.c_spl.n
        self.emit_m1_inproj_conv(Tc)
        sil_all = s.c_sil.n
        xd0, bev0, dt0 = s.c_xd.n, s.c_bev.n, s.c_dt.n
        if (blk := self.blk) or True:
            @blk.tensor
            def _(t):
                t.wait_ge(s.c_sil.s, sil_all)
                t.wait_ge(s.c_dl.s, dl_prev)
                for k in range(NB):
                    s.c_xd.inc(t.matmul(s.p_d[0:17, 0:Tc], s.xp_sb[:, k, :], s.ubar[:, k, 0:Tc],
                                        start=(k == 0), stop=(k == NB - 1)))

            @blk.scalar
            def _(sc):
                sc.wait_ge(s.c_xd.s, xd0 + NB)
                s.c_bev.inc(sc.activation(s.brow[:, 0:Tc], s.p_d[0:17, 0:Tc], AF.Copy))



            @blk.tensor
            def _(t):
                t.wait_ge(s.c_bev.s, bev0 + 1)
                s.c_dt.inc(t.matmul(s.p_d[:, 0:Tc], s.ones_sb[0:1, 0:128], s.brow[0:1, 0:Tc],
                                    start=True, stop=True))

            @blk.scalar
            def _(sc):
                sc.wait_ge(s.c_dt.s, dt0 + 1)
                sc.wait_ge(s.c_ya.s, ya_prev)
                for b in range(NB):
                    s.c_sc.inc(sc.activation(s.delta[:, b, 0:Tc], s.p_d[:, 0:Tc], AF.Exp,
                                             bias=s.dtb_sb[:, b:b + 1], scale=s.dtw_sb[:, b:b + 1]))
                    sc.wait_ge(s.c_sc.s, s.c_sc.n)
                    s.c_dl.inc(sc.activation(s.delta[:, b, 0:Tc], s.delta[:, b, 0:Tc], AF.Ln,
                                             bias=1.0))

        dl_all = s.c_dl.n
        brm0, bre0 = s.c_brm.n, s.c_bre.n
        sa0, su0, sh0, sp0, ya0 = s.c_sa.n, s.c_su.n, s.c_sh.n, s.c_sp.n, s.c_ya.n
        bev_all = s.c_bev.n
        bv_vals = []
        if (blk := self.blk) or True:
            @blk.sync
            def _(sync):
                sync.wait_ge(s.c_bev.s, bev_all)
                for n in range(NS):
                    cb = s.c_bv0 if n % 2 == 0 else s.c_bv1
                    if n >= 2:
                        sync.wait_ge(s.c_brm.s, brm0 + (n - 1))
                    cb.inc(sync.dma_start(out=s.bvec[0:1, n % 2, 0:Tc],
                                          in_=s.brow[1 + n:2 + n, 0:Tc]), 16)
                    bv_vals.append((cb, cb.n))

            @blk.tensor
            def _(t):
                for n in range(NS):
                    t.wait_ge(bv_vals[n][0].s, bv_vals[n][1])
                    if n >= 2:
                        t.wait_ge(s.c_su.s, su0 + (n - 1) * NB)
                        if halo:
                            t.wait_ge(s.c_sh.s, sh0 + (n - 1) * NB)
                        else:
                            t.wait_ge(s.c_sp.s, sp0 + (n - 1) * NB)
                    pb = s.p_e if n % 2 == 0 else s.p_f
                    s.c_brm.inc(t.matmul(pb[:, 0:Tc], s.ones_sb[0:1, 0:128], s.bvec[0:1, n % 2, 0:Tc],
                                         start=True, stop=True))

            @blk.scalar
            def _(sc):
                sc.wait_ge(s.c_dl.s, dl_all)
                for n in range(NS):
                    pb = s.p_e if n % 2 == 0 else s.p_f
                    sc.wait_ge(s.c_brm.s, brm0 + n + 1)
                    s.c_bre.inc(sc.activation(s.brep[:, n % 2, 0:Tc], pb[:, 0:Tc], AF.Copy))
                    for b in range(NB):
                        i = n * NB + b
                        col = b * NS + n
                        if i >= 2:
                            sc.wait_ge(s.c_sh.s, sh0 + (i - 2) + 1)
                        s.c_sa.inc(sc.activation(s.sa[:, i % 2, 0:Tc], s.delta[:, b, 0:Tc], AF.Exp,
                                                 scale=s.a_sb[:, col:col + 1]))

            @blk.gpsimd
            def _(g):
                g.wait_ge(s.c_dl.s, dl_all)
                for i in range(NI):
                    n, b = i // NB, i % NB
                    g.wait_ge(s.c_bre.s, bre0 + n + 1)
                    if i >= 2:
                        g.wait_ge(s.c_sh.s, sh0 + (i - 2) + 1)
                    s.c_su.inc(g.tensor_tensor(s.su[:, i % 2, 0:Tc], s.delta[:, b, 0:Tc],
                                               s.brep[:, n % 2, 0:Tc], AL.mult))
                    if (not halo) and i >= 1:
                        ip = i - 1
                        np_, bp = ip // NB, ip % NB
                        if np_ > 0:
                            g.wait_ge(s.c_sp.s, sp0 + ip + 1)
                            if ip >= NB:
                                g.wait_ge(s.c_ya.s, ya0 + (ip - NB) + 1)
                            s.c_ya.inc(g.tensor_tensor(s.yac[:, bp, 0:Tc], s.yac[:, bp, 0:Tc],
                                                       s.su[:, 2 + ip % 2, 0:Tc], AL.add))
                        else:
                            s.c_ya.inc(g.engine_nop())
                if not halo:
                    ip = NI - 1
                    bp = ip % NB
                    g.wait_ge(s.c_sp.s, sp0 + ip + 1)
                    g.wait_ge(s.c_ya.s, ya0 + (ip - NB) + 1)
                    s.c_ya.inc(g.tensor_tensor(s.yac[:, bp, 0:Tc], s.yac[:, bp, 0:Tc],
                                               s.su[:, 2 + ip % 2, 0:Tc], AL.add))

            @blk.vector
            def _(v):
                if not halo:
                    v.wait_ge(s.c_spl.s, spl_prev)
                for i in range(NI):
                    n, b = i // NB, i % NB
                    col = b * NS + n
                    v.wait_ge(s.c_sa.s, sa0 + i + 1)
                    v.wait_ge(s.c_su.s, su0 + i + 1)
                    s.c_dv.inc(v.tensor_tensor_scan(s.sh[:, i % 2, 0:Tc], s.sa[:, i % 2, 0:Tc],
                                                    s.su[:, i % 2, 0:Tc], s.state[:, col:col + 1],
                                                    AL.mult, AL.add))
                    v.wait_ge(s.c_dv.s, s.c_dv.n)
                    s.c_sh.inc(v.tensor_copy(s.state[:, col:col + 1], s.sh[:, i % 2, Tc - 1:Tc]))
                    if not halo:
                        if n == 0:
                            s.c_sp.inc(v.tensor_tensor(s.yac[:, b, 0:Tc], s.sh[:, i % 2, 0:Tc],
                                                       s.brep[:, n % 2, 0:Tc], AL.mult))
                        else:
                            if i >= 2:
                                v.wait_ge(s.c_ya.s, ya0 + (i - 2) + 1)
                            s.c_sp.inc(v.tensor_tensor(s.su[:, 2 + i % 2, 0:Tc], s.sh[:, i % 2, 0:Tc],
                                                       s.brep[:, n % 2, 0:Tc], AL.mult))

        if not halo:
            real_off = off - self.HALO
            ya_all, sil_all2 = s.c_ya.n, s.c_sil.n
            if (blk := self.blk) or True:
                @blk.sync
                def _(sync):
                    sync.wait_ge(s.c_ya.s, ya_all)
                    sync.wait_ge(s.c_sil.s, sil_all2)
                    s.c_spl.inc(sync.dma_start(out=s.ub_dram[:, :, real_off:real_off + Tc],
                                               in_=s.ubar[:, :, 0:Tc]), 16)
                    s.c_spl.inc(sync.dma_start(out=s.y_dram[:, :, real_off:real_off + Tc],
                                               in_=s.yac[:, :, 0:Tc]), 16)

    def emit_m2(self):
        s, nc = self, self.nc
        T = self.T
        m2 = ExitStack()
        self.wbig = self.sbuf(m2, "wbig", [128, 8, DI], F16)
        self.y3h = self.sbuf(m2, "y3h", [128, 2, 512], F16)
        self.y3l = self.sbuf(m2, "y3l", [128, 2, 512], F16)

        self.barrier_all([s.c_spl, s.c_sh, s.c_ya, s.c_out])
        if (blk := self.blk) or True:
            @blk.sync
            def _(sync):
                s.c_wld.inc(sync.dma_start(out=s.wbig.ap(), in_=s.wz16.ap()[:, 0, :, :]), 16)

        for ci in range(self.NCH):
            off = self.HALO + ci * T
            self.emit_xchunk(off, T)
            self.emit_m2_chunk(ci * T, T)
        m2.close()
        self.m12.close()

    def emit_m2_chunk(self, real_off, Tc):
        s, nc = self, self.nc
        zev0, g10, g20, y3c0, y3l0 = s.c_zev.n, s.c_gt1.n, s.c_gt2.n, s.c_y3c.n, s.c_y3l.n
        zgt0 = s.c_rt2.n
        y3s_even0, y3s_odd0 = s.c_y3s0.n, s.c_y3s1.n
        xin0 = s.c_xin.n
        if (blk := self.blk) or True:
            @blk.sync
            def _(sync):
                sync.wait_ge(s.c_gt2.s, g20)
                s.c_xin.inc(sync.dma_start(out=s.ubar.ap(), in_=s.ub_dram[:, :, real_off:real_off + Tc]), 16)
                s.c_xin.inc(sync.dma_start(out=s.yac.ap(), in_=s.y_dram[:, :, real_off:real_off + Tc]), 16)

        sgt2 = s.y3h
        zsg0 = s.c_zev.n

        def evac_z(sc, ct, pb):
            if ct >= 2:
                sc.wait_ge(s.c_gt2.s, g20 + (ct - 2) + 1)
            s.c_sc.inc(sc.activation(s.ucac[:, ct % 2, 0:Tc], pb[:, 0:Tc], AF.Exp, scale=-1.0))
            sc.wait_ge(s.c_sc.s, s.c_sc.n)
            s.c_sc.inc(sc.activation(s.ucac[:, ct % 2, 0:Tc], s.ucac[:, ct % 2, 0:Tc], AF.Ln, bias=1.0))
            sc.wait_ge(s.c_sc.s, s.c_sc.n)
            s.c_zev.inc(sc.activation(s.ucac[:, ct % 2, 0:Tc], s.ucac[:, ct % 2, 0:Tc], AF.Exp, scale=-1.0))

        self.emit_proj(s.wz16, Tc, evac_z, evctr=s.c_rt2)

        if (blk := self.blk) or True:
            @blk.vector
            def _(v):
                v.wait_ge(s.c_xin.s, xin0 + 32)
                for ct in range(NB):
                    s.c_gt1.inc(v.scalar_tensor_tensor(s.yac[:, ct, 0:Tc], s.ubar[:, ct, 0:Tc],
                                                       s.dpar_sb[:, ct:ct + 1], s.yac[:, ct, 0:Tc],
                                                       AL.mult, AL.add))

            @blk.vector
            def _(v):
                for ct in range(NB):
                    pb = s.p_a if ct % 2 == 0 else s.p_b
                    v.wait_ge(s.c_zev.s, zev0 + ct + 1)
                    s.c_rt2.inc(v.tensor_tensor(s.ucac[:, ct % 2, 0:Tc], pb[:, 0:Tc],
                                                s.ucac[:, ct % 2, 0:Tc], AL.mult))

            @blk.gpsimd
            def _(g):
                for ct in range(NB):
                    g.wait_ge(s.c_rt2.s, zgt0 + ct + 1)
                    g.wait_ge(s.c_gt1.s, g10 + ct + 1)
                    s.c_gt2.inc(g.tensor_tensor(s.yac[:, ct, 0:Tc], s.yac[:, ct, 0:Tc],
                                                s.ucac[:, ct % 2, 0:Tc], AL.mult))

            @blk.vector
            def _(v):
                for ct in range(NB):
                    v.wait_ge(s.c_gt2.s, g20 + ct + 1)
                    cy = s.c_y3s0 if ct % 2 == 0 else s.c_y3s1
                    base = y3s_even0 if ct % 2 == 0 else y3s_odd0
                    v.wait_ge(cy.s, base + 32 * (ct // 2))
                    s.c_y3c.inc(v.tensor_copy(s.y3h[:, ct % 2, 0:Tc], s.yac[:, ct, 0:Tc]))

            @blk.gpsimd
            def _(g):
                for ct in range(NB):
                    g.wait_ge(s.c_y3c.s, y3c0 + ct + 1)
                    s.c_y3l.inc(g.tensor_tensor(s.y3l[:, ct % 2, 0:Tc], s.yac[:, ct, 0:Tc],
                                                s.y3h[:, ct % 2, 0:Tc], AL.subtract))

            @blk.sync
            def _(sync):
                for ct in range(NB):
                    cy = s.c_y3s0 if ct % 2 == 0 else s.c_y3s1
                    sync.wait_ge(s.c_y3l.s, y3l0 + ct + 1)
                    cy.inc(sync.dma_start(out=s.y3_dram[:, 0, ct, real_off:real_off + Tc],
                                          in_=s.y3h[:, ct % 2, 0:Tc]), 16)
                    cy.inc(sync.dma_start(out=s.y3_dram[:, 1, ct, real_off:real_off + Tc],
                                          in_=s.y3l[:, ct % 2, 0:Tc]), 16)

    def emit_m3(self):
        s, nc = self, self.nc
        T = self.T
        m3 = ExitStack()
        self.wbig2 = self.sbuf(m3, "wbig2", [128, 2, 16, DM], F16)
        self.y3hc = self.sbuf(m3, "y3hc", [128, NB, 512], F16)
        self.y3lc = self.sbuf(m3, "y3lc", [128, NB, 512], F16)
        self.om = self.sbuf(m3, "om", [128, 8, 512])
        self.x1t = self.sbuf(m3, "x1t", [128, 4, DM])
        self.x1T = self.sbuf(m3, "x1T", [128, 8, 512], F32)
        self.xr = self.sbuf(m3, "xr", [128, 4, DM])
        self.lnq = self.sbuf(m3, "lnq", [128, DM])
        self.lnc = self.sbuf(m3, "lnc", [128, 4, 8])
        self.lg8 = self.sbuf(m3, "lg8", [8, 512])
        self.lgt = self.sbuf(m3, "lgt", [128, 4, 8])
        self.sfx = self.sbuf(m3, "sfx", [128, 4, 8])

        self.barrier_all([s.c_y3s0, s.c_y3s1, s.c_gt2, s.c_zev])
        if (blk := self.blk) or True:
            @blk.sync
            def _(sync):
                s.c_wld.inc(sync.dma_start(out=s.wbig2.ap(), in_=s.ow16.ap()), 16)

        for ci in range(self.NCH):
            self.emit_m3_chunk(ci * T, T)
        m3.close()

    def emit_m3_chunk(self, real_off, Tc):
        s, nc = self, self.nc
        nsub = Tc // 128
        xin0 = s.c_xin.n
        oev0, otp0, ocp0 = s.c_oev.n, s.c_otp.n, s.c_ocp.n
        mm0 = s.c_mm.n
        ln10, ln20 = s.c_ln1.n, s.c_ln2.n
        rt10, rt20, rt30 = s.c_rt1.n, s.c_rt2.n, s.c_rt3.n
        sfx0, out_prev = s.c_sfx.n, s.c_out.n
        lgr_prev, sct_prev = s.c_lgr.n, s.c_sct.n

        if (blk := self.blk) or True:
            @blk.sync
            def _(sync):
                sync.wait_ge(s.c_oev.s, oev0)
                sync.wait_ge(s.c_out.s, out_prev)
                s.c_xin.inc(sync.dma_start(out=s.y3hc[:, :, 0:Tc], in_=s.y3_dram[:, 0, :, real_off:real_off + Tc]), 16)
                s.c_xin.inc(sync.dma_start(out=s.y3lc[:, :, 0:Tc], in_=s.y3_dram[:, 1, :, real_off:real_off + Tc]), 16)
                sync.wait_ge(s.c_ocp.s, ocp0)
                for j in range(nsub):
                    s.c_xr.inc(sync.dma_start(out=s.xr[:, j, :],
                                              in_=s.xs[s.HALO + real_off + j * 128: s.HALO + real_off + (j + 1) * 128, :]), 16)

            @blk.tensor
            def _(t):
                t.wait_ge(s.c_wld.s, s.c_wld.n)
                t.wait_ge(s.c_xin.s, xin0 + 32)
                for m in range(8):
                    pb = s.p_a if m % 2 == 0 else s.p_b
                    if m >= 2:
                        t.wait_ge(s.c_oev.s, oev0 + (m - 2) + 1)
                    for k in range(16):
                        wh = s.wbig2[:, 0, k, ts(m, 128)]
                        wl = s.wbig2[:, 1, k, ts(m, 128)]
                        s.c_mm.inc(t.matmul(pb[:, 0:Tc], wh, s.y3hc[:, k, 0:Tc], start=(k == 0), stop=False))
                        s.c_mm.inc(t.matmul(pb[:, 0:Tc], wh, s.y3lc[:, k, 0:Tc], start=False, stop=False))
                        s.c_mm.inc(t.matmul(pb[:, 0:Tc], wl, s.y3hc[:, k, 0:Tc], start=False, stop=(k == 15)))

            @blk.scalar
            def _(sc):
                for m in range(8):
                    pb = s.p_a if m % 2 == 0 else s.p_b
                    sc.wait_ge(s.c_mm.s, mm0 + 48 * (m + 1))
                    s.c_oev.inc(sc.activation(s.om[:, m, 0:Tc], pb[:, 0:Tc], AF.Identity,
                                              bias=s.ob_sb[:, m:m + 1]))

            @blk.tensor
            def _(t):
                for j in range(nsub):
                    for mg in range(2):
                        u = j * 2 + mg
                        pb = s.p_e if u % 2 == 0 else s.p_f
                        t.wait_ge(s.c_oev.s, oev0 + mg * 4 + 4)
                        t.wait_ge(s.c_ocp.s, ocp0 + u - 1 + 1 if u >= 1 else ocp0)
                        for q in range(4):
                            m = mg * 4 + q
                            s.c_otp.inc(t.transpose(pb[:, ts(q, 128)], s.om[:, m, ts(j, 128)], s.ident[:, :]))

            @blk.vector
            def _(v):
                v.wait_ge(s.c_out.s, out_prev)
                v.wait_ge(s.c_xr.s, s.c_xr.n)
                for j in range(nsub):
                    for mg in range(2):
                        u = j * 2 + mg
                        pb = s.p_e if u % 2 == 0 else s.p_f
                        v.wait_ge(s.c_otp.s, otp0 + 4 * (u + 1))
                        s.c_ocp.inc(v.tensor_tensor(s.x1t[:, j, ts(mg, 512)], pb[:, :],
                                                    s.xr[:, j, ts(mg, 512)], AL.add))

            @blk.vector
            def _(v):
                for j in range(nsub):
                    v.wait_ge(s.c_ocp.s, ocp0 + 2 * (j + 1))
                    s.c_dv.inc(v.tensor_tensor(s.lnq[:, :], s.x1t[:, j, :], s.x1t[:, j, :], AL.mult))
                    v.wait_ge(s.c_dv.s, s.c_dv.n)
                    s.c_dv.inc(v.reduce_sum(s.lnc[:, j, 1:2], s.lnq[:, :], axis=mybir.AxisListType.X))
                    s.c_dv.inc(v.reduce_sum(s.lnc[:, j, 0:1], s.x1t[:, j, :], axis=mybir.AxisListType.X))
                    v.wait_ge(s.c_dv.s, s.c_dv.n)
                    s.c_dv.inc(v.tensor_scalar_mul(s.lnc[:, j, 2:3], s.lnc[:, j, 0:1], 1.0 / DM))
                    s.c_dv.inc(v.tensor_scalar_mul(s.lnc[:, j, 3:4], s.lnc[:, j, 1:2], 1.0 / DM))
                    v.wait_ge(s.c_dv.s, s.c_dv.n)
                    s.c_dv.inc(v.tensor_tensor(s.lnc[:, j, 4:5], s.lnc[:, j, 2:3], s.lnc[:, j, 2:3], AL.mult))
                    v.wait_ge(s.c_dv.s, s.c_dv.n)
                    s.c_dv.inc(v.tensor_tensor(s.lnc[:, j, 4:5], s.lnc[:, j, 3:4], s.lnc[:, j, 4:5], AL.subtract))
                    v.wait_ge(s.c_dv.s, s.c_dv.n)
                    s.c_ln1.inc(v.tensor_scalar_add(s.lnc[:, j, 4:5], s.lnc[:, j, 4:5], 1e-5))

            @blk.scalar
            def _(sc):
                for j in range(nsub):
                    sc.wait_ge(s.c_ln1.s, ln10 + j + 1)
                    s.c_sc.inc(sc.activation(s.lnc[:, j, 5:6], s.lnc[:, j, 4:5], AF.Ln))
                    sc.wait_ge(s.c_sc.s, s.c_sc.n)
                    s.c_ln2.inc(sc.activation(s.lnc[:, j, 6:7], s.lnc[:, j, 5:6], AF.Exp,
                                              scale=-0.5))

            @blk.vector
            def _(v):
                for j in range(nsub):
                    v.wait_ge(s.c_ln2.s, ln20 + j + 1)
                    s.c_dv.inc(v.tensor_tensor(s.lnc[:, j, 7:8], s.lnc[:, j, 2:3], s.lnc[:, j, 6:7], AL.mult))
                    v.wait_ge(s.c_dv.s, s.c_dv.n)
                    s.c_ln3.inc(v.tensor_scalar_mul(s.lnc[:, j, 7:8], s.lnc[:, j, 7:8], -1.0))

            @blk.scalar
            def _(sc):
                for j in range(nsub):
                    sc.wait_ge(s.c_ln3.s, s.c_ln3.n - nsub + j + 1)
                    s.c_sfx.inc(sc.activation(s.x1t[:, j, :], s.x1t[:, j, :], AF.Identity,
                                              scale=s.lnc[:, j, 6:7], bias=s.lnc[:, j, 7:8]))

            @blk.tensor
            def _(t):
                for j in range(nsub):
                    for mg in range(2):
                        u = j * 2 + mg
                        pb = s.p_e if u % 2 == 0 else s.p_f
                        t.wait_ge(s.c_sfx.s, sfx0 + j + 1)
                        if u < 2:
                            t.wait_ge(s.c_ocp.s, ocp0 + 2 * nsub - 1 + u)
                        t.wait_ge(s.c_rt1.s, rt10 + u - 1 + 1 if u >= 1 else rt10)
                        for q in range(4):
                            mc = mg * 4 + q
                            s.c_otp.inc(t.transpose(pb[:, ts(q, 128)], s.x1t[:, j, ts(mc, 128)], s.ident[:, :]))

            @blk.vector
            def _(v):
                base = s.c_otp.n - 2 * nsub * 4
                for j in range(nsub):
                    for mg in range(2):
                        u = j * 2 + mg
                        pb = s.p_e if u % 2 == 0 else s.p_f
                        v.wait_ge(s.c_otp.s, base + 4 * (u + 1))
                        s.c_rt1.inc(v.tensor_copy(s.x1T[:, ts(mg, 4).start:ts(mg, 4).stop, ts(j, 128)]
                                                  if False else s.x1T[:, mg * 4:mg * 4 + 4, ts(j, 128)], pb[:, :]))

            @blk.tensor
            def _(t):
                t.wait_ge(s.c_rt1.s, rt10 + 2 * nsub)
                t.wait_ge(s.c_bev.s, s.c_bev.n)
                for k in range(8):
                    s.c_rt2.inc(t.matmul(s.p_d[0:8, 0:Tc], s.rw_sb[:, k, :], s.x1T[:, k, 0:Tc],
                                         start=(k == 0), stop=(k == 7)))

            @blk.scalar
            def _(sc):
                sc.wait_ge(s.c_rt2.s, rt20 + 8)
                if s.FFN:
                    sc.wait_ge(s.c_out.s, out_prev)
                s.c_rt3.inc(sc.activation(s.lg8[:, 0:Tc], s.p_d[0:8, 0:Tc], AF.Copy))

            @blk.tensor
            def _(t):
                t.wait_ge(s.c_rt3.s, rt30 + 1)
                t.wait_ge(s.c_lgr.s, lgr_prev)
                t.wait_ge(s.c_sct.s, sct_prev)
                for j in range(nsub):
                    s.c_otp.inc(t.transpose(s.p_h[:, ts(j, 8)], s.lg8[0:8, ts(j, 128)], s.ident[0:8, 0:8]))

            @blk.vector
            def _(v):
                base = s.c_otp.n - nsub
                v.wait_ge(s.c_otp.s, base + nsub)
                for j in range(nsub):
                    s.c_lgr.inc(v.tensor_tensor(s.lgt[:, j, :], s.p_h[:, ts(j, 8)], s.rb_sb[:, :], AL.add))
                    v.wait_ge(s.c_lgr.s, s.c_lgr.n)
                    s.c_dv.inc(v.reduce_max(s.lnc[:, j, 0:1], s.lgt[:, j, :], axis=mybir.AxisListType.X))
                    v.wait_ge(s.c_dv.s, s.c_dv.n)
                    s.c_ln1.inc(v.tensor_scalar_mul(s.lnc[:, j, 1:2], s.lnc[:, j, 0:1], -1.0))

            @blk.scalar
            def _(sc):
                base = s.c_ln1.n - nsub
                for j in range(nsub):
                    sc.wait_ge(s.c_ln1.s, base + j + 1)
                    s.c_ln2.inc(sc.activation(s.sfx[:, j, :], s.lgt[:, j, :], AF.Exp,
                                              bias=s.lnc[:, j, 1:2]))

            @blk.vector
            def _(v):
                base = s.c_ln2.n - nsub
                for j in range(nsub):
                    v.wait_ge(s.c_ln2.s, base + j + 1)
                    s.c_dv.inc(v.reduce_sum(s.lnc[:, j, 2:3], s.sfx[:, j, :], axis=mybir.AxisListType.X))
                    v.wait_ge(s.c_dv.s, s.c_dv.n)
                    s.c_dv.inc(v.reciprocal(s.lnc[:, j, 3:4], s.lnc[:, j, 2:3]))
                    v.wait_ge(s.c_dv.s, s.c_dv.n)
                    s.c_rt1.inc(v.tensor_scalar(s.sfx[:, j, :], s.sfx[:, j, :],
                                                s.lnc[:, j, 3:4], None, AL.mult))

            if s.FFN:
                sct_p0 = s.c_otp.n

                @blk.tensor
                def _(t):
                    base = s.c_rt1.n - nsub
                    t.wait_ge(s.c_lgr.s, s.c_lgr.n)
                    for j in range(nsub):
                        t.wait_ge(s.c_rt1.s, base + j + 1)
                        s.c_otp.inc(t.transpose(s.p_h[0:8, ts(j, 128)], s.sfx[:, j, :],
                                                s.ident[:, :]))

                @blk.scalar
                def _(sc):
                    sc.wait_ge(s.c_otp.s, sct_p0 + nsub)
                    s.c_sct.inc(sc.activation(s.lg8[:, 0:nsub * 128], s.p_h[0:8, 0:nsub * 128],
                                              AF.Copy))

                @blk.sync
                def _(sync):
                    base = s.c_rt1.n - nsub
                    sync.wait_ge(s.c_sct.s, s.c_sct.n)
                    s.c_out.inc(sync.dma_start(out=s.scT_dram[:, real_off:real_off + Tc],
                                               in_=s.lg8[:, 0:Tc]), 16)
                    for j in range(nsub):
                        sync.wait_ge(s.c_rt1.s, base + j + 1)
                        s.c_out.inc(sync.dma_start(out=s.x1_out[real_off + j * 128: real_off + (j + 1) * 128, :],
                                                   in_=s.x1t[:, j, :]), 16)
            else:
                @blk.sync
                def _(sync):
                    base = s.c_rt1.n - nsub
                    for j in range(nsub):
                        sync.wait_ge(s.c_rt1.s, base + j + 1)
                        s.c_out.inc(sync.dma_start(out=s.sc_out[real_off + j * 128: real_off + (j + 1) * 128, :],
                                                   in_=s.sfx[:, j, :]), 16)
                        s.c_out.inc(sync.dma_start(out=s.x1_out[real_off + j * 128: real_off + (j + 1) * 128, :],
                                                   in_=s.x1t[:, j, :]), 16)

    def emit_ffn_split(self):
        s, nc = self, self.nc
        DFF = self.DFF
        fctx = ExitStack()
        wst = self.sbuf(fctx, "wst", [128, 2, max(DFF, DM)])
        fh = self.sbuf(fctx, "fh", [128, 2, 512], F16)
        fl = self.sbuf(fctx, "fl", [128, 2, 512], F16)
        ld0, hh0, ll0, st0 = s.c_ld.n, s.c_xch.n, s.c_xcl.n, s.c_w16.n
        jobs = [(s.w1_in, s.w116, DFF, DM // 128), (s.w2_in, s.w216, DM, DFF // 128)]
        pieces = []
        for (src, dst, W, natile) in jobs:
            for a in range(natile):
                pieces.append((src, dst, W, a))
        qof = [0]
        for (_, _, W, _) in pieces:
            qof.append(qof[-1] + W // 512)

        wvals = []
        svals = []
        if (blk := self.blk) or True:
            @blk.sync
            def _(sync):
                qs = 0
                for u in range(len(pieces) + 1):
                    if u < len(pieces):
                        src, dst, W, a = pieces[u]
                        cw = s.c_w1s0 if u % 2 == 0 else s.c_w1s1
                        if u >= 2:
                            sync.wait_ge(s.c_xcl.s, ll0 + qof[u - 1])
                        cw.inc(sync.dma_start(out=wst[:, u % 2, 0:W],
                                              in_=src.ap()[ts(a, 128), :]), 16)
                        wvals.append((cw, cw.n))
                    if u >= 1:
                        psrc, pdst, pW, pa = pieces[u - 1]
                        for q in range(pW // 512):
                            cst = s.c_w2s0 if qs % 2 == 0 else s.c_w2s1
                            sync.wait_ge(s.c_xcl.s, ll0 + qs + 1)
                            cst.inc(sync.dma_start(out=pdst[:, 0, pa, ts(q, 512)],
                                                   in_=fh[:, qs % 2, :]), 16)
                            cst.inc(sync.dma_start(out=pdst[:, 1, pa, ts(q, 512)],
                                                   in_=fl[:, qs % 2, :]), 16)
                            svals.append((cst, cst.n))
                            qs += 1

            @blk.scalar
            def _(sc):
                qq = 0
                for u, (src, dst, W, a) in enumerate(pieces):
                    sc.wait_ge(wvals[u][0].s, wvals[u][1])
                    for q in range(W // 512):
                        if qq >= 2:
                            sc.wait_ge(svals[qq - 2][0].s, svals[qq - 2][1])
                        s.c_xch.inc(sc.activation(fh[:, qq % 2, :],
                                                  wst[:, u % 2, ts(q, 512)], AF.Copy))
                        qq += 1

            @blk.vector
            def _(v):
                qq = 0
                for u, (src, dst, W, a) in enumerate(pieces):
                    for q in range(W // 512):
                        v.wait_ge(s.c_xch.s, hh0 + qq + 1)
                        if qq >= 2:
                            v.wait_ge(svals[qq - 2][0].s, svals[qq - 2][1])
                        s.c_xcl.inc(v.tensor_tensor(fl[:, qq % 2, :],
                                                    wst[:, u % 2, ts(q, 512)],
                                                    fh[:, qq % 2, :], AL.subtract))
                        qq += 1

        fctx.close()

    def emit_ffn_bvec(self):
        s, nc = self, self.nc
        DFF = self.DFF
        fctx = ExitStack()
        brow2 = self.sbuf(fctx, "brow2b", [1, max(DFF, DM)])
        ld0, tp0, ev0 = s.c_ld.n, s.c_tp.n, s.c_ev.n
        if (blk := self.blk) or True:
            @blk.sync
            def _(sync):
                s.c_ld.inc(sync.dma_start(out=brow2[0:1, 0:DFF], in_=s.b1_in.ap()), 16)

            @blk.tensor
            def _(t):
                t.wait_ge(s.c_ld.s, s.c_ld.n)
                t.wait_ge(s.c_k.s, s.c_k.n)
                for i in range(DFF // 128):
                    s.c_tp.inc(t.transpose(s.p_a[:, i:i + 1], brow2[0:1, ts(i, 128)],
                                           s.ident[0:1, 0:1]))

            @blk.scalar
            def _(sc):
                sc.wait_ge(s.c_tp.s, s.c_tp.n)
                s.c_ev.inc(sc.activation(s.b1_sb[:, :], s.p_a[:, 0:DFF // 128], AF.Copy))

            @blk.sync
            def _(sync):
                sync.wait_ge(s.c_ev.s, s.c_ev.n)
                s.c_ld.inc(sync.dma_start(out=brow2[0:1, 0:DM], in_=s.b2_in.ap()), 16)

            @blk.tensor
            def _(t):
                t.wait_ge(s.c_ld.s, s.c_ld.n)
                t.wait_ge(s.c_ev.s, s.c_ev.n)
                for i in range(DM // 128):
                    s.c_tp.inc(t.transpose(s.p_a[:, i:i + 1], brow2[0:1, ts(i, 128)],
                                           s.ident[0:1, 0:1]))

            @blk.scalar
            def _(sc):
                sc.wait_ge(s.c_tp.s, s.c_tp.n)
                s.c_ev.inc(sc.activation(s.b2_sb[:, :], s.p_a[:, 0:DM // 128], AF.Copy))
        fctx.close()

    def emit_f1_ag(self):
        s = self
        if (blk := self.blk) or True:
            @blk.gpsimd
            def _(g):
                g.wait_ge(s.c_out.s, s.c_out.n)
                s.c_ag.inc(g.collective_compute(
                    "AllGather", AL.bypass, replica_groups=[list(range(8))],
                    ins=[s.x1_out.ap().opt()], outs=[s.ag_x1.ap().opt()]))
                s.c_ag.inc(g.collective_compute(
                    "AllGather", AL.bypass, replica_groups=[list(range(8))],
                    ins=[s.scT_dram.ap().opt()], outs=[s.ag_scT.ap().opt()]))

    def emit_f2_routing(self):
        s, nc = self, self.nc
        L, TOK, K = self.L, self.TOK, self.K
        L4 = TOK // 32
        fctx = ExitStack()
        s16 = self.sbuf(fctx, "s16", [32, L4])
        gtm = self.sbuf(fctx, "gtm", [32, L4])
        eqm = self.sbuf(fctx, "eqm", [32, L4])
        cum = self.sbuf(fctx, "cum", [32, L4])
        sel = self.sbuf(fctx, "sel", [32, L4])
        one6 = self.sbuf(fctx, "one6", [32, L4])
        o66 = self.sbuf(fctx, "o66", [32, 32])
        sc9 = self.sbuf(fctx, "sc9", [32, 16])
        rowb = self.sbuf(fctx, "rowb", [1, 128])
        offs_u = self.sbuf(fctx, "offs_u", [1, 32], U32)
        iot = self.sbuf(fctx, "iot", [32, L4], I16)
        sl16 = self.sbuf(fctx, "sl16", [32, L4], I16)
        dfv = self.sbuf(fctx, "dfv", [32, L4])
        glo = self.sbuf(fctx, "glo", [32, L4], I16)
        ghi = self.sbuf(fctx, "ghi", [32, L4], I16)
        icomp = self.sbuf(fctx, "icomp", [32, L4], I16)
        gcl = self.sbuf(fctx, "gcl", [32, L4], I16)
        gch = self.sbuf(fctx, "gch", [32, L4], I16)

        LO, HI, MID, CNT, CNTT, CND, CNDL, T0, T1, NEQ, EQC, CGT = range(12)
        bis0, gr0, cmp0, mg0 = s.c_bis.n, s.c_gr.n, s.c_cmp.n, s.c_mrg.n
        NIT = 42

        if (blk := self.blk) or True:
            @blk.gpsimd
            def _(g):
                g.wait_ge(s.c_ag.s, s.c_ag.n)
                for r in range(32):
                    rr = g.value_load(s.scrow_sb[0:1, r:r + 1], min_val=0, max_val=63)
                    s.c_sdm.inc(g.dma_start(
                        out=s16[r:r + 1, :],
                        in_=s.ag_scT.ap()[bass.ds(rr, 1), ts(r % 4, L4)]), 16)

            @blk.vector
            def _(v):
                def dv(ii):
                    s.c_dv.inc(ii)
                    v.wait_ge(s.c_dv.s, s.c_dv.n)

                v.wait_ge(s.c_sdm.s, s.c_sdm.n)
                dv(v.memset(one6[:, :], 1.0))
                dv(v.memset(o66[:, :], 1.0))
                dv(v.memset(sc9[:, LO:LO + 1], 0.0))
                dv(v.memset(sc9[:, HI:HI + 1], 1.0))
                for it in range(NIT):
                    dv(v.tensor_tensor(sc9[:, MID:MID + 1], sc9[:, LO:LO + 1],
                                       sc9[:, HI:HI + 1], AL.add))
                    dv(v.tensor_scalar_mul(sc9[:, MID:MID + 1], sc9[:, MID:MID + 1], 0.5))
                    s.c_bis.inc(v.tensor_scalar(gtm[:, :], s16[:, :], sc9[:, MID:MID + 1],
                                                0.0, AL.is_ge, AL.add,
                                                accum_out=sc9[:, CNT:CNT + 1]))
                    v.wait_ge(s.c_cmp.s, cmp0 + it + 1)
                    dv(v.tensor_scalar(sc9[:, CND:CND + 1], sc9[:, CNTT:CNTT + 1],
                                       float(K), None, AL.is_ge))
                    dv(v.tensor_scalar(sc9[:, CNDL:CNDL + 1], sc9[:, CNTT:CNTT + 1],
                                       float(K), None, AL.is_lt))
                    dv(v.tensor_tensor(sc9[:, T0:T0 + 1], sc9[:, CND:CND + 1],
                                       sc9[:, MID:MID + 1], AL.mult))
                    dv(v.tensor_tensor(sc9[:, T1:T1 + 1], sc9[:, CNDL:CNDL + 1],
                                       sc9[:, LO:LO + 1], AL.mult))
                    dv(v.tensor_tensor(sc9[:, LO:LO + 1], sc9[:, T0:T0 + 1],
                                       sc9[:, T1:T1 + 1], AL.add))
                    dv(v.tensor_tensor(sc9[:, T0:T0 + 1], sc9[:, CNDL:CNDL + 1],
                                       sc9[:, MID:MID + 1], AL.mult))
                    dv(v.tensor_tensor(sc9[:, T1:T1 + 1], sc9[:, CND:CND + 1],
                                       sc9[:, HI:HI + 1], AL.mult))
                    dv(v.tensor_tensor(sc9[:, HI:HI + 1], sc9[:, T0:T0 + 1],
                                       sc9[:, T1:T1 + 1], AL.add))
                s.c_bis.inc(v.tensor_scalar(gtm[:, :], s16[:, :], sc9[:, HI:HI + 1],
                                            0.0, AL.is_ge, AL.add,
                                            accum_out=sc9[:, CGT:CGT + 1]))
                s.c_bis.inc(v.tensor_scalar(eqm[:, :], s16[:, :], sc9[:, LO:LO + 1],
                                            0.0, AL.is_equal, AL.add,
                                            accum_out=sc9[:, EQC:EQC + 1]))

                v.wait_ge(s.c_cmp.s, cmp0 + NIT + 2)
                dv(v.tensor_tensor_scan(rowb[0:1, 32:64], s.ones_sb[0:1, 0:32],
                                        rowb[0:1, 0:32], 0.0, AL.mult, AL.add))
                dv(v.tensor_tensor(rowb[0:1, 32:64], rowb[0:1, 32:64],
                                   rowb[0:1, 0:32], AL.subtract))
                s.c_bis.inc(v.engine_nop())

                v.wait_ge(s.c_cmp.s, cmp0 + NIT + 3)
                dv(v.tensor_scalar_mul(sc9[:, NEQ:NEQ + 1], sc9[:, CNTT:CNTT + 1], -1.0))
                dv(v.tensor_scalar_add(sc9[:, NEQ:NEQ + 1], sc9[:, NEQ:NEQ + 1], float(K)))
                dv(v.tensor_tensor_scan(cum[:, :], one6[:, :], eqm[:, :], 0.0,
                                        AL.mult, AL.add))
                dv(v.scalar_tensor_tensor(dfv[:, :], cum[:, :], sc9[:, T0:T0 + 1],
                                          eqm[:, :], AL.add, AL.subtract))
                dv(v.tensor_scalar(dfv[:, :], dfv[:, :], sc9[:, NEQ:NEQ + 1],
                                   None, AL.is_lt))
                dv(v.tensor_tensor(sel[:, :], dfv[:, :], eqm[:, :], AL.mult))
                dv(v.tensor_tensor(sel[:, :], sel[:, :], gtm[:, :], AL.add))
                dv(v.tensor_tensor_scan(cum[:, :], one6[:, :], sel[:, :], 0.0,
                                        AL.mult, AL.add))
                dv(v.tensor_tensor(dfv[:, :], sel[:, :], cum[:, :], AL.mult))
                dv(v.tensor_scalar_add(dfv[:, :], dfv[:, :], -1.0))
                dv(v.tensor_copy(sl16[:, :], dfv[:, :]))
                dv(v.tensor_copy(glo[:, :], s16[:, :].bitcast(I16).rearrange(
                    "p (j two) -> p two j", two=2)[:, 0, :]))
                dv(v.tensor_copy(ghi[:, :], s16[:, :].bitcast(I16).rearrange(
                    "p (j two) -> p two j", two=2)[:, 1, :]))
                dv(v.tensor_copy(sc9[:, T1:T1 + 1], cum[:, L4 - 1:L4]))
                s.c_bis.inc(v.engine_nop())

                v.wait_ge(s.c_cmp.s, cmp0 + NIT + 4)
                dv(v.tensor_tensor_scan(rowb[0:1, 96:128], s.ones_sb[0:1, 0:32],
                                        rowb[0:1, 64:96], 0.0, AL.mult, AL.add))
                dv(v.tensor_tensor(rowb[0:1, 96:128], rowb[0:1, 96:128],
                                   rowb[0:1, 64:96], AL.subtract))
                dv(v.tensor_copy(offs_u[0:1, 0:32], rowb[0:1, 96:128]))
                s.c_bis.inc(v.engine_nop())

            @blk.tensor
            def _(t):
                t.wait_ge(s.c_k.s, s.c_k.n)
                for it in range(NIT + 1):
                    t.wait_ge(s.c_bis.s, bis0 + it + 1)
                    if it >= 1:
                        t.wait_ge(s.c_cmp.s, cmp0 + it)
                    s.c_gr.inc(t.matmul(s.p_a[0:32, 0:1], o66[:, 0:32],
                                        sc9[:, CNT:CNT + 1] if it < NIT
                                        else sc9[:, CGT:CGT + 1],
                                        start=True, stop=True))
                t.wait_ge(s.c_bis.s, bis0 + NIT + 2)
                t.wait_ge(s.c_cmp.s, cmp0 + NIT + 1)
                s.c_gr.inc(t.transpose(s.p_a[0:1, 2:34], sc9[:, EQC:EQC + 1],
                                       s.ident[0:32, 0:32]))
                t.wait_ge(s.c_bis.s, bis0 + NIT + 3)
                t.wait_ge(s.c_cmp.s, cmp0 + NIT + 2)
                s.c_gr.inc(t.transpose(s.p_a[0:32, 40:41], rowb[0:1, 32:64],
                                       s.ident[0:1, 0:1]))
                t.wait_ge(s.c_bis.s, bis0 + NIT + 4)
                t.wait_ge(s.c_cmp.s, cmp0 + NIT + 3)
                s.c_gr.inc(t.transpose(s.p_a[0:1, 48:80], sc9[:, T1:T1 + 1],
                                       s.ident[0:32, 0:32]))

            @blk.scalar
            def _(sc):
                for it in range(NIT + 1):
                    sc.wait_ge(s.c_gr.s, gr0 + it + 1)
                    s.c_cmp.inc(sc.activation(sc9[:, CNTT:CNTT + 1], s.p_a[0:32, 0:1],
                                              AF.Copy))
                sc.wait_ge(s.c_gr.s, gr0 + NIT + 2)
                s.c_cmp.inc(sc.activation(rowb[0:1, 0:32], s.p_a[0:1, 2:34], AF.Copy))
                sc.wait_ge(s.c_gr.s, gr0 + NIT + 3)
                s.c_cmp.inc(sc.activation(sc9[:, T0:T0 + 1], s.p_a[0:32, 40:41],
                                          AF.Copy))
                sc.wait_ge(s.c_gr.s, gr0 + NIT + 4)
                s.c_cmp.inc(sc.activation(rowb[0:1, 64:96], s.p_a[0:1, 48:80], AF.Copy))

            @blk.gpsimd
            def _(g):
                g.wait_ge(s.c_bis.s, bis0 + NIT + 5)
                s.c_gp.inc(g.iota(iot[:, :], [[1, L4]], base=0, channel_multiplier=L4,
                                  allow_small_or_imprecise_dtypes=True))
                g.wait_ge(s.c_gp.s, s.c_gp.n)
                s.c_gp.inc(g.local_scatter(icomp[:, :], iot[:, :], sl16[:, :],
                                           channels=32, num_elems=L4, num_idxs=L4))
                g.wait_ge(s.c_gp.s, s.c_gp.n)
                s.c_gp.inc(g.local_scatter(gcl[:, :], glo[:, :], sl16[:, :],
                                           channels=32, num_elems=L4, num_idxs=L4))
                g.wait_ge(s.c_gp.s, s.c_gp.n)
                s.c_gp.inc(g.local_scatter(gch[:, :], ghi[:, :], sl16[:, :],
                                           channels=32, num_elems=L4, num_idxs=L4))
                g.wait_ge(s.c_gp.s, s.c_gp.n)
                mgi0, mgl0, mgh0 = s.c_mrg.n, s.c_sd0.n, s.c_sd1.n
                for r in range(32):
                    off = g.value_load(offs_u[0:1, r:r + 1], min_val=0, max_val=K)
                    if r >= 1:
                        g.wait_ge(s.c_mrg.s, mgi0 + 16 * r)
                    s.c_mrg.inc(g.dma_start(out=s.idxp[0:1, bass.ds(off, L4)],
                                            in_=icomp[r:r + 1, :]), 16)
                    if r >= 1:
                        g.wait_ge(s.c_sd0.s, mgl0 + 16 * r)
                    s.c_sd0.inc(g.dma_start(out=s.glo_dram[0:1, bass.ds(off, L4)],
                                            in_=gcl[r:r + 1, :]), 16)
                    if r >= 1:
                        g.wait_ge(s.c_sd1.s, mgh0 + 16 * r)
                    s.c_sd1.inc(g.dma_start(out=s.ghi_dram[0:1, bass.ds(off, L4)],
                                            in_=gch[r:r + 1, :]), 16)
                g.wait_ge(s.c_mrg.s, s.c_mrg.n)
                g.wait_ge(s.c_sd0.s, s.c_sd0.n)
                g.wait_ge(s.c_sd1.s, s.c_sd1.n)
                s.c_mrg.inc(g.dma_start(out=s.idx_list[0:1, :],
                                        in_=s.idxp[0:1, 0:K]), 16)
                g.wait_ge(s.c_mrg.s, s.c_mrg.n)
                s.c_ag.inc(g.collective_compute(
                    "AllGather", AL.bypass, replica_groups=[list(range(8))],
                    ins=[s.idx_list.ap().opt()], outs=[s.ag_idx.ap().opt()]))
        fctx.close()

    def emit_f3_expert(self):
        s, nc = self, self.nc
        K, DFF = self.K, self.DFF
        K16, KBK = K // 16, K // 128
        NFT, NHT = DFF // 128, DM // 128
        NKB, RS = self.NKB, self.RS

        fA = ExitStack()
        th = self.sbuf(fA, "th", [128, NHT, K], F16)
        tl = self.sbuf(fA, "tl", [128, NHT, K], F16)
        fB = ExitStack()
        toks = self.sbuf(fB, "toks", [128, KBK, DM])
        gidx = self.sbuf(fB, "gidx", [128, KBK], I16)
        gidx32 = self.sbuf(fB, "gidx32", [128, KBK], U32)

        F3P = self.cfg.get("F3P", 9)
        gth0, tp0, xch0, xcl0 = s.c_gth.n, s.c_tp.n, s.c_xch.n, s.c_xcl.n
        gat0 = s.c_gat.n
        if (blk := self.blk) or True:
            @blk.sync
            def _(sync):
                sync.wait_ge(s.c_ag.s, s.c_ag.n)
                s.c_gth.inc(sync.dma_start(
                    out=gidx[:, :],
                    in_=s.idx_list.ap().rearrange("o (kt p) -> (o p) kt", p=128),
                    allow_slow_non_contiguous=True), 16)

            @blk.vector
            def _(v):
                v.wait_ge(s.c_gth.s, gth0 + 16)
                s.c_gt1.inc(v.tensor_copy(gidx32[:, :], gidx[:, :]))

            if F3P < 1:
                fB.close()
                fA.close()
                return

            @blk.gpsimd
            def _(g):
                g.wait_ge(s.c_gt1.s, s.c_gt1.n)
                for kt in range(KBK):
                    s.c_gat.inc(g.indirect_dma_start(
                        out=toks[:, kt, :], out_offset=None,
                        in_=s.ag_x1[:, :],
                        in_offset=IndirectOffsetOnAxis(gidx32[:, kt:kt + 1], 0)), 16)

            if F3P < 2:
                fB.close()
                fA.close()
                return

            @blk.tensor
            def _(t):
                t.wait_ge(s.c_gat.s, s.c_gat.n)
                t.wait_ge(s.c_k.s, s.c_k.n)
                u = 0
                for kt in range(KBK):
                    for hg in range(2):
                        pb = s.p_g if u % 2 == 0 else s.p_h
                        if u >= 2:
                            t.wait_ge(s.c_xcl.s, xcl0 + (u - 2) + 1)
                        for q in range(4):
                            ht = hg * 4 + q
                            s.c_tp.inc(t.transpose(pb[:, ts(q, 128)],
                                                   toks[:, kt, ts(ht, 128)],
                                                   s.ident[:, :]))
                        u += 1

            @blk.scalar
            def _(sc):
                for u in range(KBK * 2):
                    kt, hg = u // 2, u % 2
                    pb = s.p_g if u % 2 == 0 else s.p_h
                    sc.wait_ge(s.c_tp.s, tp0 + 4 * (u + 1))
                    s.c_xch.inc(sc.activation(
                        th[:, hg * 4:hg * 4 + 4, kt * 128:(kt + 1) * 128],
                        pb[:, :], AF.Copy))

            @blk.vector
            def _(v):
                for u in range(KBK * 2):
                    kt, hg = u // 2, u % 2
                    pb = s.p_g if u % 2 == 0 else s.p_h
                    v.wait_ge(s.c_xch.s, xch0 + u + 1)
                    s.c_xcl.inc(v.tensor_tensor(
                        tl[:, hg * 4:hg * 4 + 4, kt * 128:(kt + 1) * 128],
                        pb[:, :], th[:, hg * 4:hg * 4 + 4, kt * 128:(kt + 1) * 128],
                        AL.subtract))

            if F3P < 3:
                fB.close()
                fA.close()
                return

            @blk.sync
            def _(sync):
                sync.wait_ge(s.c_xch.s, s.c_xch.n)
                sync.wait_ge(s.c_xcl.s, s.c_xcl.n)
                s.c_wlo1.inc(sync.dma_start(out=s.th_dram.ap(), in_=th.ap()), 16)
                s.c_wlo1.inc(sync.dma_start(out=s.tl_dram.ap(), in_=tl.ap()), 16)
        fB.close()
        fA.close()
        self.spill_mark = s.c_wlo1.n
        if F3P < 4:
            return

        fC = ExitStack()
        thk = self.sbuf(fC, "thk", [128, 2, NHT, RS], F16)
        tlk = self.sbuf(fC, "tlk", [128, 2, NHT, RS], F16)
        h1h = self.sbuf(fC, "h1h", [128, NFT, 512], F16)
        h1l = self.sbuf(fC, "h1l", [128, NFT, 512], F16)
        w1s = self.sbuf(fC, "w1s", [128, 2, NHT, 256], F16)
        w2s = self.sbuf(fC, "w2s", [128, 2, 2, 512], F16)
        h1f = self.sbuf(fC, "h1f", [128, 2, 512])
        eos = self.sbuf(fC, "eos", [128, 8, 512])
        self.zer512 = self.sbuf(fC, "zer512", [128, 512])
        grep = self.sbuf(fC, "grep", [128, NKB, 512])
        gsb = self.sbuf(fC, "gsb", [1, K])
        gsl = self.sbuf(fC, "gsl", [1, K], I16)
        gsh = self.sbuf(fC, "gsh", [1, K], I16)
        ctT = self.sbuf(fC, "ctT", [128, max(1, self.RS // 128), DM])

        gr0, fev0 = s.c_gr.n, s.c_fev.n
        if (blk := self.blk) or True:
            @blk.sync
            def _(sync):
                sync.wait_ge(s.c_wlo1.s, self.spill_mark)
                s.c_wlo0.inc(sync.dma_start(out=gsl[0:1, :], in_=s.glo_dram.ap()[0:1, 0:K]), 16)
                s.c_wlo0.inc(sync.dma_start(out=gsh[0:1, :], in_=s.ghi_dram.ap()[0:1, 0:K]), 16)

            @blk.vector
            def _(v):
                v.wait_ge(s.c_wlo0.s, s.c_wlo0.n)
                s.c_fts.inc(v.tensor_copy(gsb[0:1, :].bitcast(I16).rearrange(
                    "p (j two) -> p two j", two=2)[:, 0, :], gsl[0:1, :]))
                v.wait_ge(s.c_fts.s, s.c_fts.n)
                s.c_fts.inc(v.tensor_copy(gsb[0:1, :].bitcast(I16).rearrange(
                    "p (j two) -> p two j", two=2)[:, 1, :], gsh[0:1, :]))
                v.wait_ge(s.c_fts.s, s.c_fts.n)
                s.c_fts.inc(v.memset(s.zer512[:, :], 0.0))
        self.fts_mark = s.c_fts.n
        if F3P < 5:
            fC.close()
            return

        if (blk := self.blk) or True:
            @blk.tensor
            def _(t):
                t.wait_ge(s.c_fts.s, s.fts_mark)
                t.wait_ge(s.c_xcl.s, s.c_xcl.n)
                for kb in range(NKB):
                    if kb >= 1:
                        t.wait_ge(s.c_fev.s, fev0 + kb)
                    s.c_gr.inc(t.matmul(s.p_g[:, 0:RS], s.ones_sb[0:1, 0:128],
                                        gsb[0:1, ts(kb, RS)], start=True, stop=True))

            @blk.scalar
            def _(sc):
                for kb in range(NKB):
                    sc.wait_ge(s.c_gr.s, gr0 + kb + 1)
                    s.c_fev.inc(sc.activation(grep[:, kb, 0:RS], s.p_g[:, 0:RS], AF.Copy))

        self.h1m_marks = []
        nkb_run = NKB if F3P >= 6 else 1
        for kb in range(nkb_run):
            self._f3_kb(kb, thk, tlk, h1h, h1l, w1s, w2s, h1f, eos, grep, ctT)
        if F3P < 7:
            fC.close()
            return

        if (blk := self.blk) or True:
            @blk.gpsimd
            def _(g):
                g.wait_ge(s.c_sca.s, s.c_sca.n)
                s.c_ag.inc(g.collective_compute(
                    "AllGather", AL.bypass, replica_groups=[list(range(8))],
                    ins=[s.contrib.ap().opt()], outs=[s.ag_contrib.ap().opt()]))
        fC.close()

    def _f3_kb(self, kb, thk, tlk, h1h, h1l, w1s, w2s, h1f, eos, grep, ctT):
        s, nc = self, self.nc
        K, DFF = self.K, self.DFF
        NFT, NHT = DFF // 128, DM // 128
        RS = self.RS
        w10, w20 = [s.c_w1s0, s.c_w1s1], [s.c_w2s0, s.c_w2s1]
        h1m0, h1r0, h1c0, xcl0 = s.c_h1m.n, s.c_h1r.n, s.c_h1c.n, s.c_xcl.n
        eo0, fev0, ct0 = s.c_eo.n, s.c_fev.n, s.c_ct.n
        blk = self.blk
        w1vals = []
        tkvals = []

        @blk.sync
        def _(sync):
            ctk = s.c_xin0 if kb % 2 == 0 else s.c_xin1
            if kb == 0:
                sync.wait_ge(s.c_wlo1.s, s.spill_mark)
            if kb >= 2:
                sync.wait_ge(s.c_h1m.s, s.h1m_marks[kb - 2])
            ctk.inc(sync.dma_start(out=thk[:, kb % 2, :, :],
                                   in_=s.th_dram.ap()[:, :, ts(kb, RS)]), 16)
            ctk.inc(sync.dma_start(out=tlk[:, kb % 2, :, :],
                                   in_=s.tl_dram.ap()[:, :, ts(kb, RS)]), 16)
            tkvals.append((ctk, ctk.n))
            for ft in range(NFT):
                cw = w10[ft % 2]
                if ft >= 2:
                    sync.wait_ge(s.c_h1m.s, h1m0 + 24 * (ft - 1))
                cw.inc(sync.dma_start(out=w1s[:, :, :, ts(ft % 2, 128)],
                                      in_=s.w116[:, :, :, ts(ft, 128)]), 16)
                w1vals.append((cw, cw.n))

        @blk.tensor
        def _(t):
            t.wait_ge(tkvals[0][0].s, tkvals[0][1])
            for ft in range(NFT):
                pb = s.p_a if ft % 2 == 0 else s.p_b
                t.wait_ge(w1vals[ft][0].s, w1vals[ft][1])
                if ft >= 2:
                    t.wait_ge(s.c_h1r.s, h1r0 + (ft - 2) + 1)
                for ht in range(NHT):
                    wh = w1s[:, 0, ht, ts(ft % 2, 128)]
                    wl = w1s[:, 1, ht, ts(ft % 2, 128)]
                    s.c_h1m.inc(t.matmul(pb[:, 0:RS], wh, thk[:, kb % 2, ht, 0:RS],
                                         start=(ht == 0), stop=False))
                    s.c_h1m.inc(t.matmul(pb[:, 0:RS], wh, tlk[:, kb % 2, ht, 0:RS],
                                         start=False, stop=False))
                    s.c_h1m.inc(t.matmul(pb[:, 0:RS], wl, thk[:, kb % 2, ht, 0:RS],
                                         start=False, stop=(ht == NHT - 1)))
        s.h1m_marks.append(s.c_h1m.n)

        @blk.vector
        def _(v):
            v.wait_ge(s.c_fts.s, s.fts_mark)
            if kb >= 1:
                v.wait_ge(s.c_eo.s, eo0)
            for ft in range(NFT):
                pb = s.p_a if ft % 2 == 0 else s.p_b
                v.wait_ge(s.c_h1m.s, h1m0 + 24 * (ft + 1))
                if ft >= 2:
                    v.wait_ge(s.c_h1c.s, h1c0 + (ft - 2) + 1)
                    v.wait_ge(s.c_xcl.s, xcl0 + (ft - 2) + 1)
                s.c_h1r.inc(v.scalar_tensor_tensor(h1f[:, ft % 2, 0:RS], pb[:, 0:RS],
                                                   s.b1_sb[:, ft:ft + 1],
                                                   s.zer512[:, 0:RS], AL.add, AL.max))
                if ft >= 1:
                    v.wait_ge(s.c_h1c.s, h1c0 + ft)
                    s.c_xcl.inc(v.tensor_tensor(h1l[:, ft - 1, 0:RS],
                                                h1f[:, (ft - 1) % 2, 0:RS],
                                                h1h[:, ft - 1, 0:RS], AL.subtract))
            v.wait_ge(s.c_h1c.s, h1c0 + NFT)
            s.c_xcl.inc(v.tensor_tensor(h1l[:, NFT - 1, 0:RS],
                                        h1f[:, (NFT - 1) % 2, 0:RS],
                                        h1h[:, NFT - 1, 0:RS], AL.subtract))

        @blk.scalar
        def _(sc):
            for ft in range(NFT):
                sc.wait_ge(s.c_h1r.s, h1r0 + ft + 1)
                if kb >= 1:
                    sc.wait_ge(s.c_eo.s, eo0)
                s.c_h1c.inc(sc.activation(h1h[:, ft, 0:RS], h1f[:, ft % 2, 0:RS],
                                          AF.Copy))

        w2vals = []

        @blk.sync
        def _(sync):
            for mg in range(2):
                for ft in range(NFT):
                    u = mg * NFT + ft
                    cw = w20[u % 2]
                    if u >= 2:
                        sync.wait_ge(s.c_eo.s, eo0 + 12 * (u - 1))
                    cw.inc(sync.dma_start(out=w2s[:, :, u % 2, :],
                                          in_=s.w216[:, :, ft, ts(mg, 512)]), 16)
                    w2vals.append((cw, cw.n))

        @blk.tensor
        def _(t):
            banks = [s.p_c, s.p_d, s.p_e, s.p_f]
            for mg in range(2):
                for ft in range(NFT):
                    u = mg * NFT + ft
                    t.wait_ge(w2vals[u][0].s, w2vals[u][1])
                    t.wait_ge(s.c_h1c.s, h1c0 + ft + 1)
                    t.wait_ge(s.c_xcl.s, xcl0 + ft + 1)
                    if ft == 0:
                        t.wait_ge(s.c_fev.s, fev0 + 4 if mg > 0 else fev0)
                    for m in range(4):
                        wh = w2s[:, 0, u % 2, ts(m, 128)]
                        wl = w2s[:, 1, u % 2, ts(m, 128)]
                        s.c_eo.inc(t.matmul(banks[m][:, 0:RS], wh, h1h[:, ft, 0:RS],
                                            start=(ft == 0), stop=False))
                        s.c_eo.inc(t.matmul(banks[m][:, 0:RS], wh, h1l[:, ft, 0:RS],
                                            start=False, stop=False))
                        s.c_eo.inc(t.matmul(banks[m][:, 0:RS], wl, h1h[:, ft, 0:RS],
                                            start=False, stop=(ft == NFT - 1)))

        sct_pre = s.c_sct.n
        sca_pre = s.c_sca.n

        @blk.scalar
        def _(sc):
            banks = [s.p_c, s.p_d, s.p_e, s.p_f]
            for mg in range(2):
                sc.wait_ge(s.c_eo.s, eo0 + 12 * NFT * (mg + 1))
                sc.wait_ge(s.c_ct.s, ct0)
                if kb >= 1:
                    sc.wait_ge(s.c_sct.s, sct_pre)
                for m in range(4):
                    mc = mg * 4 + m
                    s.c_fev.inc(sc.activation(eos[:, mc, 0:RS], banks[m][:, 0:RS],
                                              AF.Identity, bias=s.b2_sb[:, mc:mc + 1]))

        gmul0 = s.c_ct.n

        @blk.vector
        def _(v):
            for mg in range(2):
                for m in range(4):
                    v.wait_ge(s.c_fev.s, fev0 + 4 * mg + m + 1)
                    s.c_ct.inc(v.tensor_tensor(eos[:, mg * 4 + m, 0:RS],
                                               eos[:, mg * 4 + m, 0:RS],
                                               grep[:, kb, 0:RS], AL.mult))

        tpx0 = s.c_tp.n
        ctv1 = s.c_sct.n

        @blk.tensor
        def _(t):
            for mg in range(2):
                for ktl in range(RS // 128):
                    u = mg * (RS // 128) + ktl
                    pb = s.p_g if u % 2 == 0 else s.p_h
                    t.wait_ge(s.c_ct.s, gmul0 + 4 * (mg + 1))
                    if u >= 2:
                        t.wait_ge(s.c_sct.s, ctv1 + (u - 2) + 1)
                    else:
                        t.wait_ge(s.c_sct.s, max(0, ctv1 - (2 - u)))
                    for m in range(4):
                        s.c_tp.inc(t.transpose(pb[:, ts(m, 128)],
                                               eos[:, mg * 4 + m, ts(ktl, 128)],
                                               s.ident[:, :]))

        @blk.scalar
        def _(sc):
            for mg in range(2):
                for ktl in range(RS // 128):
                    u = mg * (RS // 128) + ktl
                    pb = s.p_g if u % 2 == 0 else s.p_h
                    sc.wait_ge(s.c_tp.s, tpx0 + 4 * (u + 1))
                    if kb >= 1:
                        sc.wait_ge(s.c_sca.s, sca_pre)
                    s.c_sct.inc(sc.activation(ctT[:, ktl, ts(mg, 512)],
                                              pb[:, :], AF.Copy))

        @blk.sync
        def _(sync):
            sync.wait_ge(s.c_sct.s, s.c_sct.n)
            s.c_sca.inc(sync.dma_start(
                out=s.contrib.ap()[kb * RS:(kb + 1) * RS, :].rearrange(
                    "(blk p) h -> p blk h", p=128),
                in_=ctT[:, 0:RS // 128, :]), 16)

    def emit_f4_scatter(self):
        s, nc = self, self.nc
        K, TOK, L = self.K, self.TOK, self.L
        KBK = K // 128
        fctx = ExitStack()
        sidx = self.sbuf(fctx, "sidx", [128, 2, KBK], I16)
        sidx32 = self.sbuf(fctx, "sidx32", [128, 2, KBK], U32)
        crows = self.sbuf(fctx, "crows", [128, 2, DM])
        gth0, pul0, gt20, gat0 = s.c_gth.n, s.c_pul.n, s.c_gt2.n, s.c_gat.n
        ldvals = []
        sxvals = []
        KBKt = KBK * 8
        scvals = [(s.c_pul if rg % 2 == 0 else s.c_gat,
                   (pul0 if rg % 2 == 0 else gat0) + 16 * (rg // 2 + 1))
                  for rg in range(KBKt)]
        if (blk := self.blk) or True:
            @blk.sync
            def _(sync):
                sync.wait_ge(s.c_ag.s, s.c_ag.n)
                for e in range(8):
                    cs = s.c_xin0 if e % 2 == 0 else s.c_xin1
                    if e >= 2:
                        sync.wait_ge(s.c_gt2.s, gt20 + (e - 2) + 1)
                    cs.inc(sync.dma_start(
                        out=sidx[:, e % 2, :],
                        in_=s.ag_idx.ap()[e:e + 1, :].rearrange(
                            "o (kt p) -> (o p) kt", p=128),
                        allow_slow_non_contiguous=True), 16)
                    sxvals.append((cs, cs.n))
                    for kt in range(KBK):
                        rg = e * KBK + kt
                        cl = s.c_bv0 if rg % 2 == 0 else s.c_bv1
                        if rg >= 2:
                            sync.wait_ge(scvals[rg - 2][0].s, scvals[rg - 2][1])
                        r0 = e * K + kt * 128
                        cl.inc(sync.dma_start(out=crows[:, rg % 2, :],
                                              in_=s.ag_contrib.ap()[r0:r0 + 128, :]), 16)
                        ldvals.append((cl, cl.n))

            @blk.vector
            def _(v):
                for e in range(8):
                    v.wait_ge(sxvals[e][0].s, sxvals[e][1])
                    if e >= 2:
                        ee = (e - 1) * KBK - 1
                        v.wait_ge(scvals[ee][0].s, scvals[ee][1])
                        if ee >= 1:
                            v.wait_ge(scvals[ee - 1][0].s, scvals[ee - 1][1])
                    s.c_gt2.inc(v.tensor_copy(sidx32[:, e % 2, :], sidx[:, e % 2, :]))

            @blk.gpsimd
            def _(g):
                g.wait_ge(s.c_out.s, s.c_out.n)
                yb = g.value_load(s.scrow_sb[0:1, 32:33], min_val=0, max_val=TOK - L)
                sdm_seed = s.c_sdm.n + 16
                s.c_sdm.inc(g.dma_start(out=s.yfull.ap()[bass.ds(yb, L), :],
                                        in_=s.x1_out.ap()), 16)
                g.wait_ge(s.c_sdm.s, sdm_seed)
                for e in range(8):
                    g.wait_ge(s.c_gt2.s, gt20 + e + 1)
                    if e > 0:
                        rr = e * KBK - 1
                        g.wait_ge(scvals[rr][0].s, scvals[rr][1])
                        if rr >= 1:
                            g.wait_ge(scvals[rr - 1][0].s, scvals[rr - 1][1])
                    for kt in range(KBK):
                        rg = e * KBK + kt
                        cp_ = s.c_pul if rg % 2 == 0 else s.c_gat
                        g.wait_ge(ldvals[rg][0].s, ldvals[rg][1])
                        cp_.inc(g.indirect_dma_start(
                            out=s.yfull[:, :],
                            out_offset=IndirectOffsetOnAxis(
                                sidx32[:, e % 2, kt:kt + 1], 0),
                            in_=crows[:, rg % 2, :], in_offset=None,
                            compute_op=AL.add), 16)
        fctx.close()

    def emit_f5_ln2(self):
        s, nc = self, self.nc
        L = self.L
        fctx = ExitStack()
        ybuf = self.sbuf(fctx, "ybuf", [128, 2, DM])
        lnq2 = self.sbuf(fctx, "lnq2", [128, DM])
        lnc2 = self.sbuf(fctx, "lnc2", [128, 2, 8])
        NT = L // 128
        ln10, ln20, ln30, sfx0, out0 = (s.c_ln1.n, s.c_ln2.n, s.c_ln3.n,
                                        s.c_sfx.n, s.c_out.n)
        ybvals = []
        if (blk := self.blk) or True:
            @blk.gpsimd
            def _(g):
                g.wait_ge(s.c_pul.s, s.c_pul.n)
                g.wait_ge(s.c_gat.s, s.c_gat.n)
                for j in range(NT):
                    cy = s.c_sd0 if j % 2 == 0 else s.c_sd1
                    if j >= 2:
                        g.wait_ge(s.c_out.s, out0 + 16 * (j - 1))
                    yb = g.value_load(s.scrow_sb[0:1, 33 + j:34 + j], min_val=0,
                                      max_val=self.TOK - 128)
                    cy.inc(g.dma_start(out=ybuf[:, j % 2, :],
                                       in_=s.yfull.ap()[bass.ds(yb, 128), :]), 16)
                    ybvals.append((cy, cy.n))

            @blk.vector
            def _(v):
                def dv(ii):
                    s.c_dv.inc(ii)
                    v.wait_ge(s.c_dv.s, s.c_dv.n)

                def stats(j):
                    sl = j % 2
                    v.wait_ge(ybvals[j][0].s, ybvals[j][1])
                    dv(v.tensor_tensor(lnq2[:, :], ybuf[:, sl, :], ybuf[:, sl, :], AL.mult))
                    dv(v.reduce_sum(lnc2[:, sl, 1:2], lnq2[:, :], axis=mybir.AxisListType.X))
                    dv(v.reduce_sum(lnc2[:, sl, 0:1], ybuf[:, sl, :], axis=mybir.AxisListType.X))
                    dv(v.tensor_scalar_mul(lnc2[:, sl, 2:3], lnc2[:, sl, 0:1], 1.0 / DM))
                    dv(v.tensor_scalar_mul(lnc2[:, sl, 3:4], lnc2[:, sl, 1:2], 1.0 / DM))
                    dv(v.tensor_tensor(lnc2[:, sl, 4:5], lnc2[:, sl, 2:3], lnc2[:, sl, 2:3], AL.mult))
                    dv(v.tensor_tensor(lnc2[:, sl, 4:5], lnc2[:, sl, 3:4], lnc2[:, sl, 4:5], AL.subtract))
                    s.c_ln1.inc(v.tensor_scalar_add(lnc2[:, sl, 4:5], lnc2[:, sl, 4:5], 1e-5))

                def rmul(j):
                    sl = j % 2
                    v.wait_ge(s.c_ln2.s, ln20 + j + 1)
                    dv(v.tensor_tensor(lnc2[:, sl, 7:8], lnc2[:, sl, 2:3],
                                       lnc2[:, sl, 6:7], AL.mult))
                    s.c_ln3.inc(v.tensor_scalar_mul(lnc2[:, sl, 7:8], lnc2[:, sl, 7:8], -1.0))

                for j in range(NT):
                    stats(j)
                    if j >= 1:
                        rmul(j - 1)
                rmul(NT - 1)

            @blk.scalar
            def _(sc):
                def lnexp(j):
                    sl = j % 2
                    sc.wait_ge(s.c_ln1.s, ln10 + j + 1)
                    s.c_sc.inc(sc.activation(lnc2[:, sl, 5:6], lnc2[:, sl, 4:5], AF.Ln))
                    sc.wait_ge(s.c_sc.s, s.c_sc.n)
                    s.c_ln2.inc(sc.activation(lnc2[:, sl, 6:7], lnc2[:, sl, 5:6],
                                              AF.Exp, scale=-0.5))

                def apply(j):
                    sl = j % 2
                    sc.wait_ge(s.c_ln3.s, ln30 + j + 1)
                    s.c_sfx.inc(sc.activation(ybuf[:, sl, :], ybuf[:, sl, :],
                                              AF.Identity, scale=lnc2[:, sl, 6:7],
                                              bias=lnc2[:, sl, 7:8]))

                for j in range(NT):
                    lnexp(j)
                    if j >= 1:
                        apply(j - 1)
                apply(NT - 1)

            @blk.sync
            def _(sync):
                for j in range(NT):
                    sync.wait_ge(s.c_sfx.s, sfx0 + j + 1)
                    s.c_out.inc(sync.dma_start(out=s.out_f[ts(j, 128), :],
                                               in_=ybuf[:, j % 2, :]), 16)

            @blk.gpsimd
            def _(g):
                g.wait_ge(s.c_out.s, s.c_out.n)
        fctx.close()

    def build_full(self):
        self.declare_io()
        self.alloc_common()
        self.open_phase("init")
        self.emit_init()
        self.emit_ffn_split()
        self.emit_ffn_bvec()
        self.close_phase()
        self.open_phase("m1")
        self.emit_m1()
        self.close_phase()
        self.open_phase("m2")
        self.emit_m2()
        self.close_phase()
        self.open_phase("m3")
        self.emit_m3()
        self.close_phase()
        fup = self.cfg.get("FUP", 5)
        if fup >= 1:
            self.open_phase("f1")
            self.emit_f1_ag()
            self.close_phase()
        if fup >= 2:
            self.open_phase("f2")
            self.emit_f2_routing()
            self.close_phase()
        if fup >= 3:
            self.open_phase("f3")
            self.emit_f3_expert()
            self.close_phase()
        if fup >= 4:
            self.open_phase("f4")
            self.emit_f4_scatter()
            self.close_phase()
        if fup >= 5:
            self.open_phase("f5")
            self.emit_f5_ln2()
            self.close_phase()
        if fup < 5:
            self.open_phase("fin")
            self.finalize_mamba_only()
            self.close_phase()
        self.ctx.close()
        self.compile()
        return self.nc

    def finalize_mamba_only(self):
        s, nc = self, self.nc
        if (blk := self.blk) or True:
            @blk.gpsimd
            def _(g):
                g.wait_ge(s.c_out.s, s.c_out.n)

    def build_mamba(self):
        ph = self.cfg.get("PH", 3)
        self.declare_io()
        self.alloc_common()
        self.open_phase("init")
        self.emit_init()
        self.close_phase()
        if ph >= 1:
            self.open_phase("m1")
            self.emit_m1()
            self.close_phase()
        if ph >= 2:
            self.open_phase("m2")
            self.emit_m2()
            self.close_phase()
        if ph >= 3:
            self.open_phase("m3")
            self.emit_m3()
            self.close_phase()
        self.open_phase("fin")
        self.finalize_mamba_only()
        self.close_phase()
        self.ctx.close()
        self.compile()
        return self.nc

    def compile(self):
        import bass_rust as _br
        orig = _br.insert_act_table_loads
        _br.insert_act_table_loads = lambda x: None
        try:
            self.nc.compile()
        finally:
            _br.insert_act_table_loads = orig


import numpy as np
from concourse.bass_utils import run_bass_kernel_spmd

_MK_CACHE = {}


def _build():
    key = "full"
    if key not in _MK_CACHE:
        m = MK(dict(L=4096, HALO=128, T=512, FFN=True, DFF=4096))
        nc = m.build_full()
        _MK_CACHE[key] = (m, nc)
    return _MK_CACHE[key]


def kernel(L0_attn_x, L0_attn_in_proj_w, L0_attn_conv_w, L0_attn_x_proj_w,
           L0_attn_dt_w, L0_attn_dt_b, L0_attn_A_param, L0_attn_D_param,
           L0_attn_out_w, L0_attn_out_b, L0_ffn_router_w, L0_ffn_router_b,
           L0_ffn_w1, L0_ffn_b1, L0_ffn_w2, L0_ffn_b2):
    x = np.asarray(L0_attn_x)
    ipw = np.ascontiguousarray(np.asarray(L0_attn_in_proj_w))
    wu, wz = ipw[:DI], ipw[DI:]
    convw = np.ascontiguousarray(np.asarray(L0_attn_conv_w).reshape(DI, 4))
    m, nc = _build()
    Lc, HALO = m.L, m.HALO
    wmap = {
        "wu_in": np.ascontiguousarray(wu),
        "wz_in": np.ascontiguousarray(wz),
        "convw_in": convw,
        "xpw_in": np.ascontiguousarray(np.asarray(L0_attn_x_proj_w)),
        "dtw_in": np.ascontiguousarray(np.asarray(L0_attn_dt_w)),
        "dtb_in": np.ascontiguousarray(np.asarray(L0_attn_dt_b).reshape(1, DI)),
        "ap_in": np.ascontiguousarray(np.asarray(L0_attn_A_param)),
        "dp_in": np.ascontiguousarray(np.asarray(L0_attn_D_param).reshape(1, DI)),
        "ow_in": np.ascontiguousarray(np.asarray(L0_attn_out_w)),
        "ob_in": np.ascontiguousarray(np.asarray(L0_attn_out_b).reshape(1, DM)),
        "rw_in": np.ascontiguousarray(np.asarray(L0_ffn_router_w)),
        "rb_in": np.ascontiguousarray(np.asarray(L0_ffn_router_b).reshape(1, 8)),
    }
    w1 = np.asarray(L0_ffn_w1)
    b1 = np.asarray(L0_ffn_b1)
    w2 = np.asarray(L0_ffn_w2)
    b2 = np.asarray(L0_ffn_b2)
    in_maps = []
    for c in range(8):
        b, q = c // 4, c % 4
        xs = np.zeros((HALO + Lc, DM), np.float32)
        xs[HALO:] = x[b, q * Lc:(q + 1) * Lc]
        if q > 0:
            xs[:HALO] = x[b, q * Lc - HALO:q * Lc]
        scrow = np.zeros((1, 80), np.uint32)
        scrow[0, :32] = np.array([8 * (r // 4) + c for r in range(32)], np.uint32)
        scrow[0, 32] = c * Lc
        for j in range(Lc // 128):
            scrow[0, 33 + j] = c * Lc + j * 128
        in_maps.append({"xs": xs, **wmap, "scrow_in": scrow,
                        "w1_in": np.ascontiguousarray(w1[c]),
                        "b1_in": np.ascontiguousarray(b1[c].reshape(1, m.DFF)),
                        "w2_in": np.ascontiguousarray(w2[c]),
                        "b2_in": np.ascontiguousarray(b2[c].reshape(1, DM))})
    r = run_bass_kernel_spmd(nc, in_maps, core_ids=list(range(8)))
    out = np.empty((2, 16384, DM), np.float32)
    for c in range(8):
        b, q = c // 4, c % 4
        out[b, q * Lc:(q + 1) * Lc] = r.results[c]["out"]
    return out
